# version 11
# speedup vs baseline: 1.3347x; 1.0137x over previous
import numpy as np
import ml_dtypes
from contextlib import ExitStack
import concourse.bass as bass
import concourse.mybir as mybir
from concourse.bass_utils import run_bass_kernel_spmd

F32 = mybir.dt.float32
BF16 = mybir.dt.bfloat16
FP8 = mybir.dt.float8e4
AF = mybir.ActivationFunctionType
ALU = mybir.AluOpType
AXX = mybir.AxisListType.X

T = 4096
C = 1024
NG = 8
CH = 512
NCH = T // CH
LN_EPS = 1e-5
NCORES = 8

bfloat16 = ml_dtypes.bfloat16


class Sems:

    def __init__(self, nc, es):
        self.nc, self.es = nc, es
        self.h, self.c = {}, {}

    def get(self, n):
        if n not in self.h:
            self.h[n] = self.es.enter_context(self.nc.semaphore(n))
            self.c[n] = 0
        return self.h[n]

    def inc(self, instr, n, k=1):
        instr.then_inc(self.get(n), k)
        self.c[n] += k
        return self.c[n]

    def dinc(self, instr, n):
        return self.inc(instr, n, 16)

    def val(self, n):
        self.get(n)
        return self.c[n]

    def awach(self, eng, n, v):
        eng.wait_ge(self.get(n), v)

    def fence(self, eng, instr, tag="fnc"):
        v = self.inc(instr, tag)
        eng.wait_ge(self.get(tag), v)


class SV:

    def __init__(self, S, names):
        self.S = S
        self.base = {n: S.val(n) for n in names}

    def inc(self, instr, n, k=1):
        return self.S.inc(instr, n, k)

    def dinc(self, instr, n):
        return self.S.inc(instr, n, 16)

    def val(self, n):
        return self.S.val(n)

    def rel(self, n):
        return self.S.val(n) - self.base[n]

    def wait(self, eng, n, v_rel):
        eng.wait_ge(self.S.get(n), self.base[n] + v_rel)

    def waita(self, eng, n, v_abs):
        eng.wait_ge(self.S.get(n), v_abs)

    def fence(self, eng, instr):
        self.S.fence(eng, instr)


_UID = [0]


def _allocs(nc, es):
    _UID[0] += 1
    u = f"u{_UID[0]}_"
    sb = lambda name, *a: es.enter_context(nc.sbuf_tensor(u + name, *a))
    ps = lambda name, *a: es.enter_context(nc.psum_tensor(u + name, *a))
    return sb, ps


def barrier(nc, S, block):
    tgt = S.val("bar") + 5
    for engname in ("tensor", "vector", "scalar", "gpsimd", "sync"):
        eng = getattr(nc, engname)

        def mk(eng=eng, engname=engname):
            @getattr(block, engname)
            def _(e):
                S.inc(e.nop(), "bar")
                S.awach(e, "bar", tgt)
        mk()


def cmajor(nc, name, kind=None, dtype=F32, t=T):
    h = (nc.dram_tensor(name, [C, t], dtype, kind=kind) if kind
         else nc.dram_tensor(name, [C, t], dtype))
    return h.ap().rearrange("(g p) t -> g p t", p=128)


def pcol(nc, name, kind="ExternalInput", dtype=F32, n=1, c=C):
    h = nc.dram_tensor(name, [c, n], dtype, kind=kind)
    return h.ap().rearrange("(g p) t -> g p t", p=128)


def LN_SEMS(tag=""):
    return [n + tag for n in ("sq", "smm", "ev1", "acta", "ev2", "actc",
                              "rmm", "apl", "lninit", "sqrdy")]


class LNStats:

    def __init__(self, nc, es, sv, apl_per_chunk, tag="", rep_pairs=2,
                 stats_pairs=1):
        self.sv = sv
        self.apc = apl_per_chunk
        self.tag = tag
        self.rep_pairs = rep_pairs
        self.stats_pairs = stats_pairs
        sb, ps = _allocs(nc, es)
        nsq = 2 if rep_pairs == 2 else 1
        self.sq = [[sb(f"lnsq{p}{g}", [128, CH], BF16) for g in range(NG)]
                   for p in range(nsq)]
        self.ps_sum = [ps(f"lnpsum{i}", [1, CH], F32)
                       for i in range(stats_pairs)]
        self.ps_sq = [ps(f"lnpssq{i}", [1, CH], F32)
                      for i in range(stats_pairs)]
        self.rep_m = [ps(f"lnrepm{p}", [128, CH], F32)
                      for p in range(rep_pairs)]
        self.rep_r = [ps(f"lnrepr{p}", [128, CH], F32)
                      for p in range(rep_pairs)]
        self.m_row = [sb(f"lnmrow{p}", [1, CH], F32) for p in range(2)]
        self.sq_row = [sb(f"lnsqrow{p}", [1, CH], F32) for p in range(2)]
        self.t_row = [sb(f"lntrow{p}", [1, CH], F32) for p in range(2)]
        self.r_row = [sb(f"lnrrow{p}", [1, CH], F32) for p in range(2)]
        self.ones_bf = sb("lnonesb", [128, 1], BF16)
        self.ones_f = sb("lnonesf", [1, 128], F32)
        self.eps_c = sb("lneps", [1, 1], F32)
        self.nh_c = sb("lnnh", [1, 1], F32)

    def emit_init(self, block):
        sv = self.sv

        @block.vector
        def _(v):
            v.memset(self.ones_bf[:], 1.0)
            v.memset(self.eps_c[:], LN_EPS)
            v.memset(self.nh_c[:], -0.5)
            sv.inc(v.memset(self.ones_f[:], 1.0), "lninit" + self.tag)

    def rp(self, j):
        return j % self.rep_pairs

    def emit_chunk(self, block, j, xsl):
        sv = self.sv
        p = j % 2 if self.rep_pairs == 2 else 0
        rp = self.rp(j)

        @block.scalar
        def _(s):
            if j == 0:
                sv.wait(s, "lninit" + self.tag, 1)
            if self.rep_pairs == 2:
                if j >= 2:
                    sv.wait(s, "smm" + self.tag, (j - 1) * 2)
            else:
                if j >= 1:
                    sv.wait(s, "smm" + self.tag, j * 2)
            sv.wait(s, "sqrdy" + self.tag, j + 1)
            for g in range(NG):
                sv.inc(s.activation(self.sq[p][g][:], xsl[g], AF.Square),
                       "sq" + self.tag)
        sq_v = sv.val("sq" + self.tag)

        @block.tensor
        def _(tn):
            sp = j % self.stats_pairs
            sv.waita(tn, "sq" + self.tag, sq_v)
            if j >= self.stats_pairs:
                sv.wait(tn, "ev1" + self.tag, j - self.stats_pairs + 1)
            for g in range(NG):
                tn.matmul(self.ps_sum[sp][:], self.ones_bf[:], xsl[g],
                          start=(g == 0), stop=(g == NG - 1))
            for g in range(NG):
                mm = tn.matmul(self.ps_sq[sp][:], self.ones_bf[:],
                               self.sq[p][g][:], start=(g == 0),
                               stop=(g == NG - 1))
            sv.inc(mm, "smm" + self.tag, 2)

        @block.vector
        def _(v):
            sv.wait(v, "smm" + self.tag, (j + 1) * 2)
            if j >= 2:
                sv.wait(v, "rmm" + self.tag, j - 1)
            sp = j % self.stats_pairs
            v.tensor_scalar(self.m_row[p][:], self.ps_sum[sp][:],
                            1.0 / C, None, op0=ALU.mult)
            sv.fence(v, v.tensor_scalar(self.sq_row[p][:], self.ps_sq[sp][:],
                                        1.0 / C, None, op0=ALU.mult))
            sv.fence(v, v.tensor_tensor(self.t_row[p][:], self.m_row[p][:],
                                        self.m_row[p][:], op=ALU.mult))
            sv.inc(v.tensor_tensor(self.t_row[p][:], self.sq_row[p][:],
                                   self.t_row[p][:], op=ALU.subtract),
                   "ev1" + self.tag)

        @block.scalar
        def _(s):
            sv.wait(s, "ev1" + self.tag, j + 1)
            s.activation(self.t_row[p][:], self.t_row[p][:], AF.Ln,
                         bias=self.eps_c[:])
            sv.inc(s.activation(self.r_row[p][:], self.t_row[p][:], AF.Exp,
                                scale=self.nh_c[:]), "actc" + self.tag)

        @block.tensor
        def _(tn):
            sv.wait(tn, "actc" + self.tag, j + 1)
            if self.rep_pairs == 2:
                if j >= 2:
                    sv.wait(tn, "apl" + self.tag, (j - 1) * self.apc)
            else:
                if j >= 1:
                    sv.wait(tn, "apl" + self.tag, j * self.apc)
            tn.matmul(self.rep_m[rp][:], self.ones_f[:], self.m_row[p][:],
                      start=True, stop=True)
            sv.inc(tn.matmul(self.rep_r[rp][:], self.ones_f[:],
                             self.r_row[p][:], start=True, stop=True),
                   "rmm" + self.tag)
        return sv.val("rmm" + self.tag)



def emit_ln(nc, S, block, es, xin, out, sc=None, bi=None, out_dtype=BF16):
    sv = SV(S, ["ld", "ldf", "st", "apl"] + LN_SEMS())
    sb, ps = _allocs(nc, es)
    xb = [sb(f"xb{g}", [128, T], BF16) for g in range(NG)]
    xf = [[sb(f"xf{p}{g}", [128, CH], F32) for g in range(NG)]
          for p in range(2)]
    tmp = sb("tmp", [128, CH], F32)
    ob = [[sb(f"ob{p}{g}", [128, CH], out_dtype) for g in range(NG)]
          for p in range(2)]
    scale_bias = sc is not None
    if scale_bias:
        scs = [sb(f"scs{g}", [128, 1], F32) for g in range(NG)]
        bis = [sb(f"bis{g}", [128, 1], F32) for g in range(NG)]
    L = LNStats(nc, es, sv, apl_per_chunk=NG, stats_pairs=2)
    L.emit_init(block)

    @block.gpsimd
    def _(g_):
        for g in range(NG):
            sv.dinc(g_.dma_start(xb[g][:], xin[g]), "ld")
        if scale_bias:
            for g in range(NG):
                sv.dinc(g_.dma_start(scs[g][:], sc[g]), "ld")
                sv.dinc(g_.dma_start(bis[g][:], bi[g]), "ld")
    ld_all = sv.val("ld")

    for j in range(NCH):
        p = j % 2
        sl = slice(j * CH, (j + 1) * CH)

        @block.sync
        def _(sy, j=j, p=p, sl=sl):
            if j >= 2:
                sv.wait(sy, "apl", (j - 1) * NG)
            for g in range(NG):
                sv.dinc(sy.dma_start(xf[p][g][:], xin[g][:, sl]), "ldf")
        ldf_v = sv.val("ldf")

        @block.scalar
        def _(s, j=j):
            if j == 0:
                sv.waita(s, "ld", ld_all)
            sv.inc(s.nop(), "sqrdy")
        rmm_v = L.emit_chunk(block, j, [xb[g][:, sl] for g in range(NG)])

        @block.vector
        def _(v, j=j, p=p, rmm_v=rmm_v, ldf_v=ldf_v):
            sv.waita(v, "rmm", rmm_v)
            sv.waita(v, "ldf", ldf_v)
            if j >= 2:
                sv.wait(v, "st", 16 * NG * (j - 1))
            for g in range(NG):
                v.tensor_tensor(tmp[:], xf[p][g][:], L.rep_m[p][:],
                                op=ALU.subtract)
                if scale_bias:
                    v.tensor_tensor(tmp[:], tmp[:], L.rep_r[p][:],
                                    op=ALU.mult)
                    sv.inc(v.tensor_scalar(ob[p][g][:], tmp[:], scs[g][:],
                                           bis[g][:], op0=ALU.mult,
                                           op1=ALU.add), "apl")
                else:
                    sv.inc(v.tensor_tensor(ob[p][g][:], tmp[:],
                                           L.rep_r[p][:], op=ALU.mult),
                           "apl")
        apl_v = sv.val("apl")

        @block.sync
        def _(sy, j=j, p=p, sl=sl, apl_v=apl_v):
            sv.waita(sy, "apl", apl_v)
            for g in range(NG):
                sv.dinc(sy.dma_start(out[g][:, sl], ob[p][g][:]), "st")

    @block.sync
    def _(sy):
        sv.waita(sy, "st", sv.val("st"))


def emit_postres(nc, S, block, es, so_in, x_in, out, so_f32=False):
    sv = SV(S, ["ld", "ldx", "st", "apl"] + LN_SEMS())
    sb, ps = _allocs(nc, es)
    sob = [sb(f"sob{g}", [128, T], BF16) for g in range(NG)]
    sof = ([[sb(f"sof{p}{g}", [128, CH], F32) for g in range(NG)]
            for p in range(2)] if so_f32 else None)
    xo = [[sb(f"xo{p}{g}", [128, CH], F32) for g in range(NG)]
          for p in range(2)]
    tmp = sb("tmp", [128, CH], F32)
    ob = [[sb(f"ob{p}{g}", [128, CH], F32) for g in range(NG)]
          for p in range(2)]
    L = LNStats(nc, es, sv, apl_per_chunk=NG, stats_pairs=2)
    L.emit_init(block)

    @block.gpsimd
    def _(g_):
        for g in range(NG):
            sv.dinc(g_.dma_start(sob[g][:], so_in[g]), "ld")
    ld_all = sv.val("ld")

    for j in range(NCH):
        p = j % 2
        sl = slice(j * CH, (j + 1) * CH)

        @block.sync
        def _(sy, j=j, p=p, sl=sl):
            if j >= 2:
                sv.wait(sy, "apl", (j - 1) * NG)
            for g in range(NG):
                sv.dinc(sy.dma_start(xo[p][g][:], x_in[g][:, sl]), "ldx")
                if sof is not None:
                    sv.dinc(sy.dma_start(sof[p][g][:], so_in[g][:, sl]),
                            "ldx")
        ldx_v = sv.val("ldx")

        @block.scalar
        def _(s, j=j):
            if j == 0:
                sv.waita(s, "ld", ld_all)
            sv.inc(s.nop(), "sqrdy")
        rmm_v = L.emit_chunk(block, j, [sob[g][:, sl] for g in range(NG)])

        @block.vector
        def _(v, j=j, p=p, sl=sl, rmm_v=rmm_v, ldx_v=ldx_v):
            sv.waita(v, "rmm", rmm_v)
            sv.waita(v, "ldx", ldx_v)
            if j >= 2:
                sv.wait(v, "st", 16 * NG * (j - 1))
            for g in range(NG):
                v.tensor_tensor(tmp[:],
                                (sof[p][g][:] if sof is not None
                                 else sob[g][:, sl]),
                                L.rep_m[p][:], op=ALU.subtract)
                v.tensor_tensor(tmp[:], tmp[:], L.rep_r[p][:], op=ALU.mult)
                sv.inc(v.tensor_tensor(ob[p][g][:], tmp[:], xo[p][g][:],
                                       op=ALU.add), "apl")
        apl_v = sv.val("apl")

        @block.sync
        def _(sy, j=j, p=p, sl=sl, apl_v=apl_v):
            sv.waita(sy, "apl", apl_v)
            for g in range(NG):
                sv.dinc(sy.dma_start(out[g][:, sl], ob[p][g][:]), "st")

    @block.sync
    def _(sy):
        sv.waita(sy, "st", sv.val("st"))


def emit_postres_ln(nc, S, block, es, so_in, x_in, out_x, out_ln,
                    so_f32=False, ln_dtype=BF16, ln_sc=None, ln_bi=None):
    sv = SV(S, ["ld", "ldx", "ldb", "stx", "st2", "xc"]
            + LN_SEMS() + LN_SEMS("B"))
    sb, ps = _allocs(nc, es)
    sob = [[sb(f"sob{p}{g}", [128, CH], BF16) for g in range(NG)]
           for p in range(2)]
    sof = ([[sb(f"sof{p}{g}", [128, CH], F32) for g in range(NG)]
            for p in range(2)] if so_f32 else None)
    xo = [[sb(f"xo{p}{g}", [128, CH], F32) for g in range(NG)]
          for p in range(2)]
    tmp = sb("tmp", [128, CH], F32)
    xn = [[sb(f"xn{p}{g}", [128, CH], F32) for g in range(NG)]
          for p in range(2)]
    xnb = [[sb(f"xnb{p}{g}", [128, CH], BF16) for g in range(NG)]
           for p in range(2)]
    ob2 = [[sb(f"ob2{p}{g}", [128, CH], ln_dtype) for g in range(NG)]
           for p in range(2)]
    if ln_sc is not None:
        scs = [sb(f"scs{g}", [128, 1], F32) for g in range(NG)]
        bis = [sb(f"bis{g}", [128, 1], F32) for g in range(NG)]
    L1 = LNStats(nc, es, sv, apl_per_chunk=NG, tag="", rep_pairs=1)
    L2 = LNStats(nc, es, sv, apl_per_chunk=NG, tag="B", rep_pairs=1)
    L1.emit_init(block)
    L2.emit_init(block)

    @block.gpsimd
    def _(g_):
        if ln_sc is not None:
            for g in range(NG):
                sv.dinc(g_.dma_start(scs[g][:], ln_sc[g]), "ld")
                sv.dinc(g_.dma_start(bis[g][:], ln_bi[g]), "ld")
        sv.inc(g_.nop(), "ld")
    ld_all = sv.val("ld")

    for j in range(NCH):
        p = j % 2
        sl = slice(j * CH, (j + 1) * CH)

        @block.sync
        def _(sy, j=j, p=p, sl=sl):
            if j >= 2:
                sv.wait(sy, "apl", (j - 1) * NG)
            for g in range(NG):
                sv.dinc(sy.dma_start(xo[p][g][:], x_in[g][:, sl]), "ldx")
                if sof is not None:
                    sv.dinc(sy.dma_start(sof[p][g][:], so_in[g][:, sl]),
                            "ldx")
        ldx_v = sv.val("ldx")

        @block.gpsimd
        def _(g_, j=j, p=p, sl=sl):
            if j >= 2:
                sv.wait(g_, "smm", j * 2)
            for g in range(NG):
                sv.dinc(g_.dma_start(sob[p][g][:], so_in[g][:, sl]), "ldb")
        ldb_v = sv.val("ldb")

        @block.scalar
        def _(s, j=j, ldb_v=ldb_v):
            if j == 0:
                sv.waita(s, "ld", ld_all)
            sv.waita(s, "ldb", ldb_v)
            sv.inc(s.nop(), "sqrdy")
        rmm_v = L1.emit_chunk(block, j,
                              [sob[p][g][:] for g in range(NG)])

        @block.vector
        def _(v, j=j, p=p, sl=sl, rmm_v=rmm_v, ldx_v=ldx_v):
            sv.waita(v, "rmm", rmm_v)
            sv.waita(v, "ldx", ldx_v)
            if j >= 2:
                sv.wait(v, "aplB", (j - 1) * NG)
                sv.wait(v, "stx", 16 * NG * (j - 1))
                sv.wait(v, "smmB", (j - 1) * 2)
            rp1 = L1.rp(j)
            for g in range(NG):
                v.tensor_tensor(tmp[:],
                                (sof[p][g][:] if sof is not None
                                 else sob[p][g][:]),
                                L1.rep_m[rp1][:], op=ALU.subtract)
                v.tensor_tensor(tmp[:], tmp[:], L1.rep_r[rp1][:],
                                op=ALU.mult)
                sv.inc(v.tensor_tensor(xn[p][g][:], tmp[:], xo[p][g][:],
                                       op=ALU.add), "apl")
                sv.inc(v.tensor_copy(xnb[p][g][:], xn[p][g][:]), "xc")
        apl_v = sv.val("apl")
        xc_v = sv.val("xc")

        @block.sync
        def _(sy, j=j, p=p, sl=sl, apl_v=apl_v):
            sv.waita(sy, "apl", apl_v)
            for g in range(NG):
                sv.dinc(sy.dma_start(out_x[g][:, sl], xn[p][g][:]), "stx")

        @block.scalar
        def _(s, xc_v=xc_v):
            sv.waita(s, "xc", xc_v)
            sv.inc(s.nop(), "sqrdyB")
        rmm2_v = L2.emit_chunk(block, j, [xnb[p][g][:] for g in range(NG)])

        @block.vector
        def _(v, j=j, p=p, rmm2_v=rmm2_v):
            sv.waita(v, "rmmB", rmm2_v)
            if j >= 2:
                sv.wait(v, "st2", 16 * NG * (j - 1))
            rp2 = L2.rp(j)
            for g in range(NG):
                v.tensor_tensor(tmp[:], xn[p][g][:], L2.rep_m[rp2][:],
                                op=ALU.subtract)
                if ln_sc is not None:
                    v.tensor_tensor(tmp[:], tmp[:], L2.rep_r[rp2][:],
                                    op=ALU.mult)
                    sv.inc(v.tensor_scalar(ob2[p][g][:], tmp[:], scs[g][:],
                                           bis[g][:], op0=ALU.mult,
                                           op1=ALU.add), "aplB")
                else:
                    sv.inc(v.tensor_tensor(ob2[p][g][:], tmp[:],
                                           L2.rep_r[rp2][:], op=ALU.mult),
                           "aplB")
        aplB_v = sv.val("aplB")

        @block.sync
        def _(sy, j=j, p=p, sl=sl, aplB_v=aplB_v):
            sv.waita(sy, "aplB", aplB_v)
            for g in range(NG):
                sv.dinc(sy.dma_start(out_ln[g][:, sl], ob2[p][g][:]), "st2")

    @block.sync
    def _(sy):
        sv.waita(sy, "stx", sv.val("stx"))
        sv.waita(sy, "st2", sv.val("st2"))


def emit_mm(nc, S, block, es, xin, w_in, out, evac="copy", out_dtype=BF16,
            b_in=None):
    sv = SV(S, ["ld", "st", "mm", "ev"])
    sb, ps = _allocs(nc, es)
    xb = [sb(f"xb{g}", [128, T], BF16) for g in range(NG)]
    wb = [sb(f"wb{g}", [128, C], BF16) for g in range(NG)]
    pp = [ps(f"pp{i}", [128, CH], F32) for i in range(4)]
    ob = [sb(f"ob{i}", [128, CH], out_dtype) for i in range(4)]
    if b_in is not None:
        bs = [sb(f"bs{g}", [128, 1], F32) for g in range(NG)]

    @block.gpsimd
    def _(g_):
        for g in range(NG):
            sv.dinc(g_.dma_start(xb[g][:], xin[g]), "ld")
        for g in range(NG):
            sv.dinc(g_.dma_start(wb[g][:], w_in[g]), "ld")
        if b_in is not None:
            for g in range(NG):
                sv.dinc(g_.dma_start(bs[g][:], b_in[g]), "ld")
    ld_all = sv.val("ld")

    idx = 0
    for g2 in range(NG):
        for j in range(NCH):
            q = idx % 4
            sl = slice(j * CH, (j + 1) * CH)

            @block.tensor
            def _(tn, g2=g2, j=j, q=q, sl=sl, idx=idx):
                if idx == 0:
                    sv.waita(tn, "ld", ld_all)
                if idx >= 4:
                    sv.wait(tn, "ev", idx - 3)
                for d in range(NG):
                    mm = tn.matmul(pp[q][:],
                                   wb[d][:, g2 * 128:(g2 + 1) * 128],
                                   xb[d][:, sl], start=(d == 0),
                                   stop=(d == NG - 1))
                sv.inc(mm, "mm")

            if evac == "sigmoid":
                if j == NCH - 1:
                    @block.scalar
                    def _(s, g2=g2, idx=idx):
                        for jj in range(NCH):
                            ix = g2 * NCH + jj
                            sv.wait(s, "mm", ix + 1)
                            if ix >= 4:
                                sv.wait(s, "st", 16 * (ix - 3))
                            sv.inc(s.activation(ob[ix % 4][:], pp[ix % 4][:],
                                                AF.Sigmoid), "ev")
            else:
                @block.vector
                def _(v, g2=g2, q=q, idx=idx):
                    sv.wait(v, "mm", idx + 1)
                    if idx >= 4:
                        sv.wait(v, "st", 16 * (idx - 3))
                    if b_in is not None:
                        sv.inc(v.tensor_scalar(ob[idx % 4][:], pp[q][:],
                                               bs[g2][:], None, op0=ALU.add),
                               "ev")
                    else:
                        sv.inc(v.tensor_copy(ob[idx % 4][:], pp[q][:]), "ev")

            @block.sync
            def _(sy, g2=g2, q=q, sl=sl, idx=idx):
                sv.wait(sy, "ev", idx + 1)
                sv.dinc(sy.dma_start(out[g2][:, sl], ob[idx % 4][:]), "st")
            idx += 1

    @block.sync
    def _(sy):
        sv.waita(sy, "st", sv.val("st"))


def emit_mix(nc, S, block, es, xin, tms, outs):
    nmix = len(tms)
    sv = SV(S, ["ld", "st", "ev"])
    sb, ps = _allocs(nc, es)
    xb = [sb(f"xb{g}", [128, T], BF16) for g in range(NG)]
    tmv = [[sb(f"tm{k}{g}", [128, 1], F32) for g in range(NG)]
           for k in range(nmix)]
    tmp = sb("tmp", [128, CH], F32)
    ob = [sb(f"ob{i}", [128, CH], BF16) for i in range(4)]

    @block.gpsimd
    def _(g_):
        for g in range(NG):
            sv.dinc(g_.dma_start(xb[g][:], xin[g]), "ld")
        for k in range(nmix):
            for g in range(NG):
                sv.dinc(g_.dma_start(tmv[k][g][:], tms[k][g]), "ld")
    ld_all = sv.val("ld")

    idx = 0
    for k in range(nmix):
        for g in range(NG):
            for j in range(NCH):
                q = idx % 4
                sl = slice(j * CH, (j + 1) * CH)

                @block.vector
                def _(v, k=k, g=g, j=j, q=q, sl=sl, idx=idx):
                    if idx == 0:
                        sv.waita(v, "ld", ld_all)
                    if idx >= 4:
                        sv.wait(v, "st", 16 * (idx - 3))
                    if j == 0:
                        v.tensor_scalar(ob[q][:, 0:1], xb[g][:, 0:1],
                                        tmv[k][g][:], None, op0=ALU.mult)
                        xx = xb[g][:, 0:CH - 1]
                        v.tensor_tensor(tmp[:, 1:CH], xb[g][:, 1:CH], xx,
                                        op=ALU.subtract)
                        sv.inc(v.scalar_tensor_tensor(
                            ob[q][:, 1:CH], tmp[:, 1:CH], tmv[k][g][:], xx,
                            op0=ALU.mult, op1=ALU.add), "ev")
                    else:
                        xx = xb[g][:, j * CH - 1:(j + 1) * CH - 1]
                        v.tensor_tensor(tmp[:], xb[g][:, sl], xx,
                                        op=ALU.subtract)
                        sv.inc(v.scalar_tensor_tensor(
                            ob[q][:], tmp[:], tmv[k][g][:], xx,
                            op0=ALU.mult, op1=ALU.add), "ev")

                @block.sync
                def _(sy, k=k, g=g, q=q, sl=sl, idx=idx):
                    sv.wait(sy, "ev", idx + 1)
                    sv.dinc(sy.dma_start(outs[k][g][:, sl], ob[q][:]), "st")
                idx += 1

    @block.sync
    def _(sy):
        sv.waita(sy, "st", sv.val("st"))


def emit_projln(nc, S, block, es, xin, w_in, b_in, out):
    sv = SV(S, ["ld", "st", "qmm", "qev", "apl"] + LN_SEMS())
    sb, ps = _allocs(nc, es)
    xb = [sb(f"xb{g}", [128, T], BF16) for g in range(NG)]
    wb = [sb(f"wb{g}", [128, C], BF16) for g in range(NG)]
    bs = [sb(f"bs{g}", [128, 1], F32) for g in range(NG)]
    qf = [[sb(f"qf{p}{g}", [128, CH], F32) for g in range(NG)]
          for p in range(2)]
    qb = [[sb(f"qb{p}{g}", [128, CH], BF16) for g in range(NG)]
          for p in range(2)]
    tmp = sb("tmp", [128, CH], F32)
    ob = [[sb(f"ob{p}{g}", [128, CH], BF16) for g in range(NG)]
          for p in range(2)]
    pq = [ps(f"pq{i}", [128, CH], F32) for i in range(2)]
    L = LNStats(nc, es, sv, apl_per_chunk=NG)
    L.emit_init(block)

    @block.gpsimd
    def _(g_):
        for g in range(NG):
            sv.dinc(g_.dma_start(xb[g][:], xin[g]), "ld")
        for g in range(NG):
            sv.dinc(g_.dma_start(wb[g][:], w_in[g]), "ld")
        for g in range(NG):
            sv.dinc(g_.dma_start(bs[g][:], b_in[g]), "ld")
    ld_all = sv.val("ld")

    for j in range(NCH):
        p = j % 2
        sl = slice(j * CH, (j + 1) * CH)
        for g2 in range(NG):
            q = (j * NG + g2) % 2
            idx = j * NG + g2

            @block.tensor
            def _(tn, j=j, g2=g2, q=q, idx=idx, sl=sl):
                if idx == 0:
                    sv.waita(tn, "ld", ld_all)
                if idx >= 2:
                    sv.wait(tn, "qev", idx - 1)
                for d in range(NG):
                    mm = tn.matmul(pq[q][:],
                                   wb[d][:, g2 * 128:(g2 + 1) * 128],
                                   xb[d][:, sl], start=(d == 0),
                                   stop=(d == NG - 1))
                sv.inc(mm, "qmm")

            @block.vector
            def _(v, j=j, g2=g2, q=q, idx=idx, p=p):
                sv.wait(v, "qmm", idx + 1)
                if j >= 2:
                    sv.wait(v, "smm", (j - 1) * 2)
                    sv.wait(v, "apl", (j - 1) * NG)
                v.tensor_scalar(qf[p][g2][:], pq[q][:], bs[g2][:], None,
                                op0=ALU.add)
                sv.inc(v.tensor_copy(qb[p][g2][:], qf[p][g2][:]), "qev")
        qev_v = sv.val("qev")

        @block.scalar
        def _(s, qev_v=qev_v):
            sv.waita(s, "qev", qev_v)
            sv.inc(s.nop(), "sqrdy")
        rmm_v = L.emit_chunk(block, j, [qb[p][g][:] for g in range(NG)])

        @block.vector
        def _(v, j=j, p=p, rmm_v=rmm_v):
            sv.waita(v, "rmm", rmm_v)
            if j >= 2:
                sv.wait(v, "st", 16 * NG * (j - 1))
            for g in range(NG):
                v.tensor_tensor(tmp[:], qf[p][g][:], L.rep_m[p][:],
                                op=ALU.subtract)
                sv.inc(v.tensor_tensor(ob[p][g][:], tmp[:], L.rep_r[p][:],
                                       op=ALU.mult), "apl")
        apl_v = sv.val("apl")

        @block.sync
        def _(sy, j=j, p=p, sl=sl, apl_v=apl_v):
            sv.waita(sy, "apl", apl_v)
            for g in range(NG):
                sv.dinc(sy.dma_start(out[g][:, sl], ob[p][g][:]), "st")

    @block.sync
    def _(sy):
        sv.waita(sy, "st", sv.val("st"))


def emit_vproj(nc, S, block, es, xin, w_in, b_in, out):
    sv = SV(S, ["ld", "st", "mm", "ev", "init"])
    sb, ps = _allocs(nc, es)
    xb = [sb(f"xb{g}", [128, T], BF16) for g in range(NG)]
    wb = [sb(f"wb{g}", [128, C], BF16) for g in range(NG)]
    brow_f = sb("browf", [1, C], F32)
    brow = sb("brow", [1, C], BF16)
    ones_c = sb("onesc", [1, 128], BF16)
    pv = [ps(f"pv{i}", [128, CH], F32) for i in range(2)]
    ob = [sb(f"ob{i}", [128, CH], BF16) for i in range(4)]

    @block.gpsimd
    def _(g_):
        for g in range(NG):
            sv.dinc(g_.dma_start(xb[g][:], xin[g]), "ld")
        for g in range(NG):
            sv.dinc(g_.dma_start(wb[g][:], w_in[g]), "ld")
        sv.dinc(g_.dma_start(brow_f[:], b_in), "ld")
    ld_all = sv.val("ld")

    @block.vector
    def _(v):
        sv.waita(v, "ld", ld_all)
        v.memset(ones_c[:], 1.0)
        sv.inc(v.tensor_copy(brow[:], brow_f[:]), "init")

    idx = 0
    for tt in range(32):
        for eh in range(2):
            q = idx % 2
            esl = slice(eh * CH, (eh + 1) * CH)

            @block.tensor
            def _(tn, tt=tt, q=q, idx=idx, esl=esl):
                if idx == 0:
                    sv.wait(tn, "init", 1)
                if idx >= 2:
                    sv.wait(tn, "ev", idx - 1)
                for d in range(NG):
                    tn.matmul(pv[q][:], xb[d][:, tt * 128:(tt + 1) * 128],
                              wb[d][:, esl], start=(d == 0), stop=False)
                sv.inc(tn.matmul(pv[q][:], ones_c[:], brow[0:1, esl],
                                 start=False, stop=True), "mm")

            @block.vector
            def _(v, q=q, idx=idx):
                sv.wait(v, "mm", idx + 1)
                if idx >= 4:
                    sv.wait(v, "st", 16 * (idx - 3))
                sv.inc(v.tensor_copy(ob[idx % 4][:], pv[q][:]), "ev")

            @block.sync
            def _(sy, tt=tt, idx=idx, esl=esl):
                sv.wait(sy, "ev", idx + 1)
                sv.dinc(sy.dma_start(out[tt][:, esl], ob[idx % 4][:]), "st")
            idx += 1

    @block.sync
    def _(sy):
        sv.waita(sy, "st", sv.val("st"))


def emit_l0attn(nc, S, block, es, qn_in, kn_in, v_in, wo_in, bo_in, ls_in,
                out):
    CHA = 256
    NCA = T // CHA
    sv = SV(S, ["ld", "qld", "st", "init", "ptmm", "sgev", "temm", "oev",
                "pjmm", "pjev"])
    sb, ps = _allocs(nc, es)
    kn = [sb(f"kn{g}", [128, T], BF16) for g in range(NG)]
    vb = [sb(f"vb{n}", [128, C], BF16) for n in range(32)]
    wo = [sb(f"wo{g}", [128, C], BF16) for g in range(NG)]
    bo = [sb(f"bo{g}", [128, 1], F32) for g in range(NG)]
    ls = [sb(f"ls{g}", [128, 1], F32) for g in range(NG)]
    qc = [[sb(f"qc{p}{g}", [128, CHA], BF16) for g in range(NG)]
          for p in range(2)]
    pb = [sb(f"pbuf{p}", [128, CHA], BF16) for p in range(2)]
    obf = [sb(f"obf{g}", [128, CHA], BF16) for g in range(NG)]
    ob = [sb(f"ob{i}", [128, CHA], BF16) for i in range(4)]
    logs_c = sb("logsc", [128, 1], F32)
    scl_c = sb("sclc", [128, 1], F32)
    zrow = sb("zrow", [1, 2 * CHA], BF16)
    onec = sb("onec", [1, 128], BF16)
    psPt = [ps(f"psPt{i}", [128, CHA], F32) for i in range(2)]
    psO = [ps(f"psO{i}", [128, 2 * CHA], F32) for i in range(4)]
    psPj = [ps(f"psPj{i}", [128, CHA], F32) for i in range(2)]

    @block.gpsimd
    def _(g_):
        for g in range(NG):
            sv.dinc(g_.dma_start(kn[g][:], kn_in[g]), "ld")
        for n in range(32):
            sv.dinc(g_.dma_start(vb[n][:], v_in[n]), "ld")
        for g in range(NG):
            sv.dinc(g_.dma_start(wo[g][:], wo_in[g]), "ld")
            sv.dinc(g_.dma_start(bo[g][:], bo_in[g]), "ld")
            sv.dinc(g_.dma_start(ls[g][:], ls_in[g]), "ld")
    ld_all = sv.val("ld")

    @block.vector
    def _(v):
        v.memset(scl_c[:], 1.0 / 32.0)
        v.memset(zrow[:], 0.0)
        v.memset(onec[:], 1.0)
        sv.inc(v.memset(logs_c[:], -float(np.log(np.float32(T)))), "init")

    for sc in range(NCA):
        ssl = slice(sc * CHA, (sc + 1) * CHA)

        @block.sync
        def _(sy, sc=sc, ssl=ssl):
            if sc >= 2:
                sv.wait(sy, "ptmm", (sc - 1) * 32)
            for g in range(NG):
                sv.dinc(sy.dma_start(qc[sc % 2][g][:], qn_in[g][:, ssl]),
                        "qld")
        qld_v = sv.val("qld")

        for tt in range(33):
            hp = tt % 2
            it = sc * 32 + tt

            @block.tensor
            def _(tn, sc=sc, tt=tt, hp=hp, it=it, qld_v=qld_v):
                if tt < 32:
                    if sc == 0 and tt == 0:
                        sv.waita(tn, "ld", ld_all)
                    sv.waita(tn, "qld", qld_v)
                    if it >= 2:
                        sv.wait(tn, "sgev", it - 1)
                    if tt == 0 and sc == 0:
                        sv.wait(tn, "init", 1)
                    for g in range(NG):
                        mm = tn.matmul(psPt[hp][:],
                                       kn[g][:, tt * 128:(tt + 1) * 128],
                                       qc[sc % 2][g][:], start=(g == 0),
                                       stop=(g == NG - 1))
                    sv.inc(mm, "ptmm")
                if tt >= 1:
                    to = tt - 1
                    if to == 0:
                        if sc >= 1:
                            sv.wait(tn, "oev", sc * NG)
                        for i4 in range(4):
                            tn.matmul(psO[i4][:], onec[:], zrow[:],
                                      start=True, stop=False)
                    sv.wait(tn, "sgev", sc * 32 + to + 1)
                    for g in range(NG):
                        mm = tn.matmul(
                            psO[g // 2][:, (g % 2) * CHA:(g % 2 + 1) * CHA],
                            vb[to][:, g * 128:(g + 1) * 128], pb[to % 2][:],
                            start=False, stop=(to == 31))
                    sv.inc(mm, "temm")

        @block.scalar
        def _(s, sc=sc):
            for tt in range(32):
                it = sc * 32 + tt
                hp = tt % 2
                sv.wait(s, "ptmm", it + 1)
                if it >= 2:
                    sv.wait(s, "temm", it - 1)
                sv.inc(s.activation(pb[hp][:], psPt[hp][:], AF.Sigmoid,
                                    bias=logs_c[:], scale=scl_c[:]),
                       "sgev")

        @block.vector
        def _(v, sc=sc):
            sv.wait(v, "temm", (sc + 1) * 32)
            if sc >= 1:
                sv.wait(v, "pjmm", sc * NG)
            for g in range(NG):
                sv.inc(v.tensor_copy(
                    obf[g][:],
                    psO[g // 2][:, (g % 2) * CHA:(g % 2 + 1) * CHA]), "oev")
        oev_v = sv.val("oev")

        for g2 in range(NG):
            qj = g2 % 2
            ij = sc * NG + g2

            @block.tensor
            def _(tn, g2=g2, qj=qj, ij=ij, oev_v=oev_v):
                sv.waita(tn, "oev", oev_v)
                if ij >= 2:
                    sv.wait(tn, "pjev", ij - 1)
                for e in range(NG):
                    mm = tn.matmul(psPj[qj][:],
                                   wo[e][:, g2 * 128:(g2 + 1) * 128],
                                   obf[e][:], start=(e == 0),
                                   stop=(e == NG - 1))
                sv.inc(mm, "pjmm")

            @block.vector
            def _(v, g2=g2, qj=qj, ij=ij):
                sv.wait(v, "pjmm", ij + 1)
                if ij >= 4:
                    sv.wait(v, "st", 16 * (ij - 3))
                sv.inc(v.tensor_scalar(ob[ij % 4][:], psPj[qj][:], bo[g2][:],
                                       ls[g2][:], op0=ALU.add, op1=ALU.mult),
                       "pjev")

            @block.sync
            def _(sy, g2=g2, ij=ij, ssl=ssl):
                sv.wait(sy, "pjev", ij + 1)
                sv.dinc(sy.dma_start(out[g2][:, ssl], ob[ij % 4][:]), "st")

    @block.sync
    def _(sy):
        sv.waita(sy, "st", sv.val("st"))


def emit_l0ffn(nc, S, block, es, xin, k_in, sin_in, sk_in, b_in, out):
    sv = SV(S, ["ld", "st", "mm", "ev", "init", "quant"])
    sb, ps = _allocs(nc, es)
    xb = [sb(f"xb{g}", [128, T], BF16) for g in range(NG)]
    kb = [sb(f"kb{g}", [128, C], BF16) for g in range(NG)]
    xq = [sb(f"xq{g}", [128, T], FP8) for g in range(NG)]
    kq = [sb(f"kq{g}", [128, C], FP8) for g in range(NG)]
    bs = [sb(f"bs{g}", [128, 1], F32) for g in range(NG)]
    s_in = sb("s_in", [128, 1], F32)
    s_k = sb("s_k", [128, 1], F32)
    r_in = sb("r_in", [128, 1], F32)
    r_k = sb("r_k", [128, 1], F32)
    prod = sb("prod", [128, 1], F32)
    yb = sb("ybt", [128, CH], BF16)
    ob = [sb(f"ob{i}", [128, CH], BF16) for i in range(4)]
    pq = [ps(f"pq{i}", [128, CH], F32) for i in range(2)]

    @block.gpsimd
    def _(g_):
        for g in range(NG):
            sv.dinc(g_.dma_start(xb[g][:], xin[g]), "ld")
        for g in range(NG):
            sv.dinc(g_.dma_start(kb[g][:], k_in[g]), "ld")
            sv.dinc(g_.dma_start(bs[g][:], b_in[g]), "ld")
        sv.dinc(g_.dma_start(s_in[:], sin_in), "ld")
        sv.dinc(g_.dma_start(s_k[:], sk_in), "ld")
    ld_all = sv.val("ld")

    @block.vector
    def _(v):
        sv.waita(v, "ld", ld_all)
        v.reciprocal(r_in[:], s_in[:])
        v.reciprocal(r_k[:], s_k[:])
        sv.inc(v.tensor_tensor(prod[:], s_in[:], s_k[:], op=ALU.mult),
               "init")

    @block.scalar
    def _(s):
        sv.wait(s, "init", 1)
        for g in range(NG):
            s.activation(xq[g][:], xb[g][:], AF.Copy, scale=r_in[:])
        for g in range(NG):
            mm = s.activation(kq[g][:], kb[g][:], AF.Copy, scale=r_k[:])
        sv.inc(mm, "quant")
    quant_v = sv.val("quant")

    idx = 0
    for g2 in range(NG):
        for j in range(NCH):
            q = idx % 2
            sl = slice(j * CH, (j + 1) * CH)

            @block.tensor
            def _(tn, g2=g2, q=q, sl=sl, idx=idx):
                if idx == 0:
                    sv.waita(tn, "quant", quant_v)
                if idx >= 2:
                    sv.wait(tn, "ev", idx - 1)
                for d in range(NG):
                    mm = tn.matmul(pq[q][:],
                                   kq[d][:, g2 * 128:(g2 + 1) * 128],
                                   xq[d][:, sl], start=(d == 0),
                                   stop=(d == NG - 1))
                sv.inc(mm, "mm")

            @block.vector
            def _(v, g2=g2, q=q, idx=idx):
                sv.wait(v, "mm", idx + 1)
                if idx >= 4:
                    sv.wait(v, "st", 16 * (idx - 3))
                v.tensor_scalar(yb[:], pq[q][:], prod[:], None, op0=ALU.mult)
                sv.inc(v.tensor_scalar(ob[idx % 4][:], yb[:], bs[g2][:],
                                       None, op0=ALU.add), "ev")

            @block.sync
            def _(sy, g2=g2, sl=sl, idx=idx):
                sv.wait(sy, "ev", idx + 1)
                sv.dinc(sy.dma_start(out[g2][:, sl], ob[idx % 4][:]), "st")
            idx += 1

    @block.sync
    def _(sy):
        sv.waita(sy, "st", sv.val("st"))


def emit_wkv(nc, S, block, es, k_in, v_in, sr_in, wd_in, u_in, out):
    sv = SV(S, ["ldw", "ld", "st", "init", "sm1", "vm1", "sm2", "sm3",
                "vdone", "den", "sden", "zdone"])
    sb, ps = _allocs(nc, es)
    kb = [sb(f"kb{p}", [128, T], F32) for p in range(2)]
    vb = [sb(f"vb{p}", [128, T], F32) for p in range(2)]
    srb = [sb(f"srb{p}", [128, T], BF16) for p in range(2)]
    wd = [sb(f"wd{g}", [128, 1], F32) for g in range(NG)]
    uu = [sb(f"uu{g}", [128, 1], F32) for g in range(NG)]
    ew = [sb(f"ew{p}", [128, 1], F32) for p in range(2)]
    abuf = [sb(f"abuf{p}", [128, T], F32) for p in range(2)]
    ebuf = sb("ebuf", [128, T], F32)
    pbuf = sb("pbuf", [128, T], F32)
    zb = [sb(f"zb{p}", [128, T], BF16) for p in range(2)]
    nm1 = sb("nm1", [128, 1], F32)

    @block.vector
    def _(v):
        sv.inc(v.memset(nm1[:], -1.0), "init")

    @block.sync
    def _(sy):
        for g in range(NG):
            sv.dinc(sy.dma_start(wd[g][:], wd_in[g]), "ldw")
            sv.dinc(sy.dma_start(uu[g][:], u_in[g]), "ldw")
    ldw_v = sv.val("ldw")

    def front(g):
        p = g % 2

        @block.sync
        def _(sy, g=g, p=p):
            if g >= 2:
                sv.wait(sy, "zdone", g - 1)
            sv.dinc(sy.dma_start(kb[p][:], k_in[g]), "ld")
            sv.dinc(sy.dma_start(vb[p][:], v_in[g]), "ld")
            sv.dinc(sy.dma_start(srb[p][:], sr_in[g]), "ld")
        ld_v = sv.val("ld")

        @block.scalar
        def _(s, g=g, p=p, ld_v=ld_v):
            if g == 0:
                sv.waita(s, "ldw", ldw_v)
                sv.wait(s, "init", 1)
            if g >= 2:
                sv.wait(s, "vdone", g - 1)
            sv.inc(s.activation(ew[p][:], wd[g][:], AF.Exp), "sm1")

        @block.vector
        def _(v, g=g, p=p):
            sv.wait(v, "sm1", g + 1)
            sv.inc(v.tensor_tensor(ew[p][:], ew[p][:], nm1[:],
                                   op=ALU.mult), "vm1")

        @block.scalar
        def _(s, g=g, p=p, ld_v=ld_v):
            sv.wait(s, "vm1", g + 1)
            sv.inc(s.activation(ew[p][:], ew[p][:], AF.Exp), "sm2")
            sv.waita(s, "ld", ld_v)
            if g >= 2:
                sv.wait(s, "zdone", g - 1)
            s.activation(abuf[p][:], kb[p][:], AF.Exp)
            sv.inc(s.activation(kb[p][:], kb[p][:], AF.Exp, bias=uu[g][:]),
                   "sm3")

        @block.vector
        def _(v, g=g, p=p):
            sv.wait(v, "sm2", g + 1)
            sv.wait(v, "sm3", g + 1)
            coefb = ew[p][:].broadcast_to((128, T))
            v.tensor_tensor(ebuf[:], abuf[p][:], vb[p][:], op=ALU.mult)
            v.tensor_tensor_scan(pbuf[:], coefb, ebuf[:], 0.0,
                                 op0=ALU.mult, op1=ALU.add)
            sv.inc(v.tensor_tensor_scan(ebuf[:], coefb, abuf[p][:], 0.0,
                                        op0=ALU.mult, op1=ALU.add), "vdone")
            v.tensor_tensor(abuf[p][:], kb[p][:], vb[p][:], op=ALU.mult)
            v.tensor_tensor(abuf[p][:, 1:T], abuf[p][:, 1:T],
                            pbuf[:, 0:T - 1], op=ALU.add)
            sv.inc(v.tensor_tensor(kb[p][:, 1:T], kb[p][:, 1:T],
                                   ebuf[:, 0:T - 1], op=ALU.add), "den")

        @block.scalar
        def _(s, g=g, p=p):
            sv.wait(s, "den", g + 1)
            s.activation(kb[p][:], kb[p][:], AF.Ln)
            sv.inc(s.activation(kb[p][:], kb[p][:], AF.Exp, scale=nm1[:]),
                   "sden")

    def tail(g):
        p = g % 2

        @block.vector
        def _(v, g=g, p=p):
            sv.wait(v, "sden", g + 1)
            if g >= 2:
                sv.wait(v, "st", 16 * (g - 1))
            v.tensor_tensor(abuf[p][:], abuf[p][:], kb[p][:], op=ALU.mult)
            sv.inc(v.tensor_tensor(zb[p][:], abuf[p][:], srb[p][:],
                                   op=ALU.mult), "zdone")

        @block.sync
        def _(sy, g=g, p=p):
            sv.wait(sy, "zdone", g + 1)
            sv.dinc(sy.dma_start(out[g], zb[p][:]), "st")

    for g in range(NG):
        front(g)
        if g >= 1:
            tail(g - 1)
    tail(NG - 1)

    @block.sync
    def _(sy):
        sv.waita(sy, "st", sv.val("st"))


def emit_lif(nc, S, block, es, a_in, b_in, out):
    sv = SV(S, ["lda", "ldb", "st", "init", "rel", "spk", "scan"])
    NL = 32
    NCK = T // NL
    LAN = NG * NCK
    sb, ps = _allocs(nc, es)
    B = sb("B", [128, NG * T], F32)
    A = [sb(f"A{p}", [128, T], F32) for p in range(2)]
    wst = [sb(f"wst{p}", [128, LAN], F32) for p in range(2)]
    m = sb("m", [128, LAN], F32)
    mask = sb("mask", [128, LAN], F32)
    bs = [sb(f"bs{p}", [128, CH], F32) for p in range(2)]
    ob = [sb(f"obl{p}", [128, CH], F32) for p in range(2)]

    @block.vector
    def _(v):
        v.memset(mask[:], 1.0)
        for g in range(NG):
            v.memset(mask[:, g * NCK:g * NCK + 1], 0.0)
        sv.inc(v.memset(wst[0][:], 0.0), "init")

    for g in range(NG):
        pa = g % 2

        @block.sync
        def _(sy, g=g, pa=pa):
            if g >= 2:
                sv.wait(sy, "rel", g - 1)
            sv.dinc(sy.dma_start(A[pa][:], a_in[g]), "lda")
        lda_v = sv.val("lda")

        @block.vector
        def _(v, g=g, pa=pa, lda_v=lda_v):
            sv.waita(v, "lda", lda_v)
            dst = B[:, g * T:(g + 1) * T].rearrange("p (i ch) -> p i ch",
                                                    i=NL)
            srcv = A[pa][:].rearrange("p (ch i) -> p i ch", ch=NCK)
            sv.inc(v.tensor_copy(dst, srcv), "rel")
    rel_v = sv.val("rel")
    Bv = B[:].rearrange("p (g i ch) -> p g i ch", g=NG, i=NL)

    @block.vector
    def _(v, rel_v=rel_v):
        sv.waita(v, "rel", rel_v)
        sv.wait(v, "init", 1)
        wprev = 0
        for j in range(1, NL):
            xw = B[:].rearrange("p (g r) -> p g r", g=NG)[
                :, :, j * NCK - 1:j * NCK - 1 + NCK]
            mm_ = m[:].rearrange("p (g ch) -> p g ch", g=NG)
            wp = wst[wprev][:].rearrange("p (g ch) -> p g ch", g=NG)
            wc = wst[1 - wprev][:].rearrange("p (g ch) -> p g ch", g=NG)
            v.scalar_tensor_tensor(mm_, wp, 2.0, wp, op0=ALU.is_lt,
                                   op1=ALU.mult)
            v.scalar_tensor_tensor(wc, mm_, 0.5, xw, op0=ALU.mult,
                                   op1=ALU.add)
            wprev = 1 - wprev
        v.tensor_tensor(wst[wprev][:], wst[wprev][:], mask[:], op=ALU.mult)
        for i in range(NL):
            prev = (wst[wprev][:].rearrange("p (g ch) -> p g ch", g=NG)
                    if i == 0 else Bv[:, :, i - 1, :])
            mm_ = m[:].rearrange("p (g ch) -> p g ch", g=NG)
            v.scalar_tensor_tensor(mm_, prev, 2.0, prev, op0=ALU.is_lt,
                                   op1=ALU.mult)
            v.scalar_tensor_tensor(Bv[:, :, i, :], mm_, 0.5, Bv[:, :, i, :],
                                   op0=ALU.mult, op1=ALU.add)
        sv.inc(v.nop(), "scan")

    for g in range(NG):
        for j in range(NCH):
            pa = (g * NCH + j) % 2
            sl = slice(j * CH, (j + 1) * CH)
            it = g * NCH + j

            @block.gpsimd
            def _(gp, g=g, pa=pa, sl=sl, it=it):
                if it >= 2:
                    sv.wait(gp, "spk", it - 1)
                sv.dinc(gp.dma_start(bs[pa][:], b_in[g][:, sl]), "ldb")
            ldb_v = sv.val("ldb")

            @block.vector
            def _(v, g=g, j=j, pa=pa, ldb_v=ldb_v, it=it):
                sv.waita(v, "ldb", ldb_v)
                if it >= 2:
                    sv.wait(v, "st", 16 * (it - 1))
                pv = B[:, g * T:(g + 1) * T].rearrange(
                    "p (i ch) -> p ch i", i=NL)[:, j * 16:(j + 1) * 16, :]
                bb = bs[pa][:].rearrange("p (ch i) -> p ch i", ch=16)
                oo = ob[pa][:].rearrange("p (ch i) -> p ch i", ch=16)
                sv.inc(v.scalar_tensor_tensor(oo, pv, 2.0, bb,
                                              op0=ALU.is_ge, op1=ALU.add),
                       "spk")

            @block.sync
            def _(sy, g=g, pa=pa, sl=sl, it=it):
                sv.wait(sy, "spk", it + 1)
                sv.dinc(sy.dma_start(out[g][:, sl], ob[pa][:]), "st")

    @block.sync
    def _(sy):
        sv.waita(sy, "st", sv.val("st"))


def emit_cmix(nc, S, block, es, xk_in, sr_in, wk_in, wv_in, out):
    H4 = 4096
    CHA = 256
    NCA = T // CHA
    sv = SV(S, ["ld", "ldx", "st", "zinit", "kmm", "kev", "vmm", "cev"])
    sb, ps = _allocs(nc, es)
    wk = [sb(f"wk{g}", [128, H4], BF16) for g in range(NG)]
    wv = [sb(f"wv{h}", [128, C], BF16) for h in range(32)]
    xk = [[sb(f"xk{p}{g}", [128, CHA], BF16) for g in range(NG)]
          for p in range(2)]
    sr = [[sb(f"sr{p}{g}", [128, CHA], BF16) for g in range(NG)]
          for p in range(2)]
    kk = [sb(f"kk{p}", [128, CHA], BF16) for p in range(2)]
    kt = sb("ktmp", [128, CHA], F32)
    ob = [sb(f"ob{i}", [128, CHA], F32) for i in range(4)]
    zrow = sb("zrow", [1, 2 * CHA], BF16)
    onec = sb("onec", [1, 128], BF16)
    psK = [ps(f"psK{i}", [128, CHA], F32) for i in range(2)]
    psC = [ps(f"psC{i}", [128, 2 * CHA], F32) for i in range(4)]

    @block.vector
    def _(v):
        v.memset(zrow[:], 0.0)
        sv.inc(v.memset(onec[:], 1.0), "zinit")

    @block.gpsimd
    def _(g_):
        for g in range(NG):
            sv.dinc(g_.dma_start(wk[g][:], wk_in[g]), "ld")
        for h in range(32):
            sv.dinc(g_.dma_start(wv[h][:], wv_in[h]), "ld")
    ld_all = sv.val("ld")

    for sc in range(NCA):
        p = sc % 2
        ssl = slice(sc * CHA, (sc + 1) * CHA)

        @block.gpsimd
        def _(gp, sc=sc, p=p, ssl=ssl):
            if sc >= 2:
                sv.wait(gp, "kmm", (sc - 1) * 32)
                sv.wait(gp, "cev", (sc - 1) * NG)
            for g in range(NG):
                sv.dinc(gp.dma_start(xk[p][g][:], xk_in[g][:, ssl]), "ldx")
                sv.dinc(gp.dma_start(sr[p][g][:], sr_in[g][:, ssl]), "ldx")
        ldx_v = sv.val("ldx")

        for h in range(33):
            hp = h % 2
            it = sc * 32 + h

            @block.tensor
            def _(tn, sc=sc, h=h, hp=hp, it=it, p=p, ldx_v=ldx_v):
                if h < 32:
                    if it == 0:
                        sv.waita(tn, "ld", ld_all)
                        sv.wait(tn, "zinit", 1)
                    sv.waita(tn, "ldx", ldx_v)
                    if it >= 2:
                        sv.wait(tn, "kev", it - 1)

                    for d in range(NG):
                        mm = tn.matmul(psK[hp][:],
                                       wk[d][:, h * 128:(h + 1) * 128],
                                       xk[p][d][:], start=(d == 0),
                                       stop=(d == NG - 1))
                    sv.inc(mm, "kmm")
                if h >= 1:
                    ho = h - 1
                    if ho == 0:
                        if sc >= 1:
                            sv.wait(tn, "cev", sc * NG)
                        for i4 in range(4):
                            tn.matmul(psC[i4][:], onec[:], zrow[:],
                                      start=True, stop=False)
                    sv.wait(tn, "kev", sc * 32 + ho + 1)
                    for g2 in range(NG):
                        mm = tn.matmul(
                            psC[g2 // 2][:, (g2 % 2) * CHA:(g2 % 2 + 1) * CHA],
                            wv[ho][:, g2 * 128:(g2 + 1) * 128],
                            kk[ho % 2][:], start=False, stop=(ho == 31))
                    sv.inc(mm, "vmm")

            if h < 32:
                @block.vector
                def _(v, h=h, hp=hp, it=it):
                    sv.wait(v, "kmm", it + 1)
                    if it >= 2:
                        sv.wait(v, "vmm", it - 1)
                    v.tensor_scalar(kt[:], psK[hp][:], 0.0, None,
                                    op0=ALU.max)
                    sv.inc(v.tensor_tensor(kk[hp][:], kt[:], psK[hp][:],
                                           op=ALU.mult), "kev")

        for g2 in range(NG):
            ic = sc * NG + g2

            @block.vector
            def _(v, sc=sc, g2=g2, p=p, ic=ic):
                sv.wait(v, "vmm", (sc + 1) * 32)
                if ic >= 4:
                    sv.wait(v, "st", 16 * (ic - 3))
                sv.inc(v.tensor_tensor(
                    ob[ic % 4][:],
                    psC[g2 // 2][:, (g2 % 2) * CHA:(g2 % 2 + 1) * CHA],
                    sr[p][g2][:], op=ALU.mult), "cev")

            @block.sync
            def _(sy, g2=g2, ssl=ssl, ic=ic):
                sv.wait(sy, "cev", ic + 1)
                sv.dinc(sy.dma_start(out[g2][:, ssl], ob[ic % 4][:]), "st")

    @block.sync
    def _(sy):
        sv.waita(sy, "st", sv.val("st"))


def emit_router(nc, S, block, es, h_in, emb_in, id_in, wm_out, wmt_out):
    E = 16
    sv = SV(S, ["ld", "stw", "init", "lmm", "wmd", "tmm", "tev"])
    sb, ps = _allocs(nc, es)
    hb = [sb(f"hb{g}", [128, T], F32) for g in range(NG)]
    embf = sb("embf", [128, E], F32)
    ident = sb("identsb", [128, 128], F32)
    wms = [sb(f"wms{p}", [128, 512], F32) for p in range(2)]
    wmts = [sb(f"wmts{p}", [16, T], F32) for p in range(2)]
    el = sb("el", [128, 512], F32)
    w2b = sb("w2b", [128, 512], F32)
    mx = sb("mx", [128, 32], F32)
    psL = [ps(f"psL{p}", [128, 512], F32) for p in range(2)]
    psT = [ps(f"psT{p}", [128, 512], F32) for p in range(2)]

    @block.gpsimd
    def _(g_):
        for g in range(NG):
            sv.dinc(g_.dma_start(hb[g][:], h_in[g]), "ld")
        sv.dinc(g_.dma_start(embf[:], emb_in), "ld")
        sv.dinc(g_.dma_start(ident[:], id_in), "ld")
    ld_all = sv.val("ld")

    @block.vector
    def _(v):
        sv.waita(v, "ld", ld_all)
        sv.inc(v.nop(), "init")

    for g in range(NG):
        pg = g % 2

        @block.tensor
        def _(tn, g=g, pg=pg):
            if g == 0:
                sv.wait(tn, "init", 1)
            if g >= 2:
                sv.wait(tn, "wmd", 3 * (g - 2) + 1)
            for st in range(32):
                mm = tn.matmul(psL[pg][:, st * 16:(st + 1) * 16],
                               hb[g][:, st * 128:(st + 1) * 128],
                               embf[:], start=True, stop=True)
            sv.inc(mm, "lmm")

        @block.vector
        def _(v, g=g, pg=pg):
            sv.wait(v, "lmm", g + 1)
            if g >= 2:
                sv.wait(v, "stw", 16 * 2 * (g - 1))
            bc = lambda a: a.unsqueeze(-1).broadcast_to((128, 32, E))
            Lv = psL[pg][:].rearrange("p (s e) -> p s e", s=32)
            elv = el[:].rearrange("p (s e) -> p s e", s=32)
            wv_ = wms[pg][:].rearrange("p (s e) -> p s e", s=32)
            w2v = w2b[:].rearrange("p (s e) -> p s e", s=32)
            sv.fence(v, v.tensor_reduce(mx[:], Lv, axis=AXX, op=ALU.max))
            sv.inc(v.tensor_tensor(elv, Lv, bc(mx[:]), op=ALU.subtract),
                   "wmd")

        @block.scalar
        def _(s, g=g):
            sv.wait(s, "wmd", 3 * g + 1)
            sv.inc(s.activation(el[:], el[:], AF.Exp), "wmd")

        @block.vector
        def _(v, g=g, pg=pg):
            sv.wait(v, "wmd", 3 * g + 2)
            bc = lambda a: a.unsqueeze(-1).broadcast_to((128, 32, E))
            elv = el[:].rearrange("p (s e) -> p s e", s=32)
            wv_ = wms[pg][:].rearrange("p (s e) -> p s e", s=32)
            w2v = w2b[:].rearrange("p (s e) -> p s e", s=32)
            sv.fence(v, v.tensor_reduce(mx[:], elv, axis=AXX, op=ALU.add))
            sv.fence(v, v.reciprocal(mx[:], mx[:]))
            sv.fence(v, v.tensor_tensor(wv_, elv, bc(mx[:]), op=ALU.mult))
            sv.fence(v, v.tensor_reduce(mx[:], wv_, axis=AXX, op=ALU.max))
            sv.fence(v, v.tensor_tensor(w2v, wv_, bc(mx[:]), op=ALU.is_ge))
            sv.fence(v, v.scalar_tensor_tensor(w2b[:], w2b[:], -1e30,
                                               wms[pg][:], op0=ALU.mult,
                                               op1=ALU.add))
            sv.fence(v, v.tensor_reduce(mx[:], w2v, axis=AXX, op=ALU.max))
            sv.fence(v, v.tensor_tensor(w2v, wv_, bc(mx[:]), op=ALU.is_ge))
            sv.inc(v.tensor_tensor(wms[pg][:], wms[pg][:], w2b[:],
                                   op=ALU.mult), "wmd")

        for st in range(32):
            it = g * 32 + st

            @block.tensor
            def _(tn, g=g, st=st, it=it, pg=pg):
                sv.wait(tn, "wmd", 3 * (g + 1))
                if it >= 2:
                    sv.wait(tn, "tev", it - 1)
                sv.inc(tn.transpose(psT[st % 2][0:16, 0:128],
                                    wms[pg][:, st * 16:st * 16 + 16],
                                    ident[:]), "tmm")

            @block.vector
            def _(v, st=st, it=it, pg=pg):
                sv.wait(v, "tmm", it + 1)
                sv.inc(v.tensor_copy(wmts[pg][:, st * 128:(st + 1) * 128],
                                     psT[st % 2][0:16, 0:128]), "tev")

        @block.sync
        def _(sy, g=g, pg=pg):
            sv.wait(sy, "tev", (g + 1) * 32)
            sv.dinc(sy.dma_start(
                wm_out[g].rearrange("s p e -> p s e"),
                wms[pg][:].rearrange("p (s e) -> p s e", s=32)), "stw")
            sv.dinc(sy.dma_start(wmt_out[g], wmts[pg][:]), "stw")

    @block.sync
    def _(sy):
        sv.waita(sy, "stw", sv.val("stw"))


def emit_experts(nc, S, block, es, h_in, w1_in, b1_in, w2_in, b2_in, wm_in,
                 wmt_in, id_in, out):
    E, II = 16, 512
    SCE = 512
    NSC = T // SCE
    sv = SV(S, ["ld", "ldw", "stc", "init", "cdone", "bmm", "bev", "ctev",
                "hmm", "gev", "emm", "eev", "c16", "ctmm"])
    sb, ps = _allocs(nc, es)
    hb = [sb(f"hb{g}", [128, T], BF16) for g in range(NG)]
    w1b = [sb(f"w1b{e}", [128, II], BF16) for e in range(E)]
    w2b = [[sb(f"w2b{e}{i}", [128, 128], BF16) for i in range(4)]
           for e in range(E)]
    b1s = [[sb(f"b1s{e}{i}", [128, 1], F32) for i in range(4)]
           for e in range(E)]
    b2s = sb("b2s", [16, 128], F32)
    wmg = [sb(f"wmg{p}", [128, 512], F32) for p in range(2)]
    wmtg = [sb(f"wmtg{p}", [16, T], F32) for p in range(2)]
    identf = sb("identfsb", [128, 128], F32)
    identb = sb("identb", [128, 128], BF16)
    hm = [[sb(f"hm{p}{i}", [128, SCE], BF16) for i in range(4)]
          for p in range(2)]
    comb = sb("comb", [128, T], F32)
    cmb16 = [sb(f"cmb16{p}", [128, 128], BF16) for p in range(2)]
    ctb = [sb(f"ctb{p}", [128, T], BF16) for p in range(2)]
    psH = [ps(f"psH{i}", [128, 512], F32) for i in range(4)]
    psE = [ps(f"psE{i}", [128, 512], F32) for i in range(2)]
    psB = ps("psB", [128, 512], F32)
    psT = ps("psTe", [128, 512], BF16)

    @block.gpsimd
    def _(g_):
        for g in range(NG):
            sv.dinc(g_.dma_start(hb[g][:], h_in[g]), "ld")
        for e in range(E):
            sv.dinc(g_.dma_start(w1b[e][:], w1_in[e]), "ld")
            for i in range(4):
                sv.dinc(g_.dma_start(w2b[e][i][:], w2_in[e, i]), "ld")
                sv.dinc(g_.dma_start(b1s[e][i][:], b1_in[e, i]), "ld")
        sv.dinc(g_.dma_start(b2s[:], b2_in), "ld")
        sv.dinc(g_.dma_start(identf[:], id_in), "ld")
    ld_all = sv.val("ld")

    @block.vector
    def _(v):
        sv.waita(v, "ld", ld_all)
        sv.inc(v.tensor_copy(identb[:], identf[:]), "init")

    def psh(j, i):
        return psH[i][:]

    def te_mm2(tn, j, g, e, sc):
        jp = j % 2
        pe = j % 2
        if j >= 2:
            sv.wait(tn, "eev", (j - 1) * 4)
        for st2 in range(4):
            for i in range(4):
                mm = tn.matmul(psE[pe][:, st2 * 128:(st2 + 1) * 128],
                               hm[jp][i][:, st2 * 128:(st2 + 1) * 128],
                               w2b[e][i][:], start=(i == 0), stop=(i == 3))
        sv.inc(mm, "emm")

    def ve_evac(j, g, e, sc):
        @block.vector
        def _(v, g=g, e=e, sc=sc, j=j):
            pg = g % 2
            pe = j % 2
            sv.wait(v, "emm", j + 1)
            for st2 in range(4):
                st = sc * 4 + st2
                csl = comb[:, st * 128:(st + 1) * 128]
                sv.inc(v.scalar_tensor_tensor(
                    csl, psE[pe][:, st2 * 128:(st2 + 1) * 128],
                    wmg[pg][:, st * 16 + e:st * 16 + e + 1], csl,
                    op0=ALU.mult, op1=ALU.add), "eev")

    j = 0
    for g in range(NG):
        pg = g % 2

        @block.sync
        def _(sy, g=g, pg=pg):
            if g >= 2:
                sv.wait(sy, "cdone", g - 1)
            sv.dinc(sy.dma_start(
                wmg[pg][:].rearrange("p (s e) -> p s e", s=32),
                wm_in[g].rearrange("s p e -> p s e")), "ldw")
            sv.dinc(sy.dma_start(wmtg[pg][:], wmt_in[g]), "ldw")
        ldw_v = sv.val("ldw")

        for st in range(32):
            ib = g * 32 + st

            @block.tensor
            def _(tn, g=g, st=st, ib=ib, pg=pg, ldw_v=ldw_v):
                if ib == 0:
                    sv.waita(tn, "ld", ld_all)
                sv.waita(tn, "ldw", ldw_v)
                if ib >= 1:
                    sv.wait(tn, "bev", ib)
                sv.inc(tn.matmul(psB[:, (ib % 4) * 128:(ib % 4 + 1) * 128],
                                 wmtg[pg][:, st * 128:(st + 1) * 128],
                                 b2s[:], start=True, stop=True), "bmm")

            @block.vector
            def _(v, g=g, st=st, ib=ib):
                sv.wait(v, "bmm", ib + 1)
                if st == 0 and g >= 1:
                    sv.wait(v, "ctev", g * 32)
                sv.inc(v.tensor_copy(
                    comb[:, st * 128:(st + 1) * 128],
                    psB[:, (ib % 4) * 128:(ib % 4 + 1) * 128]), "bev")
        bev_v = sv.val("bev")

        first_j = j
        pend = []

        def flush_scalar():
            if not pend:
                return
            batch = list(pend)
            pend.clear()

            @block.scalar
            def _(s, batch=batch):
                for (jj, ee) in batch:
                    jp2 = jj % 2
                    if jj >= 2:
                        sv.wait(s, "emm", jj - 1)
                    for i in range(4):
                        sv.wait(s, "hmm", jj * 4 + i + 1)
                        sv.inc(s.activation(hm[jp2][i][:], psh(jj, i),
                                            AF.Gelu, bias=b1s[ee][i][:]),
                               "gev")

        for e in range(E):
            for sc in range(NSC):
                @block.tensor
                def _(tn, g=g, e=e, sc=sc, j=j, first_j=first_j,
                      bev_v=bev_v):
                    if e == 0 and sc == 0:
                        sv.waita(tn, "bev", bev_v)
                    for i in range(4):
                        if j >= 1:
                            sv.wait(tn, "gev", (j - 1) * 4 + i + 1)
                        mm = tn.matmul(psh(j, i),
                                       w1b[e][:, i * 128:(i + 1) * 128],
                                       hb[g][:, sc * SCE:(sc + 1) * SCE],
                                       start=True, stop=True)
                        sv.inc(mm, "hmm")
                    if j > first_j:
                        sv.wait(tn, "gev", j * 4)
                        ge, gsc = divmod(j - 1 - first_j, NSC)
                        te_mm2(tn, j - 1, g, ge, gsc)

                pend.append((j, e))
                if len(pend) >= 16:
                    flush_scalar()

                if j > first_j:
                    ge, gsc = divmod(j - 1 - first_j, NSC)
                    ve_evac(j - 1, g, ge, gsc)
                j += 1
        flush_scalar()

        @block.tensor
        def _(tn, g=g, j=j):
            sv.wait(tn, "gev", j * 4)
            te_mm2(tn, j - 1, g, E - 1, NSC - 1)
        ve_evac(j - 1, g, E - 1, NSC - 1)

        eev_need = j * 4
        for st in range(32):
            ic = g * 32 + st

            @block.vector
            def _(v, g=g, st=st, ic=ic, eev_need=eev_need):
                sv.waita(v, "eev", sv.base["eev"] + eev_need)
                if ic >= 2:
                    sv.wait(v, "ctmm", ic - 1)
                sv.inc(v.tensor_copy(cmb16[st % 2][:],
                                     comb[:, st * 128:(st + 1) * 128]),
                       "c16")

            @block.tensor
            def _(tn, st=st, ic=ic):
                sv.wait(tn, "c16", ic + 1)
                sv.wait(tn, "init", 1)
                if ic >= 1:
                    sv.wait(tn, "ctev", ic)
                sv.inc(tn.transpose(
                    psT[:, (st % 2) * 128:(st % 2) * 128 + 128],
                    cmb16[st % 2][:], identb[:]), "ctmm")

            @block.vector
            def _(v, g=g, st=st, ic=ic, pg=pg):
                sv.wait(v, "ctmm", ic + 1)
                if st == 0 and g >= 2:
                    sv.wait(v, "stc", 16 * (g - 1))
                sv.inc(v.tensor_copy(
                    ctb[pg][:, st * 128:(st + 1) * 128],
                    psT[:, (st % 2) * 128:(st % 2) * 128 + 128]), "ctev")

        @block.sync
        def _(sy, g=g, pg=pg):
            sv.wait(sy, "ctev", (g + 1) * 32)
            sv.inc(sy.nop(), "cdone")
            sv.dinc(sy.dma_start(out[g], ctb[pg][:]), "stc")

    @block.sync
    def _(sy):
        sv.waita(sy, "stc", sv.val("stc"))



def build_fused():
    nc = bass.Bass("TRN2", target_bir_lowering=False)
    EI = "ExternalInput"
    xT = cmajor(nc, "xT", EI, F32)
    wqT = pcol(nc, "wqT", EI, F32, n=C)
    wkT = pcol(nc, "wkT", EI, F32, n=C)
    wvT = pcol(nc, "wvT", EI, F32, n=C)
    woT = pcol(nc, "woT", EI, F32, n=C)
    bq = pcol(nc, "bq", EI)
    bk = pcol(nc, "bk", EI)
    bv_row = nc.dram_tensor("bvrow", [1, C], F32, kind=EI).ap()
    bo = pcol(nc, "bo", EI)
    lsc = pcol(nc, "lsc", EI)
    fker = pcol(nc, "fker", EI, BF16, n=C)
    sin_r = nc.dram_tensor("sinr", [128, 1], F32, kind=EI).ap()
    sk_r = nc.dram_tensor("skr", [128, 1], F32, kind=EI).ap()
    fbias = pcol(nc, "fbias", EI)
    ln1s = pcol(nc, "ln1s", EI)
    ln1b = pcol(nc, "ln1b", EI)
    ln2s = pcol(nc, "ln2s", EI)
    ln2b = pcol(nc, "ln2b", EI)
    tmk = pcol(nc, "tmk", EI)
    tmv = pcol(nc, "tmv", EI)
    tmr = pcol(nc, "tmr", EI)
    wdec = pcol(nc, "wdec", EI)
    ufr = pcol(nc, "ufr", EI)
    l1wk = pcol(nc, "l1wk", EI, F32, n=C)
    l1wv = pcol(nc, "l1wv", EI, F32, n=C)
    l1wr = pcol(nc, "l1wr", EI, F32, n=C)
    l1wo = pcol(nc, "l1wo", EI, F32, n=C)
    ftmk = pcol(nc, "ftmk", EI)
    ftmr = pcol(nc, "ftmr", EI)
    fwk = pcol(nc, "fwk", EI, F32, n=4096)
    fwr = pcol(nc, "fwr", EI, F32, n=C)
    fwv = pcol(nc, "fwv", EI, F32, n=C, c=4096)
    wm_w = pcol(nc, "wmw", EI, F32, n=C)
    bm = pcol(nc, "bm", EI)
    emb = nc.dram_tensor("emb", [128, 16], F32, kind=EI).ap()
    ident = nc.dram_tensor("ident", [128, 128], F32, kind=EI).ap()
    w1 = nc.dram_tensor("w1", [16 * 128, 512], F32, kind=EI).ap().rearrange(
        "(e p) i -> e p i", p=128)
    b1 = nc.dram_tensor("b1", [16 * 4 * 128, 1], F32, kind=EI).ap().rearrange(
        "(e i p) o -> e i p o", i=4, p=128)
    w2 = nc.dram_tensor("w2", [16 * 512, 128], F32, kind=EI).ap().rearrange(
        "(e i p) o -> e i p o", i=4, p=128)
    b2 = nc.dram_tensor("b2", [16, 128], F32, kind=EI).ap()
    ffwo = pcol(nc, "ffwo", EI, F32, n=C)
    ffbo = pcol(nc, "ffbo", EI)
    x4 = cmajor(nc, "x4", "ExternalOutput", F32)
    ln0 = cmajor(nc, "s_ln0", None, BF16)
    qn = cmajor(nc, "s_qn", None, BF16)
    kn = cmajor(nc, "s_kn", None, BF16)
    vT = nc.dram_tensor("s_vT", [T, C], BF16).ap().rearrange(
        "(n p) e -> n p e", p=128)
    a0 = cmajor(nc, "s_a0", None, BF16)
    x1 = cmajor(nc, "s_x1", None, F32)
    lnx1 = cmajor(nc, "s_lnx1", None, BF16)
    ybs = cmajor(nc, "s_yb", None, BF16)
    x2 = cmajor(nc, "s_x2", None, F32)
    z0 = cmajor(nc, "s_z0", None, F32)
    ln1 = cmajor(nc, "s_ln1", None, BF16)
    xk = cmajor(nc, "s_xk", None, BF16)
    xv = cmajor(nc, "s_xv", None, BF16)
    xr = cmajor(nc, "s_xr", None, BF16)
    k1 = cmajor(nc, "s_k1", None, F32)
    v1 = cmajor(nc, "s_v1", None, F32)
    srs = cmajor(nc, "s_sr", None, BF16)
    zs = cmajor(nc, "s_z", None, BF16)
    o1 = cmajor(nc, "s_o1", None, F32)
    x_i = cmajor(nc, "s_xi", None, F32)
    ln2 = cmajor(nc, "s_ln2", None, BF16)
    xk2 = cmajor(nc, "s_xk2", None, BF16)
    xr2 = cmajor(nc, "s_xr2", None, BF16)
    sr2 = cmajor(nc, "s_sr2", None, BF16)
    cmv = cmajor(nc, "s_cm", None, F32)
    att = cmajor(nc, "s_att", None, F32)
    x3 = cmajor(nc, "s_x3", None, F32)
    ln4 = cmajor(nc, "s_ln4", None, BF16)
    hs = cmajor(nc, "s_h", None, F32)
    wm = nc.dram_tensor("s_wm", [NG * 32 * 128, 16], F32).ap().rearrange(
        "(g s p) e -> g s p e", s=32, p=128)
    wmt = nc.dram_tensor("s_wmt", [NG * 16, T], F32).ap().rearrange(
        "(g p) t -> g p t", p=16)
    combt = cmajor(nc, "s_combt", None, BF16)
    mout = cmajor(nc, "s_mout", None, BF16)

    es0 = ExitStack()
    with es0:
        S = Sems(nc, es0)
        with nc.Block() as block:
            def stage(emitfn, *a, **k):
                with ExitStack() as es:
                    emitfn(nc, S, block, es, *a, **k)
                barrier(nc, S, block)

            stage(emit_ln, xT, ln0)
            stage(emit_projln, ln0, wqT, bq, qn)
            stage(emit_projln, ln0, wkT, bk, kn)
            stage(emit_vproj, ln0, wvT, bv_row, vT)
            stage(emit_l0attn, qn, kn, vT, woT, bo, lsc, a0)
            stage(emit_postres, a0, xT, x1)
            stage(emit_ln, x1, lnx1)
            stage(emit_l0ffn, lnx1, fker, sin_r, sk_r, fbias, ybs)
            stage(emit_postres, ybs, x1, x2)
            stage(emit_ln, x2, z0, out_dtype=F32)
            stage(emit_ln, z0, ln1, sc=ln1s, bi=ln1b)
            stage(emit_mix, ln1, [tmk, tmv, tmr], [xk, xv, xr])
            stage(emit_mm, xk, l1wk, k1, evac="copy", out_dtype=F32)
            stage(emit_mm, xv, l1wv, v1, evac="copy", out_dtype=F32)
            stage(emit_mm, xr, l1wr, srs, evac="sigmoid")
            stage(emit_wkv, k1, v1, srs, wdec, ufr, zs)
            stage(emit_mm, zs, l1wo, o1, evac="copy", out_dtype=F32)
            stage(emit_lif, o1, z0, x_i)
            stage(emit_ln, x_i, ln2, sc=ln2s, bi=ln2b)
            stage(emit_mix, ln2, [ftmk, ftmr], [xk2, xr2])
            stage(emit_mm, xr2, fwr, sr2, evac="sigmoid")
            stage(emit_cmix, xk2, sr2, fwk, fwv, cmv)
            stage(emit_lif, cmv, x_i, att)
            stage(emit_postres, att, x2, x3, so_f32=True)
            stage(emit_ln, x3, ln4)
            stage(emit_mm, ln4, wm_w, hs, evac="copy", out_dtype=F32,
                  b_in=bm)
            stage(emit_router, hs, emb, ident, wm, wmt)
            stage(emit_experts, hs, w1, b1, w2, b2, wm, wmt, ident, combt)
            stage(emit_mm, combt, ffwo, mout, evac="copy", out_dtype=BF16,
                  b_in=ffbo)
            stage(emit_postres, mout, x3, x4)
    return nc


_FUSED_NC = [None]


def kernel(**inp):
    f32 = np.float32
    x = inp["L0_attn_x"]
    col = lambda a: np.ascontiguousarray(np.asarray(a, f32).reshape(-1, 1))
    rep = lambda s: np.repeat(np.asarray(s, f32), 128).reshape(128, 1)
    ca = np.ascontiguousarray
    shared = {
        "wqT": ca(inp["L0_attn_Wq"].T.astype(f32)),
        "wkT": ca(inp["L0_attn_Wk"].T.astype(f32)),
        "wvT": ca(inp["L0_attn_Wv"].T.astype(f32)),
        "woT": ca(inp["L0_attn_Wo"].T.astype(f32)),
        "bq": col(inp["L0_attn_bq"]),
        "bk": col(inp["L0_attn_bk"]),
        "bvrow": ca(inp["L0_attn_bv"].astype(f32).reshape(1, -1)),
        "bo": col(inp["L0_attn_bo"]),
        "lsc": col(inp["L0_attn_layerscale"]),
        "fker": ca(inp["L0_ffn_kernel"]),
        "sinr": rep(inp["L0_ffn_input_scale"]),
        "skr": rep(inp["L0_ffn_kernel_scale"]),
        "fbias": ca(inp["L0_ffn_bias"].astype(f32).reshape(-1, 1)),
        "ln1s": col(inp["L1_attn_ln1_s"]),
        "ln1b": col(inp["L1_attn_ln1_b"]),
        "ln2s": col(inp["L1_attn_ln2_s"]),
        "ln2b": col(inp["L1_attn_ln2_b"]),
        "tmk": col(inp["L1_attn_tmk"]),
        "tmv": col(inp["L1_attn_tmv"]),
        "tmr": col(inp["L1_attn_tmr"]),
        "wdec": col(inp["L1_attn_w_decay"]),
        "ufr": col(inp["L1_attn_u_first"]),
        "l1wk": ca(inp["L1_attn_Wk"].astype(f32)),
        "l1wv": ca(inp["L1_attn_Wv"].astype(f32)),
        "l1wr": ca(inp["L1_attn_Wr"].astype(f32)),
        "l1wo": ca(inp["L1_attn_Wo"].astype(f32)),
        "ftmk": col(inp["L1_attn_f_tmk"]),
        "ftmr": col(inp["L1_attn_f_tmr"]),
        "fwk": ca(inp["L1_attn_fWk"].astype(f32)),
        "fwr": ca(inp["L1_attn_fWr"].astype(f32)),
        "fwv": ca(inp["L1_attn_fWv"].astype(f32)),
        "wmw": ca(inp["L1_ffn_Wm"].astype(f32)),
        "bm": col(inp["L1_ffn_bm"]),
        "emb": ca(inp["L1_ffn_expert_emb"].astype(f32)),
        "ident": np.eye(128, dtype=f32),
        "w1": ca(inp["L1_ffn_W1"].astype(f32).reshape(-1, 512)),
        "b1": ca(inp["L1_ffn_b1"].astype(f32).reshape(-1, 1)),
        "w2": ca(inp["L1_ffn_W2"].astype(f32).reshape(-1, 128)),
        "b2": ca(inp["L1_ffn_b2"].astype(f32)),
        "ffwo": ca(inp["L1_ffn_Wo"].astype(f32)),
        "ffbo": col(inp["L1_ffn_bo"]),
    }
    if _FUSED_NC[0] is None:
        _FUSED_NC[0] = build_fused()
    nc = _FUSED_NC[0]
    in_maps = []
    for b in range(NCORES):
        m = dict(shared)
        m["xT"] = np.ascontiguousarray(x[b].T)
        in_maps.append(m)
    res = run_bass_kernel_spmd(nc, in_maps, core_ids=list(range(NCORES)))
    out = np.stack([np.ascontiguousarray(res.results[b]["x4"].T)
                    for b in range(NCORES)])
    return out.astype(np.float32)


# version 12
# speedup vs baseline: 1.3676x; 1.0246x over previous
import numpy as np
import ml_dtypes
from contextlib import ExitStack
import concourse.bass as bass
import concourse.mybir as mybir
from concourse.bass_utils import run_bass_kernel_spmd

F32 = mybir.dt.float32
BF16 = mybir.dt.bfloat16
FP8 = mybir.dt.float8e4
AF = mybir.ActivationFunctionType
ALU = mybir.AluOpType
AXX = mybir.AxisListType.X

T = 4096
C = 1024
NG = 8
CH = 512
NCH = T // CH
LN_EPS = 1e-5
NCORES = 8

bfloat16 = ml_dtypes.bfloat16


class Sems:

    def __init__(self, nc, es):
        self.nc, self.es = nc, es
        self.h, self.c = {}, {}

    def get(self, n):
        if n not in self.h:
            self.h[n] = self.es.enter_context(self.nc.semaphore(n))
            self.c[n] = 0
        return self.h[n]

    def inc(self, instr, n, k=1):
        instr.then_inc(self.get(n), k)
        self.c[n] += k
        return self.c[n]

    def dinc(self, instr, n):
        return self.inc(instr, n, 16)

    def val(self, n):
        self.get(n)
        return self.c[n]

    def awach(self, eng, n, v):
        eng.wait_ge(self.get(n), v)

    def fence(self, eng, instr, tag="fnc"):
        v = self.inc(instr, tag)
        eng.wait_ge(self.get(tag), v)


class SV:

    def __init__(self, S, names):
        self.S = S
        self.base = {n: S.val(n) for n in names}

    def inc(self, instr, n, k=1):
        return self.S.inc(instr, n, k)

    def dinc(self, instr, n):
        return self.S.inc(instr, n, 16)

    def val(self, n):
        return self.S.val(n)

    def rel(self, n):
        return self.S.val(n) - self.base[n]

    def wait(self, eng, n, v_rel):
        eng.wait_ge(self.S.get(n), self.base[n] + v_rel)

    def waita(self, eng, n, v_abs):
        eng.wait_ge(self.S.get(n), v_abs)

    def fence(self, eng, instr):
        self.S.fence(eng, instr)


_UID = [0]


def _allocs(nc, es):
    _UID[0] += 1
    u = f"u{_UID[0]}_"
    sb = lambda name, *a: es.enter_context(nc.sbuf_tensor(u + name, *a))
    ps = lambda name, *a: es.enter_context(nc.psum_tensor(u + name, *a))
    return sb, ps


def barrier(nc, S, block):
    tgt = S.val("bar") + 5
    for engname in ("tensor", "vector", "scalar", "gpsimd", "sync"):
        eng = getattr(nc, engname)

        def mk(eng=eng, engname=engname):
            @getattr(block, engname)
            def _(e):
                S.inc(e.nop(), "bar")
                S.awach(e, "bar", tgt)
        mk()


def cmajor(nc, name, kind=None, dtype=F32, t=T):
    h = (nc.dram_tensor(name, [C, t], dtype, kind=kind) if kind
         else nc.dram_tensor(name, [C, t], dtype))
    return h.ap().rearrange("(g p) t -> g p t", p=128)


def pcol(nc, name, kind="ExternalInput", dtype=F32, n=1, c=C):
    h = nc.dram_tensor(name, [c, n], dtype, kind=kind)
    return h.ap().rearrange("(g p) t -> g p t", p=128)


def LN_SEMS(tag=""):
    return [n + tag for n in ("sq", "smm", "ev1", "acta", "ev2", "actc",
                              "rmm", "apl", "lninit", "sqrdy")]


class LNStats:

    def __init__(self, nc, es, sv, apl_per_chunk, tag="", rep_pairs=2):
        self.sv = sv
        self.apc = apl_per_chunk
        self.tag = tag
        self.rep_pairs = rep_pairs
        sb, ps = _allocs(nc, es)
        nsq = 2 if rep_pairs == 2 else 1
        self.sq = [[sb(f"lnsq{p}{g}", [128, CH], BF16) for g in range(NG)]
                   for p in range(nsq)]
        self.ps_sum = ps("lnpsum", [1, CH], F32)
        self.ps_sq = ps("lnpssq", [1, CH], F32)
        self.rep_m = [ps(f"lnrepm{p}", [128, CH], F32)
                      for p in range(rep_pairs)]
        self.rep_r = [ps(f"lnrepr{p}", [128, CH], F32)
                      for p in range(rep_pairs)]
        self.m_row = [sb(f"lnmrow{p}", [1, CH], F32) for p in range(2)]
        self.sq_row = [sb(f"lnsqrow{p}", [1, CH], F32) for p in range(2)]
        self.t_row = [sb(f"lntrow{p}", [1, CH], F32) for p in range(2)]
        self.r_row = [sb(f"lnrrow{p}", [1, CH], F32) for p in range(2)]
        self.ones_bf = sb("lnonesb", [128, 1], BF16)
        self.ones_f = sb("lnonesf", [1, 128], F32)
        self.eps_c = sb("lneps", [1, 1], F32)
        self.nh_c = sb("lnnh", [1, 1], F32)

    def emit_init(self, block):
        sv = self.sv

        @block.vector
        def _(v):
            v.memset(self.ones_bf[:], 1.0)
            v.memset(self.eps_c[:], LN_EPS)
            v.memset(self.nh_c[:], -0.5)
            sv.inc(v.memset(self.ones_f[:], 1.0), "lninit" + self.tag)

    def rp(self, j):
        return j % self.rep_pairs

    def emit_chunk(self, block, j, xsl):
        sv = self.sv
        p = j % 2 if self.rep_pairs == 2 else 0
        rp = self.rp(j)

        @block.scalar
        def _(s):
            if j == 0:
                sv.wait(s, "lninit" + self.tag, 1)
            if self.rep_pairs == 2:
                if j >= 2:
                    sv.wait(s, "smm" + self.tag, (j - 1) * 2)
            else:
                if j >= 1:
                    sv.wait(s, "smm" + self.tag, j * 2)
            sv.wait(s, "sqrdy" + self.tag, j + 1)
            for g in range(NG):
                sv.inc(s.activation(self.sq[p][g][:], xsl[g], AF.Square),
                       "sq" + self.tag)
        sq_v = sv.val("sq" + self.tag)

        @block.tensor
        def _(tn):
            sv.waita(tn, "sq" + self.tag, sq_v)
            if j >= 1:
                sv.wait(tn, "ev1" + self.tag, j)
            for g in range(NG):
                tn.matmul(self.ps_sum[:], self.ones_bf[:], xsl[g],
                          start=(g == 0), stop=(g == NG - 1))
            for g in range(NG):
                mm = tn.matmul(self.ps_sq[:], self.ones_bf[:],
                               self.sq[p][g][:], start=(g == 0),
                               stop=(g == NG - 1))
            sv.inc(mm, "smm" + self.tag, 2)

        @block.vector
        def _(v):
            sv.wait(v, "smm" + self.tag, (j + 1) * 2)
            if j >= 2:
                sv.wait(v, "rmm" + self.tag, j - 1)
            v.tensor_scalar(self.m_row[p][:], self.ps_sum[:],
                            1.0 / C, None, op0=ALU.mult)
            sv.fence(v, v.tensor_scalar(self.sq_row[p][:], self.ps_sq[:],
                                        1.0 / C, None, op0=ALU.mult))
            sv.fence(v, v.tensor_tensor(self.t_row[p][:], self.m_row[p][:],
                                        self.m_row[p][:], op=ALU.mult))
            sv.inc(v.tensor_tensor(self.t_row[p][:], self.sq_row[p][:],
                                   self.t_row[p][:], op=ALU.subtract),
                   "ev1" + self.tag)

        @block.scalar
        def _(s):
            sv.wait(s, "ev1" + self.tag, j + 1)
            s.activation(self.t_row[p][:], self.t_row[p][:], AF.Ln,
                         bias=self.eps_c[:])
            sv.inc(s.activation(self.r_row[p][:], self.t_row[p][:], AF.Exp,
                                scale=self.nh_c[:]), "actc" + self.tag)

        @block.tensor
        def _(tn):
            sv.wait(tn, "actc" + self.tag, j + 1)
            if self.rep_pairs == 2:
                if j >= 2:
                    sv.wait(tn, "apl" + self.tag, (j - 1) * self.apc)
            else:
                if j >= 1:
                    sv.wait(tn, "apl" + self.tag, j * self.apc)
            tn.matmul(self.rep_m[rp][:], self.ones_f[:], self.m_row[p][:],
                      start=True, stop=True)
            sv.inc(tn.matmul(self.rep_r[rp][:], self.ones_f[:],
                             self.r_row[p][:], start=True, stop=True),
                   "rmm" + self.tag)
        return sv.val("rmm" + self.tag)



def emit_ln(nc, S, block, es, xin, out, sc=None, bi=None, out_dtype=BF16):
    sv = SV(S, ["ld", "ldf", "st", "apl"] + LN_SEMS())
    sb, ps = _allocs(nc, es)
    xb = [sb(f"xb{g}", [128, T], BF16) for g in range(NG)]
    xf = [[sb(f"xf{p}{g}", [128, CH], F32) for g in range(NG)]
          for p in range(2)]
    tmp = sb("tmp", [128, CH], F32)
    ob = [[sb(f"ob{p}{g}", [128, CH], out_dtype) for g in range(NG)]
          for p in range(2)]
    scale_bias = sc is not None
    if scale_bias:
        scs = [sb(f"scs{g}", [128, 1], F32) for g in range(NG)]
        bis = [sb(f"bis{g}", [128, 1], F32) for g in range(NG)]
    L = LNStats(nc, es, sv, apl_per_chunk=NG)
    L.emit_init(block)

    @block.gpsimd
    def _(g_):
        for g in range(NG):
            sv.dinc(g_.dma_start(xb[g][:], xin[g]), "ld")
        if scale_bias:
            for g in range(NG):
                sv.dinc(g_.dma_start(scs[g][:], sc[g]), "ld")
                sv.dinc(g_.dma_start(bis[g][:], bi[g]), "ld")
    ld_all = sv.val("ld")

    for j in range(NCH):
        p = j % 2
        sl = slice(j * CH, (j + 1) * CH)

        @block.gpsimd
        def _(gp, j=j, p=p, sl=sl):
            if j >= 2:
                sv.wait(gp, "apl", (j - 1) * NG)
            for g in range(NG):
                sv.dinc(gp.dma_start(xf[p][g][:], xin[g][:, sl]), "ldf")
        ldf_v = sv.val("ldf")

        @block.scalar
        def _(s, j=j):
            if j == 0:
                sv.waita(s, "ld", ld_all)
            sv.inc(s.nop(), "sqrdy")
        rmm_v = L.emit_chunk(block, j, [xb[g][:, sl] for g in range(NG)])

        @block.vector
        def _(v, j=j, p=p, rmm_v=rmm_v, ldf_v=ldf_v):
            sv.waita(v, "rmm", rmm_v)
            sv.waita(v, "ldf", ldf_v)
            if j >= 2:
                sv.wait(v, "st", 16 * NG * (j - 1))
            for g in range(NG):
                v.tensor_tensor(tmp[:], xf[p][g][:], L.rep_m[p][:],
                                op=ALU.subtract)
                if scale_bias:
                    v.tensor_tensor(tmp[:], tmp[:], L.rep_r[p][:],
                                    op=ALU.mult)
                    sv.inc(v.tensor_scalar(ob[p][g][:], tmp[:], scs[g][:],
                                           bis[g][:], op0=ALU.mult,
                                           op1=ALU.add), "apl")
                else:
                    sv.inc(v.tensor_tensor(ob[p][g][:], tmp[:],
                                           L.rep_r[p][:], op=ALU.mult),
                           "apl")
        apl_v = sv.val("apl")

        @block.sync
        def _(sy, j=j, p=p, sl=sl, apl_v=apl_v):
            sv.waita(sy, "apl", apl_v)
            for g in range(NG):
                sv.dinc(sy.dma_start(out[g][:, sl], ob[p][g][:]), "st")

    @block.sync
    def _(sy):
        sv.waita(sy, "st", sv.val("st"))


def emit_postres(nc, S, block, es, so_in, x_in, out, so_f32=False):
    sv = SV(S, ["ld", "ldx", "st", "apl"] + LN_SEMS())
    sb, ps = _allocs(nc, es)
    sob = [sb(f"sob{g}", [128, T], BF16) for g in range(NG)]
    sof = ([[sb(f"sof{p}{g}", [128, CH], F32) for g in range(NG)]
            for p in range(2)] if so_f32 else None)
    xo = [[sb(f"xo{p}{g}", [128, CH], F32) for g in range(NG)]
          for p in range(2)]
    tmp = sb("tmp", [128, CH], F32)
    ob = [[sb(f"ob{p}{g}", [128, CH], F32) for g in range(NG)]
          for p in range(2)]
    L = LNStats(nc, es, sv, apl_per_chunk=NG)
    L.emit_init(block)

    @block.gpsimd
    def _(g_):
        for g in range(NG):
            sv.dinc(g_.dma_start(sob[g][:], so_in[g]), "ld")
    ld_all = sv.val("ld")

    for j in range(NCH):
        p = j % 2
        sl = slice(j * CH, (j + 1) * CH)

        @block.gpsimd
        def _(gp, j=j, p=p, sl=sl):
            if j >= 2:
                sv.wait(gp, "apl", (j - 1) * NG)
            for g in range(NG):
                sv.dinc(gp.dma_start(xo[p][g][:], x_in[g][:, sl]), "ldx")
                if sof is not None:
                    sv.dinc(gp.dma_start(sof[p][g][:], so_in[g][:, sl]),
                            "ldx")
        ldx_v = sv.val("ldx")

        @block.scalar
        def _(s, j=j):
            if j == 0:
                sv.waita(s, "ld", ld_all)
            sv.inc(s.nop(), "sqrdy")
        rmm_v = L.emit_chunk(block, j, [sob[g][:, sl] for g in range(NG)])

        @block.vector
        def _(v, j=j, p=p, sl=sl, rmm_v=rmm_v, ldx_v=ldx_v):
            sv.waita(v, "rmm", rmm_v)
            sv.waita(v, "ldx", ldx_v)
            if j >= 2:
                sv.wait(v, "st", 16 * NG * (j - 1))
            for g in range(NG):
                v.tensor_tensor(tmp[:],
                                (sof[p][g][:] if sof is not None
                                 else sob[g][:, sl]),
                                L.rep_m[p][:], op=ALU.subtract)
                v.tensor_tensor(tmp[:], tmp[:], L.rep_r[p][:], op=ALU.mult)
                sv.inc(v.tensor_tensor(ob[p][g][:], tmp[:], xo[p][g][:],
                                       op=ALU.add), "apl")
        apl_v = sv.val("apl")

        @block.sync
        def _(sy, j=j, p=p, sl=sl, apl_v=apl_v):
            sv.waita(sy, "apl", apl_v)
            for g in range(NG):
                sv.dinc(sy.dma_start(out[g][:, sl], ob[p][g][:]), "st")

    @block.sync
    def _(sy):
        sv.waita(sy, "st", sv.val("st"))


def emit_postres_ln(nc, S, block, es, so_in, x_in, out_x, out_ln,
                    so_f32=False, ln_dtype=BF16, ln_sc=None, ln_bi=None):
    sv = SV(S, ["ld", "ldx", "ldb", "stx", "st2", "xc"]
            + LN_SEMS() + LN_SEMS("B"))
    sb, ps = _allocs(nc, es)
    sob = [[sb(f"sob{p}{g}", [128, CH], BF16) for g in range(NG)]
           for p in range(2)]
    sof = ([[sb(f"sof{p}{g}", [128, CH], F32) for g in range(NG)]
            for p in range(2)] if so_f32 else None)
    xo = [[sb(f"xo{p}{g}", [128, CH], F32) for g in range(NG)]
          for p in range(2)]
    tmp = sb("tmp", [128, CH], F32)
    xn = [[sb(f"xn{p}{g}", [128, CH], F32) for g in range(NG)]
          for p in range(2)]
    xnb = [[sb(f"xnb{p}{g}", [128, CH], BF16) for g in range(NG)]
           for p in range(2)]
    ob2 = [[sb(f"ob2{p}{g}", [128, CH], ln_dtype) for g in range(NG)]
           for p in range(2)]
    if ln_sc is not None:
        scs = [sb(f"scs{g}", [128, 1], F32) for g in range(NG)]
        bis = [sb(f"bis{g}", [128, 1], F32) for g in range(NG)]
    L1 = LNStats(nc, es, sv, apl_per_chunk=NG, tag="", rep_pairs=1)
    L2 = LNStats(nc, es, sv, apl_per_chunk=NG, tag="B", rep_pairs=1)
    L1.emit_init(block)
    L2.emit_init(block)

    @block.gpsimd
    def _(g_):
        if ln_sc is not None:
            for g in range(NG):
                sv.dinc(g_.dma_start(scs[g][:], ln_sc[g]), "ld")
                sv.dinc(g_.dma_start(bis[g][:], ln_bi[g]), "ld")
        sv.inc(g_.nop(), "ld")
    ld_all = sv.val("ld")

    for j in range(NCH):
        p = j % 2
        sl = slice(j * CH, (j + 1) * CH)

        @block.sync
        def _(sy, j=j, p=p, sl=sl):
            if j >= 2:
                sv.wait(sy, "apl", (j - 1) * NG)
            for g in range(NG):
                sv.dinc(sy.dma_start(xo[p][g][:], x_in[g][:, sl]), "ldx")
                if sof is not None:
                    sv.dinc(sy.dma_start(sof[p][g][:], so_in[g][:, sl]),
                            "ldx")
        ldx_v = sv.val("ldx")

        @block.gpsimd
        def _(g_, j=j, p=p, sl=sl):
            if j >= 2:
                sv.wait(g_, "smm", j * 2)
            for g in range(NG):
                sv.dinc(g_.dma_start(sob[p][g][:], so_in[g][:, sl]), "ldb")
        ldb_v = sv.val("ldb")

        @block.scalar
        def _(s, j=j, ldb_v=ldb_v):
            if j == 0:
                sv.waita(s, "ld", ld_all)
            sv.waita(s, "ldb", ldb_v)
            sv.inc(s.nop(), "sqrdy")
        rmm_v = L1.emit_chunk(block, j,
                              [sob[p][g][:] for g in range(NG)])

        @block.vector
        def _(v, j=j, p=p, sl=sl, rmm_v=rmm_v, ldx_v=ldx_v):
            sv.waita(v, "rmm", rmm_v)
            sv.waita(v, "ldx", ldx_v)
            if j >= 2:
                sv.wait(v, "aplB", (j - 1) * NG)
                sv.wait(v, "stx", 16 * NG * (j - 1))
                sv.wait(v, "smmB", (j - 1) * 2)
            rp1 = L1.rp(j)
            for g in range(NG):
                v.tensor_tensor(tmp[:],
                                (sof[p][g][:] if sof is not None
                                 else sob[p][g][:]),
                                L1.rep_m[rp1][:], op=ALU.subtract)
                v.tensor_tensor(tmp[:], tmp[:], L1.rep_r[rp1][:],
                                op=ALU.mult)
                sv.inc(v.tensor_tensor(xn[p][g][:], tmp[:], xo[p][g][:],
                                       op=ALU.add), "apl")
                sv.inc(v.tensor_copy(xnb[p][g][:], xn[p][g][:]), "xc")
        apl_v = sv.val("apl")
        xc_v = sv.val("xc")

        @block.sync
        def _(sy, j=j, p=p, sl=sl, apl_v=apl_v):
            sv.waita(sy, "apl", apl_v)
            for g in range(NG):
                sv.dinc(sy.dma_start(out_x[g][:, sl], xn[p][g][:]), "stx")

        @block.scalar
        def _(s, xc_v=xc_v):
            sv.waita(s, "xc", xc_v)
            sv.inc(s.nop(), "sqrdyB")
        rmm2_v = L2.emit_chunk(block, j, [xnb[p][g][:] for g in range(NG)])

        @block.vector
        def _(v, j=j, p=p, rmm2_v=rmm2_v):
            sv.waita(v, "rmmB", rmm2_v)
            if j >= 2:
                sv.wait(v, "st2", 16 * NG * (j - 1))
            rp2 = L2.rp(j)
            for g in range(NG):
                v.tensor_tensor(tmp[:], xn[p][g][:], L2.rep_m[rp2][:],
                                op=ALU.subtract)
                if ln_sc is not None:
                    v.tensor_tensor(tmp[:], tmp[:], L2.rep_r[rp2][:],
                                    op=ALU.mult)
                    sv.inc(v.tensor_scalar(ob2[p][g][:], tmp[:], scs[g][:],
                                           bis[g][:], op0=ALU.mult,
                                           op1=ALU.add), "aplB")
                else:
                    sv.inc(v.tensor_tensor(ob2[p][g][:], tmp[:],
                                           L2.rep_r[rp2][:], op=ALU.mult),
                           "aplB")
        aplB_v = sv.val("aplB")

        @block.sync
        def _(sy, j=j, p=p, sl=sl, aplB_v=aplB_v):
            sv.waita(sy, "aplB", aplB_v)
            for g in range(NG):
                sv.dinc(sy.dma_start(out_ln[g][:, sl], ob2[p][g][:]), "st2")

    @block.sync
    def _(sy):
        sv.waita(sy, "stx", sv.val("stx"))
        sv.waita(sy, "st2", sv.val("st2"))


def emit_mm(nc, S, block, es, xin, w_in, out, evac="copy", out_dtype=BF16,
            b_in=None):
    sv = SV(S, ["ld", "st", "mm", "ev"])
    sb, ps = _allocs(nc, es)
    xb = [sb(f"xb{g}", [128, T], BF16) for g in range(NG)]
    wb = [sb(f"wb{g}", [128, C], BF16) for g in range(NG)]
    pp = [ps(f"pp{i}", [128, CH], F32) for i in range(4)]
    ob = [sb(f"ob{i}", [128, CH], out_dtype) for i in range(4)]
    if b_in is not None:
        bs = [sb(f"bs{g}", [128, 1], F32) for g in range(NG)]

    @block.gpsimd
    def _(g_):
        for g in range(NG):
            sv.dinc(g_.dma_start(xb[g][:], xin[g]), "ld")
        for g in range(NG):
            sv.dinc(g_.dma_start(wb[g][:], w_in[g]), "ld")
        if b_in is not None:
            for g in range(NG):
                sv.dinc(g_.dma_start(bs[g][:], b_in[g]), "ld")
    ld_all = sv.val("ld")

    idx = 0
    for g2 in range(NG):
        for j in range(NCH):
            q = idx % 4
            sl = slice(j * CH, (j + 1) * CH)

            @block.tensor
            def _(tn, g2=g2, j=j, q=q, sl=sl, idx=idx):
                if idx == 0:
                    sv.waita(tn, "ld", ld_all)
                if idx >= 4:
                    sv.wait(tn, "ev", idx - 3)
                for d in range(NG):
                    mm = tn.matmul(pp[q][:],
                                   wb[d][:, g2 * 128:(g2 + 1) * 128],
                                   xb[d][:, sl], start=(d == 0),
                                   stop=(d == NG - 1))
                sv.inc(mm, "mm")

            if evac == "sigmoid":
                if j == NCH - 1:
                    @block.scalar
                    def _(s, g2=g2, idx=idx):
                        for jj in range(NCH):
                            ix = g2 * NCH + jj
                            sv.wait(s, "mm", ix + 1)
                            if ix >= 4:
                                sv.wait(s, "st", 16 * (ix - 3))
                            sv.inc(s.activation(ob[ix % 4][:], pp[ix % 4][:],
                                                AF.Sigmoid), "ev")
            else:
                @block.vector
                def _(v, g2=g2, q=q, idx=idx):
                    sv.wait(v, "mm", idx + 1)
                    if idx >= 4:
                        sv.wait(v, "st", 16 * (idx - 3))
                    if b_in is not None:
                        sv.inc(v.tensor_scalar(ob[idx % 4][:], pp[q][:],
                                               bs[g2][:], None, op0=ALU.add),
                               "ev")
                    else:
                        sv.inc(v.tensor_copy(ob[idx % 4][:], pp[q][:]), "ev")

            @block.sync
            def _(sy, g2=g2, q=q, sl=sl, idx=idx):
                sv.wait(sy, "ev", idx + 1)
                sv.dinc(sy.dma_start(out[g2][:, sl], ob[idx % 4][:]), "st")
            idx += 1

    @block.sync
    def _(sy):
        sv.waita(sy, "st", sv.val("st"))


def emit_mix(nc, S, block, es, xin, tms, outs):
    nmix = len(tms)
    sv = SV(S, ["ld", "st", "ev"])
    sb, ps = _allocs(nc, es)
    xb = [sb(f"xb{g}", [128, T], BF16) for g in range(NG)]
    tmv = [[sb(f"tm{k}{g}", [128, 1], F32) for g in range(NG)]
           for k in range(nmix)]
    tmp = sb("tmp", [128, CH], F32)
    ob = [sb(f"ob{i}", [128, CH], BF16) for i in range(4)]

    @block.gpsimd
    def _(g_):
        for g in range(NG):
            sv.dinc(g_.dma_start(xb[g][:], xin[g]), "ld")
        for k in range(nmix):
            for g in range(NG):
                sv.dinc(g_.dma_start(tmv[k][g][:], tms[k][g]), "ld")
    ld_all = sv.val("ld")

    idx = 0
    for k in range(nmix):
        for g in range(NG):
            for j in range(NCH):
                q = idx % 4
                sl = slice(j * CH, (j + 1) * CH)

                @block.vector
                def _(v, k=k, g=g, j=j, q=q, sl=sl, idx=idx):
                    if idx == 0:
                        sv.waita(v, "ld", ld_all)
                    if idx >= 4:
                        sv.wait(v, "st", 16 * (idx - 3))
                    if j == 0:
                        v.tensor_scalar(ob[q][:, 0:1], xb[g][:, 0:1],
                                        tmv[k][g][:], None, op0=ALU.mult)
                        xx = xb[g][:, 0:CH - 1]
                        v.tensor_tensor(tmp[:, 1:CH], xb[g][:, 1:CH], xx,
                                        op=ALU.subtract)
                        sv.inc(v.scalar_tensor_tensor(
                            ob[q][:, 1:CH], tmp[:, 1:CH], tmv[k][g][:], xx,
                            op0=ALU.mult, op1=ALU.add), "ev")
                    else:
                        xx = xb[g][:, j * CH - 1:(j + 1) * CH - 1]
                        v.tensor_tensor(tmp[:], xb[g][:, sl], xx,
                                        op=ALU.subtract)
                        sv.inc(v.scalar_tensor_tensor(
                            ob[q][:], tmp[:], tmv[k][g][:], xx,
                            op0=ALU.mult, op1=ALU.add), "ev")

                @block.sync
                def _(sy, k=k, g=g, q=q, sl=sl, idx=idx):
                    sv.wait(sy, "ev", idx + 1)
                    sv.dinc(sy.dma_start(outs[k][g][:, sl], ob[q][:]), "st")
                idx += 1

    @block.sync
    def _(sy):
        sv.waita(sy, "st", sv.val("st"))


def emit_projln(nc, S, block, es, xin, w_in, b_in, out):
    sv = SV(S, ["ld", "st", "qmm", "qev", "apl"] + LN_SEMS())
    sb, ps = _allocs(nc, es)
    xb = [sb(f"xb{g}", [128, T], BF16) for g in range(NG)]
    wb = [sb(f"wb{g}", [128, C], BF16) for g in range(NG)]
    bs = [sb(f"bs{g}", [128, 1], F32) for g in range(NG)]
    qf = [[sb(f"qf{p}{g}", [128, CH], F32) for g in range(NG)]
          for p in range(2)]
    qb = [[sb(f"qb{p}{g}", [128, CH], BF16) for g in range(NG)]
          for p in range(2)]
    tmp = sb("tmp", [128, CH], F32)
    ob = [[sb(f"ob{p}{g}", [128, CH], BF16) for g in range(NG)]
          for p in range(2)]
    pq = [ps(f"pq{i}", [128, CH], F32) for i in range(2)]
    L = LNStats(nc, es, sv, apl_per_chunk=NG)
    L.emit_init(block)

    @block.gpsimd
    def _(g_):
        for g in range(NG):
            sv.dinc(g_.dma_start(xb[g][:], xin[g]), "ld")
        for g in range(NG):
            sv.dinc(g_.dma_start(wb[g][:], w_in[g]), "ld")
        for g in range(NG):
            sv.dinc(g_.dma_start(bs[g][:], b_in[g]), "ld")
    ld_all = sv.val("ld")

    for j in range(NCH):
        p = j % 2
        sl = slice(j * CH, (j + 1) * CH)
        for g2 in range(NG):
            q = (j * NG + g2) % 2
            idx = j * NG + g2

            @block.tensor
            def _(tn, j=j, g2=g2, q=q, idx=idx, sl=sl):
                if idx == 0:
                    sv.waita(tn, "ld", ld_all)
                if idx >= 2:
                    sv.wait(tn, "qev", idx - 1)
                for d in range(NG):
                    mm = tn.matmul(pq[q][:],
                                   wb[d][:, g2 * 128:(g2 + 1) * 128],
                                   xb[d][:, sl], start=(d == 0),
                                   stop=(d == NG - 1))
                sv.inc(mm, "qmm")

            @block.vector
            def _(v, j=j, g2=g2, q=q, idx=idx, p=p):
                sv.wait(v, "qmm", idx + 1)
                if j >= 2:
                    sv.wait(v, "smm", (j - 1) * 2)
                    sv.wait(v, "apl", (j - 1) * NG)
                v.tensor_scalar(qf[p][g2][:], pq[q][:], bs[g2][:], None,
                                op0=ALU.add)
                sv.inc(v.tensor_copy(qb[p][g2][:], qf[p][g2][:]), "qev")
        qev_v = sv.val("qev")

        @block.scalar
        def _(s, qev_v=qev_v):
            sv.waita(s, "qev", qev_v)
            sv.inc(s.nop(), "sqrdy")
        rmm_v = L.emit_chunk(block, j, [qb[p][g][:] for g in range(NG)])

        @block.vector
        def _(v, j=j, p=p, rmm_v=rmm_v):
            sv.waita(v, "rmm", rmm_v)
            if j >= 2:
                sv.wait(v, "st", 16 * NG * (j - 1))
            for g in range(NG):
                v.tensor_tensor(tmp[:], qf[p][g][:], L.rep_m[p][:],
                                op=ALU.subtract)
                sv.inc(v.tensor_tensor(ob[p][g][:], tmp[:], L.rep_r[p][:],
                                       op=ALU.mult), "apl")
        apl_v = sv.val("apl")

        @block.sync
        def _(sy, j=j, p=p, sl=sl, apl_v=apl_v):
            sv.waita(sy, "apl", apl_v)
            for g in range(NG):
                sv.dinc(sy.dma_start(out[g][:, sl], ob[p][g][:]), "st")

    @block.sync
    def _(sy):
        sv.waita(sy, "st", sv.val("st"))


def emit_vproj(nc, S, block, es, xin, w_in, b_in, out):
    sv = SV(S, ["ld", "st", "mm", "ev", "init"])
    sb, ps = _allocs(nc, es)
    xb = [sb(f"xb{g}", [128, T], BF16) for g in range(NG)]
    wb = [sb(f"wb{g}", [128, C], BF16) for g in range(NG)]
    brow_f = sb("browf", [1, C], F32)
    brow = sb("brow", [1, C], BF16)
    ones_c = sb("onesc", [1, 128], BF16)
    pv = [ps(f"pv{i}", [128, CH], F32) for i in range(2)]
    ob = [sb(f"ob{i}", [128, CH], BF16) for i in range(4)]

    @block.gpsimd
    def _(g_):
        for g in range(NG):
            sv.dinc(g_.dma_start(xb[g][:], xin[g]), "ld")
        for g in range(NG):
            sv.dinc(g_.dma_start(wb[g][:], w_in[g]), "ld")
        sv.dinc(g_.dma_start(brow_f[:], b_in), "ld")
    ld_all = sv.val("ld")

    @block.vector
    def _(v):
        sv.waita(v, "ld", ld_all)
        v.memset(ones_c[:], 1.0)
        sv.inc(v.tensor_copy(brow[:], brow_f[:]), "init")

    idx = 0
    for tt in range(32):
        for eh in range(2):
            q = idx % 2
            esl = slice(eh * CH, (eh + 1) * CH)

            @block.tensor
            def _(tn, tt=tt, q=q, idx=idx, esl=esl):
                if idx == 0:
                    sv.wait(tn, "init", 1)
                if idx >= 2:
                    sv.wait(tn, "ev", idx - 1)
                for d in range(NG):
                    tn.matmul(pv[q][:], xb[d][:, tt * 128:(tt + 1) * 128],
                              wb[d][:, esl], start=(d == 0), stop=False)
                sv.inc(tn.matmul(pv[q][:], ones_c[:], brow[0:1, esl],
                                 start=False, stop=True), "mm")

            @block.vector
            def _(v, q=q, idx=idx):
                sv.wait(v, "mm", idx + 1)
                if idx >= 4:
                    sv.wait(v, "st", 16 * (idx - 3))
                sv.inc(v.tensor_copy(ob[idx % 4][:], pv[q][:]), "ev")

            @block.sync
            def _(sy, tt=tt, idx=idx, esl=esl):
                sv.wait(sy, "ev", idx + 1)
                sv.dinc(sy.dma_start(out[tt][:, esl], ob[idx % 4][:]), "st")
            idx += 1

    @block.sync
    def _(sy):
        sv.waita(sy, "st", sv.val("st"))


def emit_l0attn(nc, S, block, es, qn_in, kn_in, v_in, wo_in, bo_in, ls_in,
                out):
    CHA = 256
    NCA = T // CHA
    sv = SV(S, ["ld", "qld", "st", "init", "ptmm", "sgev", "temm", "oev",
                "pjmm", "pjev"])
    sb, ps = _allocs(nc, es)
    kn = [sb(f"kn{g}", [128, T], BF16) for g in range(NG)]
    vb = [sb(f"vb{n}", [128, C], BF16) for n in range(32)]
    wo = [sb(f"wo{g}", [128, C], BF16) for g in range(NG)]
    bo = [sb(f"bo{g}", [128, 1], F32) for g in range(NG)]
    ls = [sb(f"ls{g}", [128, 1], F32) for g in range(NG)]
    qc = [[sb(f"qc{p}{g}", [128, CHA], BF16) for g in range(NG)]
          for p in range(2)]
    pb = [sb(f"pbuf{p}", [128, CHA], BF16) for p in range(2)]
    obf = [sb(f"obf{g}", [128, CHA], BF16) for g in range(NG)]
    ob = [sb(f"ob{i}", [128, CHA], BF16) for i in range(4)]
    logs_c = sb("logsc", [128, 1], F32)
    scl_c = sb("sclc", [128, 1], F32)
    zrow = sb("zrow", [1, 2 * CHA], BF16)
    onec = sb("onec", [1, 128], BF16)
    psPt = [ps(f"psPt{i}", [128, CHA], F32) for i in range(2)]
    psO = [ps(f"psO{i}", [128, 2 * CHA], F32) for i in range(4)]
    psPj = [ps(f"psPj{i}", [128, CHA], F32) for i in range(2)]

    @block.gpsimd
    def _(g_):
        for g in range(NG):
            sv.dinc(g_.dma_start(kn[g][:], kn_in[g]), "ld")
        for n in range(32):
            sv.dinc(g_.dma_start(vb[n][:], v_in[n]), "ld")
        for g in range(NG):
            sv.dinc(g_.dma_start(wo[g][:], wo_in[g]), "ld")
            sv.dinc(g_.dma_start(bo[g][:], bo_in[g]), "ld")
            sv.dinc(g_.dma_start(ls[g][:], ls_in[g]), "ld")
    ld_all = sv.val("ld")

    @block.vector
    def _(v):
        v.memset(scl_c[:], 1.0 / 32.0)
        v.memset(zrow[:], 0.0)
        v.memset(onec[:], 1.0)
        sv.inc(v.memset(logs_c[:], -float(np.log(np.float32(T)))), "init")

    for sc in range(NCA):
        ssl = slice(sc * CHA, (sc + 1) * CHA)

        @block.sync
        def _(sy, sc=sc, ssl=ssl):
            if sc >= 2:
                sv.wait(sy, "ptmm", (sc - 1) * 32)
            for g in range(NG):
                sv.dinc(sy.dma_start(qc[sc % 2][g][:], qn_in[g][:, ssl]),
                        "qld")
        qld_v = sv.val("qld")

        for tt in range(33):
            hp = tt % 2
            it = sc * 32 + tt

            @block.tensor
            def _(tn, sc=sc, tt=tt, hp=hp, it=it, qld_v=qld_v):
                if tt < 32:
                    if sc == 0 and tt == 0:
                        sv.waita(tn, "ld", ld_all)
                    sv.waita(tn, "qld", qld_v)
                    if it >= 2:
                        sv.wait(tn, "sgev", it - 1)
                    if tt == 0 and sc == 0:
                        sv.wait(tn, "init", 1)
                    for g in range(NG):
                        mm = tn.matmul(psPt[hp][:],
                                       kn[g][:, tt * 128:(tt + 1) * 128],
                                       qc[sc % 2][g][:], start=(g == 0),
                                       stop=(g == NG - 1))
                    sv.inc(mm, "ptmm")
                if tt >= 1:
                    to = tt - 1
                    if to == 0:
                        if sc >= 1:
                            sv.wait(tn, "oev", sc * NG)
                        for i4 in range(4):
                            tn.matmul(psO[i4][:], onec[:], zrow[:],
                                      start=True, stop=False)
                    sv.wait(tn, "sgev", sc * 32 + to + 1)
                    for g in range(NG):
                        mm = tn.matmul(
                            psO[g // 2][:, (g % 2) * CHA:(g % 2 + 1) * CHA],
                            vb[to][:, g * 128:(g + 1) * 128], pb[to % 2][:],
                            start=False, stop=(to == 31))
                    sv.inc(mm, "temm")

        @block.scalar
        def _(s, sc=sc):
            for tt in range(32):
                it = sc * 32 + tt
                hp = tt % 2
                sv.wait(s, "ptmm", it + 1)
                if it >= 2:
                    sv.wait(s, "temm", it - 1)
                sv.inc(s.activation(pb[hp][:], psPt[hp][:], AF.Sigmoid,
                                    bias=logs_c[:], scale=scl_c[:]),
                       "sgev")

        @block.vector
        def _(v, sc=sc):
            sv.wait(v, "temm", (sc + 1) * 32)
            if sc >= 1:
                sv.wait(v, "pjmm", sc * NG)
            for g in range(NG):
                sv.inc(v.tensor_copy(
                    obf[g][:],
                    psO[g // 2][:, (g % 2) * CHA:(g % 2 + 1) * CHA]), "oev")
        oev_v = sv.val("oev")

        for g2 in range(NG):
            qj = g2 % 2
            ij = sc * NG + g2

            @block.tensor
            def _(tn, g2=g2, qj=qj, ij=ij, oev_v=oev_v):
                sv.waita(tn, "oev", oev_v)
                if ij >= 2:
                    sv.wait(tn, "pjev", ij - 1)
                for e in range(NG):
                    mm = tn.matmul(psPj[qj][:],
                                   wo[e][:, g2 * 128:(g2 + 1) * 128],
                                   obf[e][:], start=(e == 0),
                                   stop=(e == NG - 1))
                sv.inc(mm, "pjmm")

            @block.vector
            def _(v, g2=g2, qj=qj, ij=ij):
                sv.wait(v, "pjmm", ij + 1)
                if ij >= 4:
                    sv.wait(v, "st", 16 * (ij - 3))
                sv.inc(v.tensor_scalar(ob[ij % 4][:], psPj[qj][:], bo[g2][:],
                                       ls[g2][:], op0=ALU.add, op1=ALU.mult),
                       "pjev")

            @block.sync
            def _(sy, g2=g2, ij=ij, ssl=ssl):
                sv.wait(sy, "pjev", ij + 1)
                sv.dinc(sy.dma_start(out[g2][:, ssl], ob[ij % 4][:]), "st")

    @block.sync
    def _(sy):
        sv.waita(sy, "st", sv.val("st"))


def emit_l0ffn(nc, S, block, es, xin, k_in, sin_in, sk_in, b_in, out):
    sv = SV(S, ["ld", "st", "mm", "ev", "init", "quant"])
    sb, ps = _allocs(nc, es)
    xb = [sb(f"xb{g}", [128, T], BF16) for g in range(NG)]
    kb = [sb(f"kb{g}", [128, C], BF16) for g in range(NG)]
    xq = [sb(f"xq{g}", [128, T], FP8) for g in range(NG)]
    kq = [sb(f"kq{g}", [128, C], FP8) for g in range(NG)]
    bs = [sb(f"bs{g}", [128, 1], F32) for g in range(NG)]
    s_in = sb("s_in", [128, 1], F32)
    s_k = sb("s_k", [128, 1], F32)
    r_in = sb("r_in", [128, 1], F32)
    r_k = sb("r_k", [128, 1], F32)
    prod = sb("prod", [128, 1], F32)
    yb = sb("ybt", [128, CH], BF16)
    ob = [sb(f"ob{i}", [128, CH], BF16) for i in range(4)]
    pq = [ps(f"pq{i}", [128, CH], F32) for i in range(2)]

    @block.gpsimd
    def _(g_):
        for g in range(NG):
            sv.dinc(g_.dma_start(xb[g][:], xin[g]), "ld")
        for g in range(NG):
            sv.dinc(g_.dma_start(kb[g][:], k_in[g]), "ld")
            sv.dinc(g_.dma_start(bs[g][:], b_in[g]), "ld")
        sv.dinc(g_.dma_start(s_in[:], sin_in), "ld")
        sv.dinc(g_.dma_start(s_k[:], sk_in), "ld")
    ld_all = sv.val("ld")

    @block.vector
    def _(v):
        sv.waita(v, "ld", ld_all)
        v.reciprocal(r_in[:], s_in[:])
        v.reciprocal(r_k[:], s_k[:])
        sv.inc(v.tensor_tensor(prod[:], s_in[:], s_k[:], op=ALU.mult),
               "init")

    @block.scalar
    def _(s):
        sv.wait(s, "init", 1)
        for g in range(NG):
            s.activation(xq[g][:], xb[g][:], AF.Copy, scale=r_in[:])
        for g in range(NG):
            mm = s.activation(kq[g][:], kb[g][:], AF.Copy, scale=r_k[:])
        sv.inc(mm, "quant")
    quant_v = sv.val("quant")

    idx = 0
    for g2 in range(NG):
        for j in range(NCH):
            q = idx % 2
            sl = slice(j * CH, (j + 1) * CH)

            @block.tensor
            def _(tn, g2=g2, q=q, sl=sl, idx=idx):
                if idx == 0:
                    sv.waita(tn, "quant", quant_v)
                if idx >= 2:
                    sv.wait(tn, "ev", idx - 1)
                for d in range(NG):
                    mm = tn.matmul(pq[q][:],
                                   kq[d][:, g2 * 128:(g2 + 1) * 128],
                                   xq[d][:, sl], start=(d == 0),
                                   stop=(d == NG - 1))
                sv.inc(mm, "mm")

            @block.vector
            def _(v, g2=g2, q=q, idx=idx):
                sv.wait(v, "mm", idx + 1)
                if idx >= 4:
                    sv.wait(v, "st", 16 * (idx - 3))
                v.tensor_scalar(yb[:], pq[q][:], prod[:], None, op0=ALU.mult)
                sv.inc(v.tensor_scalar(ob[idx % 4][:], yb[:], bs[g2][:],
                                       None, op0=ALU.add), "ev")

            @block.sync
            def _(sy, g2=g2, sl=sl, idx=idx):
                sv.wait(sy, "ev", idx + 1)
                sv.dinc(sy.dma_start(out[g2][:, sl], ob[idx % 4][:]), "st")
            idx += 1

    @block.sync
    def _(sy):
        sv.waita(sy, "st", sv.val("st"))


def emit_wkv(nc, S, block, es, k_in, v_in, sr_in, wd_in, u_in, out):
    sv = SV(S, ["ldw", "ld", "st", "init", "sm1", "vm1", "sm2", "sm3",
                "vdone", "den", "sden", "zdone"])
    sb, ps = _allocs(nc, es)
    kb = [sb(f"kb{p}", [128, T], F32) for p in range(2)]
    vb = [sb(f"vb{p}", [128, T], F32) for p in range(2)]
    srb = [sb(f"srb{p}", [128, T], BF16) for p in range(2)]
    wd = [sb(f"wd{g}", [128, 1], F32) for g in range(NG)]
    uu = [sb(f"uu{g}", [128, 1], F32) for g in range(NG)]
    ew = sb("ew", [128, 1], F32)
    abuf = sb("abuf", [128, T], F32)
    ebuf = sb("ebuf", [128, T], F32)
    pbuf = sb("pbuf", [128, T], F32)
    zb = [sb(f"zb{p}", [128, T], BF16) for p in range(2)]
    nm1 = sb("nm1", [128, 1], F32)

    @block.vector
    def _(v):
        sv.inc(v.memset(nm1[:], -1.0), "init")

    @block.sync
    def _(sy):
        for g in range(NG):
            sv.dinc(sy.dma_start(wd[g][:], wd_in[g]), "ldw")
            sv.dinc(sy.dma_start(uu[g][:], u_in[g]), "ldw")
    ldw_v = sv.val("ldw")

    for g in range(NG):
        p = g % 2

        @block.sync
        def _(sy, g=g, p=p):
            if g >= 2:
                sv.wait(sy, "zdone", g - 1)
            sv.dinc(sy.dma_start(kb[p][:], k_in[g]), "ld")
            sv.dinc(sy.dma_start(vb[p][:], v_in[g]), "ld")
            sv.dinc(sy.dma_start(srb[p][:], sr_in[g]), "ld")
        ld_v = sv.val("ld")

        @block.scalar
        def _(s, g=g, ld_v=ld_v):
            if g == 0:
                sv.waita(s, "ldw", ldw_v)
                sv.wait(s, "init", 1)
            if g >= 1:
                sv.wait(s, "vdone", g)
            sv.inc(s.activation(ew[:], wd[g][:], AF.Exp), "sm1")

        @block.vector
        def _(v, g=g):
            sv.wait(v, "sm1", g + 1)
            sv.inc(v.tensor_tensor(ew[:], ew[:], nm1[:], op=ALU.mult), "vm1")

        @block.scalar
        def _(s, g=g, p=p, ld_v=ld_v):
            sv.wait(s, "vm1", g + 1)
            sv.inc(s.activation(ew[:], ew[:], AF.Exp), "sm2")
            sv.waita(s, "ld", ld_v)
            if g >= 2:
                sv.wait(s, "zdone", g - 1)
            s.activation(abuf[:], kb[p][:], AF.Exp)
            sv.inc(s.activation(kb[p][:], kb[p][:], AF.Exp, bias=uu[g][:]),
                   "sm3")

        @block.vector
        def _(v, g=g, p=p):
            sv.wait(v, "sm2", g + 1)
            coefb = ew[:].broadcast_to((128, T))
            sv.wait(v, "sm3", g + 1)
            v.tensor_tensor(ebuf[:], abuf[:], vb[p][:], op=ALU.mult)
            v.tensor_tensor_scan(pbuf[:], coefb, ebuf[:], 0.0,
                                 op0=ALU.mult, op1=ALU.add)
            sv.inc(v.tensor_tensor_scan(ebuf[:], coefb, abuf[:], 0.0,
                                        op0=ALU.mult, op1=ALU.add), "vdone")
            v.tensor_tensor(abuf[:], kb[p][:], vb[p][:], op=ALU.mult)
            v.tensor_tensor(abuf[:, 1:T], abuf[:, 1:T], pbuf[:, 0:T - 1],
                            op=ALU.add)
            sv.inc(v.tensor_tensor(kb[p][:, 1:T], kb[p][:, 1:T],
                                   ebuf[:, 0:T - 1], op=ALU.add), "den")

        @block.scalar
        def _(s, g=g, p=p):
            sv.wait(s, "den", g + 1)
            s.activation(kb[p][:], kb[p][:], AF.Ln)
            sv.inc(s.activation(kb[p][:], kb[p][:], AF.Exp, scale=nm1[:]),
                   "sden")

        @block.vector
        def _(v, g=g, p=p):
            sv.wait(v, "sden", g + 1)
            if g >= 2:
                sv.wait(v, "st", 16 * (g - 1))
            v.tensor_tensor(abuf[:], abuf[:], kb[p][:], op=ALU.mult)
            sv.inc(v.tensor_tensor(zb[p][:], abuf[:], srb[p][:],
                                   op=ALU.mult), "zdone")

        @block.sync
        def _(sy, g=g, p=p):
            sv.wait(sy, "zdone", g + 1)
            sv.dinc(sy.dma_start(out[g], zb[p][:]), "st")

    @block.sync
    def _(sy):
        sv.waita(sy, "st", sv.val("st"))


def emit_lif(nc, S, block, es, a_in, b_in, out):
    sv = SV(S, ["lda", "ldb", "st", "init", "rel", "spk", "scan"])
    NL = 32
    NCK = T // NL
    LAN = NG * NCK
    sb, ps = _allocs(nc, es)
    B = sb("B", [128, NG * T], F32)
    A = [sb(f"A{p}", [128, T], F32) for p in range(2)]
    wst = [sb(f"wst{p}", [128, LAN], F32) for p in range(2)]
    m = sb("m", [128, LAN], F32)
    mask = sb("mask", [128, LAN], F32)
    bs = [sb(f"bs{p}", [128, CH], F32) for p in range(2)]
    ob = [sb(f"obl{p}", [128, CH], F32) for p in range(2)]

    @block.vector
    def _(v):
        v.memset(mask[:], 1.0)
        for g in range(NG):
            v.memset(mask[:, g * NCK:g * NCK + 1], 0.0)
        sv.inc(v.memset(wst[0][:], 0.0), "init")

    for g in range(NG):
        pa = g % 2

        @block.sync
        def _(sy, g=g, pa=pa):
            if g >= 2:
                sv.wait(sy, "rel", g - 1)
            sv.dinc(sy.dma_start(A[pa][:], a_in[g]), "lda")
        lda_v = sv.val("lda")

        @block.vector
        def _(v, g=g, pa=pa, lda_v=lda_v):
            sv.waita(v, "lda", lda_v)
            dst = B[:, g * T:(g + 1) * T].rearrange("p (i ch) -> p i ch",
                                                    i=NL)
            srcv = A[pa][:].rearrange("p (ch i) -> p i ch", ch=NCK)
            sv.inc(v.tensor_copy(dst, srcv), "rel")
    rel_v = sv.val("rel")
    Bv = B[:].rearrange("p (g i ch) -> p g i ch", g=NG, i=NL)

    @block.vector
    def _(v, rel_v=rel_v):
        sv.waita(v, "rel", rel_v)
        sv.wait(v, "init", 1)
        wprev = 0
        for j in range(1, NL):
            xw = B[:].rearrange("p (g r) -> p g r", g=NG)[
                :, :, j * NCK - 1:j * NCK - 1 + NCK]
            mm_ = m[:].rearrange("p (g ch) -> p g ch", g=NG)
            wp = wst[wprev][:].rearrange("p (g ch) -> p g ch", g=NG)
            wc = wst[1 - wprev][:].rearrange("p (g ch) -> p g ch", g=NG)
            v.scalar_tensor_tensor(mm_, wp, 2.0, wp, op0=ALU.is_lt,
                                   op1=ALU.mult)
            v.scalar_tensor_tensor(wc, mm_, 0.5, xw, op0=ALU.mult,
                                   op1=ALU.add)
            wprev = 1 - wprev
        v.tensor_tensor(wst[wprev][:], wst[wprev][:], mask[:], op=ALU.mult)
        for i in range(NL):
            prev = (wst[wprev][:].rearrange("p (g ch) -> p g ch", g=NG)
                    if i == 0 else Bv[:, :, i - 1, :])
            mm_ = m[:].rearrange("p (g ch) -> p g ch", g=NG)
            v.scalar_tensor_tensor(mm_, prev, 2.0, prev, op0=ALU.is_lt,
                                   op1=ALU.mult)
            v.scalar_tensor_tensor(Bv[:, :, i, :], mm_, 0.5, Bv[:, :, i, :],
                                   op0=ALU.mult, op1=ALU.add)
        sv.inc(v.nop(), "scan")

    for g in range(NG):
        for j in range(NCH):
            pa = (g * NCH + j) % 2
            sl = slice(j * CH, (j + 1) * CH)
            it = g * NCH + j

            @block.gpsimd
            def _(gp, g=g, pa=pa, sl=sl, it=it):
                if it >= 2:
                    sv.wait(gp, "spk", it - 1)
                sv.dinc(gp.dma_start(bs[pa][:], b_in[g][:, sl]), "ldb")
            ldb_v = sv.val("ldb")

            @block.vector
            def _(v, g=g, j=j, pa=pa, ldb_v=ldb_v, it=it):
                sv.waita(v, "ldb", ldb_v)
                if it >= 2:
                    sv.wait(v, "st", 16 * (it - 1))
                pv = B[:, g * T:(g + 1) * T].rearrange(
                    "p (i ch) -> p ch i", i=NL)[:, j * 16:(j + 1) * 16, :]
                bb = bs[pa][:].rearrange("p (ch i) -> p ch i", ch=16)
                oo = ob[pa][:].rearrange("p (ch i) -> p ch i", ch=16)
                sv.inc(v.scalar_tensor_tensor(oo, pv, 2.0, bb,
                                              op0=ALU.is_ge, op1=ALU.add),
                       "spk")

            @block.sync
            def _(sy, g=g, pa=pa, sl=sl, it=it):
                sv.wait(sy, "spk", it + 1)
                sv.dinc(sy.dma_start(out[g][:, sl], ob[pa][:]), "st")

    @block.sync
    def _(sy):
        sv.waita(sy, "st", sv.val("st"))


def emit_cmix(nc, S, block, es, xk_in, sr_in, wk_in, wv_in, out):
    H4 = 4096
    CHA = 256
    NCA = T // CHA
    sv = SV(S, ["ld", "ldx", "st", "zinit", "kmm", "kev", "vmm", "cev"])
    sb, ps = _allocs(nc, es)
    wk = [sb(f"wk{g}", [128, H4], BF16) for g in range(NG)]
    wv = [sb(f"wv{h}", [128, C], BF16) for h in range(32)]
    xk = [[sb(f"xk{p}{g}", [128, CHA], BF16) for g in range(NG)]
          for p in range(2)]
    sr = [[sb(f"sr{p}{g}", [128, CHA], BF16) for g in range(NG)]
          for p in range(2)]
    kk = [sb(f"kk{p}", [128, CHA], BF16) for p in range(2)]
    kt = sb("ktmp", [128, CHA], F32)
    ob = [sb(f"ob{i}", [128, CHA], F32) for i in range(4)]
    zrow = sb("zrow", [1, 2 * CHA], BF16)
    onec = sb("onec", [1, 128], BF16)
    psK = [ps(f"psK{i}", [128, CHA], F32) for i in range(2)]
    psC = [ps(f"psC{i}", [128, 2 * CHA], F32) for i in range(4)]

    @block.vector
    def _(v):
        v.memset(zrow[:], 0.0)
        sv.inc(v.memset(onec[:], 1.0), "zinit")

    @block.gpsimd
    def _(g_):
        for g in range(NG):
            sv.dinc(g_.dma_start(wk[g][:], wk_in[g]), "ld")
        for h in range(32):
            sv.dinc(g_.dma_start(wv[h][:], wv_in[h]), "ld")
    ld_all = sv.val("ld")

    for sc in range(NCA):
        p = sc % 2
        ssl = slice(sc * CHA, (sc + 1) * CHA)

        @block.gpsimd
        def _(gp, sc=sc, p=p, ssl=ssl):
            if sc >= 2:
                sv.wait(gp, "kmm", (sc - 1) * 32)
                sv.wait(gp, "cev", (sc - 1) * NG)
            for g in range(NG):
                sv.dinc(gp.dma_start(xk[p][g][:], xk_in[g][:, ssl]), "ldx")
                sv.dinc(gp.dma_start(sr[p][g][:], sr_in[g][:, ssl]), "ldx")
        ldx_v = sv.val("ldx")

        for h in range(33):
            hp = h % 2
            it = sc * 32 + h

            @block.tensor
            def _(tn, sc=sc, h=h, hp=hp, it=it, p=p, ldx_v=ldx_v):
                if h < 32:
                    if it == 0:
                        sv.waita(tn, "ld", ld_all)
                        sv.wait(tn, "zinit", 1)
                    sv.waita(tn, "ldx", ldx_v)
                    if it >= 2:
                        sv.wait(tn, "kev", it - 1)

                    for d in range(NG):
                        mm = tn.matmul(psK[hp][:],
                                       wk[d][:, h * 128:(h + 1) * 128],
                                       xk[p][d][:], start=(d == 0),
                                       stop=(d == NG - 1))
                    sv.inc(mm, "kmm")
                if h >= 1:
                    ho = h - 1
                    if ho == 0:
                        if sc >= 1:
                            sv.wait(tn, "cev", sc * NG)
                        for i4 in range(4):
                            tn.matmul(psC[i4][:], onec[:], zrow[:],
                                      start=True, stop=False)
                    sv.wait(tn, "kev", sc * 32 + ho + 1)
                    for g2 in range(NG):
                        mm = tn.matmul(
                            psC[g2 // 2][:, (g2 % 2) * CHA:(g2 % 2 + 1) * CHA],
                            wv[ho][:, g2 * 128:(g2 + 1) * 128],
                            kk[ho % 2][:], start=False, stop=(ho == 31))
                    sv.inc(mm, "vmm")

            if h < 32:
                @block.vector
                def _(v, h=h, hp=hp, it=it):
                    sv.wait(v, "kmm", it + 1)
                    if it >= 2:
                        sv.wait(v, "vmm", it - 1)
                    v.tensor_scalar(kt[:], psK[hp][:], 0.0, None,
                                    op0=ALU.max)
                    sv.inc(v.tensor_tensor(kk[hp][:], kt[:], psK[hp][:],
                                           op=ALU.mult), "kev")

        for g2 in range(NG):
            ic = sc * NG + g2

            @block.vector
            def _(v, sc=sc, g2=g2, p=p, ic=ic):
                sv.wait(v, "vmm", (sc + 1) * 32)
                if ic >= 4:
                    sv.wait(v, "st", 16 * (ic - 3))
                sv.inc(v.tensor_tensor(
                    ob[ic % 4][:],
                    psC[g2 // 2][:, (g2 % 2) * CHA:(g2 % 2 + 1) * CHA],
                    sr[p][g2][:], op=ALU.mult), "cev")

            @block.sync
            def _(sy, g2=g2, ssl=ssl, ic=ic):
                sv.wait(sy, "cev", ic + 1)
                sv.dinc(sy.dma_start(out[g2][:, ssl], ob[ic % 4][:]), "st")

    @block.sync
    def _(sy):
        sv.waita(sy, "st", sv.val("st"))


def emit_router(nc, S, block, es, h_in, emb_in, id_in, wm_out, wmt_out):
    E = 16
    sv = SV(S, ["ld", "stw", "init", "lmm", "wmd", "tmm", "tev"])
    sb, ps = _allocs(nc, es)
    hb = [sb(f"hb{g}", [128, T], F32) for g in range(NG)]
    embf = sb("embf", [128, E], F32)
    ident = sb("identsb", [128, 128], F32)
    wms = [sb(f"wms{p}", [128, 512], F32) for p in range(2)]
    wmts = [sb(f"wmts{p}", [16, T], F32) for p in range(2)]
    el = sb("el", [128, 512], F32)
    w2b = sb("w2b", [128, 512], F32)
    mx = sb("mx", [128, 32], F32)
    psL = [ps(f"psL{p}", [128, 512], F32) for p in range(2)]
    psT = [ps(f"psT{p}", [128, 512], F32) for p in range(2)]

    @block.gpsimd
    def _(g_):
        for g in range(NG):
            sv.dinc(g_.dma_start(hb[g][:], h_in[g]), "ld")
        sv.dinc(g_.dma_start(embf[:], emb_in), "ld")
        sv.dinc(g_.dma_start(ident[:], id_in), "ld")
    ld_all = sv.val("ld")

    @block.vector
    def _(v):
        sv.waita(v, "ld", ld_all)
        sv.inc(v.nop(), "init")

    for g in range(NG):
        pg = g % 2

        @block.tensor
        def _(tn, g=g, pg=pg):
            if g == 0:
                sv.wait(tn, "init", 1)
            if g >= 2:
                sv.wait(tn, "wmd", 3 * (g - 2) + 1)
            for st in range(32):
                mm = tn.matmul(psL[pg][:, st * 16:(st + 1) * 16],
                               hb[g][:, st * 128:(st + 1) * 128],
                               embf[:], start=True, stop=True)
            sv.inc(mm, "lmm")

        @block.vector
        def _(v, g=g, pg=pg):
            sv.wait(v, "lmm", g + 1)
            if g >= 2:
                sv.wait(v, "stw", 16 * 2 * (g - 1))
            bc = lambda a: a.unsqueeze(-1).broadcast_to((128, 32, E))
            Lv = psL[pg][:].rearrange("p (s e) -> p s e", s=32)
            elv = el[:].rearrange("p (s e) -> p s e", s=32)
            wv_ = wms[pg][:].rearrange("p (s e) -> p s e", s=32)
            w2v = w2b[:].rearrange("p (s e) -> p s e", s=32)
            sv.fence(v, v.tensor_reduce(mx[:], Lv, axis=AXX, op=ALU.max))
            sv.inc(v.tensor_tensor(elv, Lv, bc(mx[:]), op=ALU.subtract),
                   "wmd")

        @block.scalar
        def _(s, g=g):
            sv.wait(s, "wmd", 3 * g + 1)
            sv.inc(s.activation(el[:], el[:], AF.Exp), "wmd")

        @block.vector
        def _(v, g=g, pg=pg):
            sv.wait(v, "wmd", 3 * g + 2)
            bc = lambda a: a.unsqueeze(-1).broadcast_to((128, 32, E))
            elv = el[:].rearrange("p (s e) -> p s e", s=32)
            wv_ = wms[pg][:].rearrange("p (s e) -> p s e", s=32)
            w2v = w2b[:].rearrange("p (s e) -> p s e", s=32)
            sv.fence(v, v.tensor_reduce(mx[:], elv, axis=AXX, op=ALU.add))
            sv.fence(v, v.reciprocal(mx[:], mx[:]))
            sv.fence(v, v.tensor_tensor(wv_, elv, bc(mx[:]), op=ALU.mult))
            sv.fence(v, v.tensor_reduce(mx[:], wv_, axis=AXX, op=ALU.max))
            sv.fence(v, v.tensor_tensor(w2v, wv_, bc(mx[:]), op=ALU.is_ge))
            sv.fence(v, v.scalar_tensor_tensor(w2b[:], w2b[:], -1e30,
                                               wms[pg][:], op0=ALU.mult,
                                               op1=ALU.add))
            sv.fence(v, v.tensor_reduce(mx[:], w2v, axis=AXX, op=ALU.max))
            sv.fence(v, v.tensor_tensor(w2v, wv_, bc(mx[:]), op=ALU.is_ge))
            sv.inc(v.tensor_tensor(wms[pg][:], wms[pg][:], w2b[:],
                                   op=ALU.mult), "wmd")

        for st in range(32):
            it = g * 32 + st

            @block.tensor
            def _(tn, g=g, st=st, it=it, pg=pg):
                sv.wait(tn, "wmd", 3 * (g + 1))
                if it >= 2:
                    sv.wait(tn, "tev", it - 1)
                sv.inc(tn.transpose(psT[st % 2][0:16, 0:128],
                                    wms[pg][:, st * 16:st * 16 + 16],
                                    ident[:]), "tmm")

            @block.vector
            def _(v, st=st, it=it, pg=pg):
                sv.wait(v, "tmm", it + 1)
                sv.inc(v.tensor_copy(wmts[pg][:, st * 128:(st + 1) * 128],
                                     psT[st % 2][0:16, 0:128]), "tev")

        @block.sync
        def _(sy, g=g, pg=pg):
            sv.wait(sy, "tev", (g + 1) * 32)
            sv.dinc(sy.dma_start(
                wm_out[g].rearrange("s p e -> p s e"),
                wms[pg][:].rearrange("p (s e) -> p s e", s=32)), "stw")
            sv.dinc(sy.dma_start(wmt_out[g], wmts[pg][:]), "stw")

    @block.sync
    def _(sy):
        sv.waita(sy, "stw", sv.val("stw"))


def emit_experts(nc, S, block, es, h_in, w1_in, b1_in, w2_in, b2_in, wm_in,
                 wmt_in, id_in, out):
    E, II = 16, 512
    SCE = 512
    NSC = T // SCE
    sv = SV(S, ["ld", "ldw", "stc", "init", "cdone", "bmm", "bev", "ctev",
                "hmm", "gev", "emm", "eev", "c16", "ctmm"])
    sb, ps = _allocs(nc, es)
    hb = [sb(f"hb{g}", [128, T], BF16) for g in range(NG)]
    w1b = [sb(f"w1b{e}", [128, II], BF16) for e in range(E)]
    w2b = [[sb(f"w2b{e}{i}", [128, 128], BF16) for i in range(4)]
           for e in range(E)]
    b1s = [[sb(f"b1s{e}{i}", [128, 1], F32) for i in range(4)]
           for e in range(E)]
    b2s = sb("b2s", [16, 128], F32)
    wmg = [sb(f"wmg{p}", [128, 512], F32) for p in range(2)]
    wmtg = [sb(f"wmtg{p}", [16, T], F32) for p in range(2)]
    identf = sb("identfsb", [128, 128], F32)
    identb = sb("identb", [128, 128], BF16)
    hm = [[sb(f"hm{p}{i}", [128, SCE], BF16) for i in range(4)]
          for p in range(2)]
    comb = sb("comb", [128, T], F32)
    cmb16 = [sb(f"cmb16{p}", [128, 128], BF16) for p in range(2)]
    ctb = [sb(f"ctb{p}", [128, T], BF16) for p in range(2)]
    psH = [ps(f"psH{i}", [128, 512], F32) for i in range(4)]
    psE = [ps(f"psE{i}", [128, 512], F32) for i in range(2)]
    psB = ps("psB", [128, 512], F32)
    psT = ps("psTe", [128, 512], BF16)

    @block.gpsimd
    def _(g_):
        for g in range(NG):
            sv.dinc(g_.dma_start(hb[g][:], h_in[g]), "ld")
        for e in range(E):
            sv.dinc(g_.dma_start(w1b[e][:], w1_in[e]), "ld")
            for i in range(4):
                sv.dinc(g_.dma_start(w2b[e][i][:], w2_in[e, i]), "ld")
                sv.dinc(g_.dma_start(b1s[e][i][:], b1_in[e, i]), "ld")
        sv.dinc(g_.dma_start(b2s[:], b2_in), "ld")
        sv.dinc(g_.dma_start(identf[:], id_in), "ld")
    ld_all = sv.val("ld")

    @block.vector
    def _(v):
        sv.waita(v, "ld", ld_all)
        sv.inc(v.tensor_copy(identb[:], identf[:]), "init")

    def psh(j, i):
        return psH[i][:]

    def te_mm2(tn, j, g, e, sc):
        jp = j % 2
        pe = j % 2
        if j >= 2:
            sv.wait(tn, "eev", (j - 1) * 4)
        for st2 in range(4):
            for i in range(4):
                mm = tn.matmul(psE[pe][:, st2 * 128:(st2 + 1) * 128],
                               hm[jp][i][:, st2 * 128:(st2 + 1) * 128],
                               w2b[e][i][:], start=(i == 0), stop=(i == 3))
        sv.inc(mm, "emm")

    def ve_evac(j, g, e, sc):
        @block.vector
        def _(v, g=g, e=e, sc=sc, j=j):
            pg = g % 2
            pe = j % 2
            sv.wait(v, "emm", j + 1)
            for st2 in range(4):
                st = sc * 4 + st2
                csl = comb[:, st * 128:(st + 1) * 128]
                sv.inc(v.scalar_tensor_tensor(
                    csl, psE[pe][:, st2 * 128:(st2 + 1) * 128],
                    wmg[pg][:, st * 16 + e:st * 16 + e + 1], csl,
                    op0=ALU.mult, op1=ALU.add), "eev")

    j = 0
    for g in range(NG):
        pg = g % 2

        @block.sync
        def _(sy, g=g, pg=pg):
            if g >= 2:
                sv.wait(sy, "cdone", g - 1)
            sv.dinc(sy.dma_start(
                wmg[pg][:].rearrange("p (s e) -> p s e", s=32),
                wm_in[g].rearrange("s p e -> p s e")), "ldw")
            sv.dinc(sy.dma_start(wmtg[pg][:], wmt_in[g]), "ldw")
        ldw_v = sv.val("ldw")

        for st in range(32):
            ib = g * 32 + st

            @block.tensor
            def _(tn, g=g, st=st, ib=ib, pg=pg, ldw_v=ldw_v):
                if ib == 0:
                    sv.waita(tn, "ld", ld_all)
                sv.waita(tn, "ldw", ldw_v)
                if ib >= 1:
                    sv.wait(tn, "bev", ib)
                sv.inc(tn.matmul(psB[:, (ib % 4) * 128:(ib % 4 + 1) * 128],
                                 wmtg[pg][:, st * 128:(st + 1) * 128],
                                 b2s[:], start=True, stop=True), "bmm")

            @block.vector
            def _(v, g=g, st=st, ib=ib):
                sv.wait(v, "bmm", ib + 1)
                if st == 0 and g >= 1:
                    sv.wait(v, "ctev", g * 32)
                sv.inc(v.tensor_copy(
                    comb[:, st * 128:(st + 1) * 128],
                    psB[:, (ib % 4) * 128:(ib % 4 + 1) * 128]), "bev")
        bev_v = sv.val("bev")

        first_j = j
        pend = []

        def flush_scalar():
            if not pend:
                return
            batch = list(pend)
            pend.clear()

            @block.scalar
            def _(s, batch=batch):
                for (jj, ee) in batch:
                    jp2 = jj % 2
                    if jj >= 2:
                        sv.wait(s, "emm", jj - 1)
                    for i in range(4):
                        sv.wait(s, "hmm", jj * 4 + i + 1)
                        sv.inc(s.activation(hm[jp2][i][:], psh(jj, i),
                                            AF.Gelu, bias=b1s[ee][i][:]),
                               "gev")

        for e in range(E):
            for sc in range(NSC):
                @block.tensor
                def _(tn, g=g, e=e, sc=sc, j=j, first_j=first_j,
                      bev_v=bev_v):
                    if e == 0 and sc == 0:
                        sv.waita(tn, "bev", bev_v)
                    for i in range(4):
                        if j >= 1:
                            sv.wait(tn, "gev", (j - 1) * 4 + i + 1)
                        mm = tn.matmul(psh(j, i),
                                       w1b[e][:, i * 128:(i + 1) * 128],
                                       hb[g][:, sc * SCE:(sc + 1) * SCE],
                                       start=True, stop=True)
                        sv.inc(mm, "hmm")
                    if j > first_j:
                        sv.wait(tn, "gev", j * 4)
                        ge, gsc = divmod(j - 1 - first_j, NSC)
                        te_mm2(tn, j - 1, g, ge, gsc)

                pend.append((j, e))
                if len(pend) >= 16:
                    flush_scalar()

                if j > first_j:
                    ge, gsc = divmod(j - 1 - first_j, NSC)
                    ve_evac(j - 1, g, ge, gsc)
                j += 1
        flush_scalar()

        @block.tensor
        def _(tn, g=g, j=j):
            sv.wait(tn, "gev", j * 4)
            te_mm2(tn, j - 1, g, E - 1, NSC - 1)
        ve_evac(j - 1, g, E - 1, NSC - 1)

        eev_need = j * 4
        for st in range(32):
            ic = g * 32 + st

            @block.vector
            def _(v, g=g, st=st, ic=ic, eev_need=eev_need):
                sv.waita(v, "eev", sv.base["eev"] + eev_need)
                if ic >= 2:
                    sv.wait(v, "ctmm", ic - 1)
                sv.inc(v.tensor_copy(cmb16[st % 2][:],
                                     comb[:, st * 128:(st + 1) * 128]),
                       "c16")

            @block.tensor
            def _(tn, st=st, ic=ic):
                sv.wait(tn, "c16", ic + 1)
                sv.wait(tn, "init", 1)
                if ic >= 1:
                    sv.wait(tn, "ctev", ic)
                sv.inc(tn.transpose(
                    psT[:, (st % 2) * 128:(st % 2) * 128 + 128],
                    cmb16[st % 2][:], identb[:]), "ctmm")

            @block.vector
            def _(v, g=g, st=st, ic=ic, pg=pg):
                sv.wait(v, "ctmm", ic + 1)
                if st == 0 and g >= 2:
                    sv.wait(v, "stc", 16 * (g - 1))
                sv.inc(v.tensor_copy(
                    ctb[pg][:, st * 128:(st + 1) * 128],
                    psT[:, (st % 2) * 128:(st % 2) * 128 + 128]), "ctev")

        @block.sync
        def _(sy, g=g, pg=pg):
            sv.wait(sy, "ctev", (g + 1) * 32)
            sv.inc(sy.nop(), "cdone")
            sv.dinc(sy.dma_start(out[g], ctb[pg][:]), "stc")

    @block.sync
    def _(sy):
        sv.waita(sy, "stc", sv.val("stc"))



def build_fused():
    nc = bass.Bass("TRN2", target_bir_lowering=False)
    EI = "ExternalInput"
    xT = cmajor(nc, "xT", EI, F32)
    wqT = pcol(nc, "wqT", EI, F32, n=C)
    wkT = pcol(nc, "wkT", EI, F32, n=C)
    wvT = pcol(nc, "wvT", EI, F32, n=C)
    woT = pcol(nc, "woT", EI, F32, n=C)
    bq = pcol(nc, "bq", EI)
    bk = pcol(nc, "bk", EI)
    bv_row = nc.dram_tensor("bvrow", [1, C], F32, kind=EI).ap()
    bo = pcol(nc, "bo", EI)
    lsc = pcol(nc, "lsc", EI)
    fker = pcol(nc, "fker", EI, BF16, n=C)
    sin_r = nc.dram_tensor("sinr", [128, 1], F32, kind=EI).ap()
    sk_r = nc.dram_tensor("skr", [128, 1], F32, kind=EI).ap()
    fbias = pcol(nc, "fbias", EI)
    ln1s = pcol(nc, "ln1s", EI)
    ln1b = pcol(nc, "ln1b", EI)
    ln2s = pcol(nc, "ln2s", EI)
    ln2b = pcol(nc, "ln2b", EI)
    tmk = pcol(nc, "tmk", EI)
    tmv = pcol(nc, "tmv", EI)
    tmr = pcol(nc, "tmr", EI)
    wdec = pcol(nc, "wdec", EI)
    ufr = pcol(nc, "ufr", EI)
    l1wk = pcol(nc, "l1wk", EI, F32, n=C)
    l1wv = pcol(nc, "l1wv", EI, F32, n=C)
    l1wr = pcol(nc, "l1wr", EI, F32, n=C)
    l1wo = pcol(nc, "l1wo", EI, F32, n=C)
    ftmk = pcol(nc, "ftmk", EI)
    ftmr = pcol(nc, "ftmr", EI)
    fwk = pcol(nc, "fwk", EI, F32, n=4096)
    fwr = pcol(nc, "fwr", EI, F32, n=C)
    fwv = pcol(nc, "fwv", EI, F32, n=C, c=4096)
    wm_w = pcol(nc, "wmw", EI, F32, n=C)
    bm = pcol(nc, "bm", EI)
    emb = nc.dram_tensor("emb", [128, 16], F32, kind=EI).ap()
    ident = nc.dram_tensor("ident", [128, 128], F32, kind=EI).ap()
    w1 = nc.dram_tensor("w1", [16 * 128, 512], F32, kind=EI).ap().rearrange(
        "(e p) i -> e p i", p=128)
    b1 = nc.dram_tensor("b1", [16 * 4 * 128, 1], F32, kind=EI).ap().rearrange(
        "(e i p) o -> e i p o", i=4, p=128)
    w2 = nc.dram_tensor("w2", [16 * 512, 128], F32, kind=EI).ap().rearrange(
        "(e i p) o -> e i p o", i=4, p=128)
    b2 = nc.dram_tensor("b2", [16, 128], F32, kind=EI).ap()
    ffwo = pcol(nc, "ffwo", EI, F32, n=C)
    ffbo = pcol(nc, "ffbo", EI)
    x4 = cmajor(nc, "x4", "ExternalOutput", F32)
    ln0 = cmajor(nc, "s_ln0", None, BF16)
    qn = cmajor(nc, "s_qn", None, BF16)
    kn = cmajor(nc, "s_kn", None, BF16)
    vT = nc.dram_tensor("s_vT", [T, C], BF16).ap().rearrange(
        "(n p) e -> n p e", p=128)
    a0 = cmajor(nc, "s_a0", None, BF16)
    x1 = cmajor(nc, "s_x1", None, F32)
    lnx1 = cmajor(nc, "s_lnx1", None, BF16)
    ybs = cmajor(nc, "s_yb", None, BF16)
    x2 = cmajor(nc, "s_x2", None, F32)
    z0 = cmajor(nc, "s_z0", None, F32)
    ln1 = cmajor(nc, "s_ln1", None, BF16)
    xk = cmajor(nc, "s_xk", None, BF16)
    xv = cmajor(nc, "s_xv", None, BF16)
    xr = cmajor(nc, "s_xr", None, BF16)
    k1 = cmajor(nc, "s_k1", None, F32)
    v1 = cmajor(nc, "s_v1", None, F32)
    srs = cmajor(nc, "s_sr", None, BF16)
    zs = cmajor(nc, "s_z", None, BF16)
    o1 = cmajor(nc, "s_o1", None, F32)
    x_i = cmajor(nc, "s_xi", None, F32)
    ln2 = cmajor(nc, "s_ln2", None, BF16)
    xk2 = cmajor(nc, "s_xk2", None, BF16)
    xr2 = cmajor(nc, "s_xr2", None, BF16)
    sr2 = cmajor(nc, "s_sr2", None, BF16)
    cmv = cmajor(nc, "s_cm", None, F32)
    att = cmajor(nc, "s_att", None, F32)
    x3 = cmajor(nc, "s_x3", None, F32)
    ln4 = cmajor(nc, "s_ln4", None, BF16)
    hs = cmajor(nc, "s_h", None, F32)
    wm = nc.dram_tensor("s_wm", [NG * 32 * 128, 16], F32).ap().rearrange(
        "(g s p) e -> g s p e", s=32, p=128)
    wmt = nc.dram_tensor("s_wmt", [NG * 16, T], F32).ap().rearrange(
        "(g p) t -> g p t", p=16)
    combt = cmajor(nc, "s_combt", None, BF16)
    mout = cmajor(nc, "s_mout", None, BF16)

    es0 = ExitStack()
    with es0:
        S = Sems(nc, es0)
        with nc.Block() as block:
            def stage(emitfn, *a, **k):
                with ExitStack() as es:
                    emitfn(nc, S, block, es, *a, **k)
                barrier(nc, S, block)

            stage(emit_ln, xT, ln0)
            stage(emit_projln, ln0, wqT, bq, qn)
            stage(emit_projln, ln0, wkT, bk, kn)
            stage(emit_vproj, ln0, wvT, bv_row, vT)
            stage(emit_l0attn, qn, kn, vT, woT, bo, lsc, a0)
            stage(emit_postres, a0, xT, x1)
            stage(emit_ln, x1, lnx1)
            stage(emit_l0ffn, lnx1, fker, sin_r, sk_r, fbias, ybs)
            stage(emit_postres, ybs, x1, x2)
            stage(emit_ln, x2, z0, out_dtype=F32)
            stage(emit_ln, z0, ln1, sc=ln1s, bi=ln1b)
            stage(emit_mix, ln1, [tmk, tmv, tmr], [xk, xv, xr])
            stage(emit_mm, xk, l1wk, k1, evac="copy", out_dtype=F32)
            stage(emit_mm, xv, l1wv, v1, evac="copy", out_dtype=F32)
            stage(emit_mm, xr, l1wr, srs, evac="sigmoid")
            stage(emit_wkv, k1, v1, srs, wdec, ufr, zs)
            stage(emit_mm, zs, l1wo, o1, evac="copy", out_dtype=F32)
            stage(emit_lif, o1, z0, x_i)
            stage(emit_ln, x_i, ln2, sc=ln2s, bi=ln2b)
            stage(emit_mix, ln2, [ftmk, ftmr], [xk2, xr2])
            stage(emit_mm, xr2, fwr, sr2, evac="sigmoid")
            stage(emit_cmix, xk2, sr2, fwk, fwv, cmv)
            stage(emit_lif, cmv, x_i, att)
            stage(emit_postres, att, x2, x3, so_f32=True)
            stage(emit_ln, x3, ln4)
            stage(emit_mm, ln4, wm_w, hs, evac="copy", out_dtype=F32,
                  b_in=bm)
            stage(emit_router, hs, emb, ident, wm, wmt)
            stage(emit_experts, hs, w1, b1, w2, b2, wm, wmt, ident, combt)
            stage(emit_mm, combt, ffwo, mout, evac="copy", out_dtype=BF16,
                  b_in=ffbo)
            stage(emit_postres, mout, x3, x4)
    return nc


_FUSED_NC = [None]


def kernel(**inp):
    f32 = np.float32
    x = inp["L0_attn_x"]
    col = lambda a: np.ascontiguousarray(np.asarray(a, f32).reshape(-1, 1))
    rep = lambda s: np.repeat(np.asarray(s, f32), 128).reshape(128, 1)
    ca = np.ascontiguousarray
    shared = {
        "wqT": ca(inp["L0_attn_Wq"].T.astype(f32)),
        "wkT": ca(inp["L0_attn_Wk"].T.astype(f32)),
        "wvT": ca(inp["L0_attn_Wv"].T.astype(f32)),
        "woT": ca(inp["L0_attn_Wo"].T.astype(f32)),
        "bq": col(inp["L0_attn_bq"]),
        "bk": col(inp["L0_attn_bk"]),
        "bvrow": ca(inp["L0_attn_bv"].astype(f32).reshape(1, -1)),
        "bo": col(inp["L0_attn_bo"]),
        "lsc": col(inp["L0_attn_layerscale"]),
        "fker": ca(inp["L0_ffn_kernel"]),
        "sinr": rep(inp["L0_ffn_input_scale"]),
        "skr": rep(inp["L0_ffn_kernel_scale"]),
        "fbias": ca(inp["L0_ffn_bias"].astype(f32).reshape(-1, 1)),
        "ln1s": col(inp["L1_attn_ln1_s"]),
        "ln1b": col(inp["L1_attn_ln1_b"]),
        "ln2s": col(inp["L1_attn_ln2_s"]),
        "ln2b": col(inp["L1_attn_ln2_b"]),
        "tmk": col(inp["L1_attn_tmk"]),
        "tmv": col(inp["L1_attn_tmv"]),
        "tmr": col(inp["L1_attn_tmr"]),
        "wdec": col(inp["L1_attn_w_decay"]),
        "ufr": col(inp["L1_attn_u_first"]),
        "l1wk": ca(inp["L1_attn_Wk"].astype(f32)),
        "l1wv": ca(inp["L1_attn_Wv"].astype(f32)),
        "l1wr": ca(inp["L1_attn_Wr"].astype(f32)),
        "l1wo": ca(inp["L1_attn_Wo"].astype(f32)),
        "ftmk": col(inp["L1_attn_f_tmk"]),
        "ftmr": col(inp["L1_attn_f_tmr"]),
        "fwk": ca(inp["L1_attn_fWk"].astype(f32)),
        "fwr": ca(inp["L1_attn_fWr"].astype(f32)),
        "fwv": ca(inp["L1_attn_fWv"].astype(f32)),
        "wmw": ca(inp["L1_ffn_Wm"].astype(f32)),
        "bm": col(inp["L1_ffn_bm"]),
        "emb": ca(inp["L1_ffn_expert_emb"].astype(f32)),
        "ident": np.eye(128, dtype=f32),
        "w1": ca(inp["L1_ffn_W1"].astype(f32).reshape(-1, 512)),
        "b1": ca(inp["L1_ffn_b1"].astype(f32).reshape(-1, 1)),
        "w2": ca(inp["L1_ffn_W2"].astype(f32).reshape(-1, 128)),
        "b2": ca(inp["L1_ffn_b2"].astype(f32)),
        "ffwo": ca(inp["L1_ffn_Wo"].astype(f32)),
        "ffbo": col(inp["L1_ffn_bo"]),
    }
    if _FUSED_NC[0] is None:
        _FUSED_NC[0] = build_fused()
    nc = _FUSED_NC[0]
    in_maps = []
    for b in range(NCORES):
        m = dict(shared)
        m["xT"] = np.ascontiguousarray(x[b].T)
        in_maps.append(m)
    res = run_bass_kernel_spmd(nc, in_maps, core_ids=list(range(NCORES)))
    out = np.stack([np.ascontiguousarray(res.results[b]["x4"].T)
                    for b in range(NCORES)])
    return out.astype(np.float32)


# version 13 (final)
# speedup vs baseline: 1.3870x; 1.0142x over previous
import numpy as np
import ml_dtypes
from contextlib import ExitStack
import concourse.bass as bass
import concourse.mybir as mybir
from concourse.bass_utils import run_bass_kernel_spmd

F32 = mybir.dt.float32
BF16 = mybir.dt.bfloat16
FP8 = mybir.dt.float8e4
AF = mybir.ActivationFunctionType
ALU = mybir.AluOpType
AXX = mybir.AxisListType.X

T = 4096
C = 1024
NG = 8
CH = 512
NCH = T // CH
LN_EPS = 1e-5
NCORES = 8

bfloat16 = ml_dtypes.bfloat16


class Sems:

    def __init__(self, nc, es):
        self.nc, self.es = nc, es
        self.h, self.c = {}, {}

    def get(self, n):
        if n not in self.h:
            self.h[n] = self.es.enter_context(self.nc.semaphore(n))
            self.c[n] = 0
        return self.h[n]

    def inc(self, instr, n, k=1):
        instr.then_inc(self.get(n), k)
        self.c[n] += k
        return self.c[n]

    def dinc(self, instr, n):
        return self.inc(instr, n, 16)

    def val(self, n):
        self.get(n)
        return self.c[n]

    def awach(self, eng, n, v):
        eng.wait_ge(self.get(n), v)

    def fence(self, eng, instr, tag="fnc"):
        v = self.inc(instr, tag)
        eng.wait_ge(self.get(tag), v)


class SV:

    def __init__(self, S, names):
        self.S = S
        self.base = {n: S.val(n) for n in names}

    def inc(self, instr, n, k=1):
        return self.S.inc(instr, n, k)

    def dinc(self, instr, n):
        return self.S.inc(instr, n, 16)

    def val(self, n):
        return self.S.val(n)

    def rel(self, n):
        return self.S.val(n) - self.base[n]

    def wait(self, eng, n, v_rel):
        eng.wait_ge(self.S.get(n), self.base[n] + v_rel)

    def waita(self, eng, n, v_abs):
        eng.wait_ge(self.S.get(n), v_abs)

    def fence(self, eng, instr):
        self.S.fence(eng, instr)


_UID = [0]


def _allocs(nc, es):
    _UID[0] += 1
    u = f"u{_UID[0]}_"
    sb = lambda name, *a: es.enter_context(nc.sbuf_tensor(u + name, *a))
    ps = lambda name, *a: es.enter_context(nc.psum_tensor(u + name, *a))
    return sb, ps


def barrier(nc, S, block):
    tgt = S.val("bar") + 5
    for engname in ("tensor", "vector", "scalar", "gpsimd", "sync"):
        eng = getattr(nc, engname)

        def mk(eng=eng, engname=engname):
            @getattr(block, engname)
            def _(e):
                S.inc(e.nop(), "bar")
                S.awach(e, "bar", tgt)
        mk()


def cmajor(nc, name, kind=None, dtype=F32, t=T):
    h = (nc.dram_tensor(name, [C, t], dtype, kind=kind) if kind
         else nc.dram_tensor(name, [C, t], dtype))
    return h.ap().rearrange("(g p) t -> g p t", p=128)


def pcol(nc, name, kind="ExternalInput", dtype=F32, n=1, c=C):
    h = nc.dram_tensor(name, [c, n], dtype, kind=kind)
    return h.ap().rearrange("(g p) t -> g p t", p=128)


def LN_SEMS(tag=""):
    return [n + tag for n in ("sq", "smm", "ev1", "acta", "ev2", "actc",
                              "rmm", "apl", "lninit", "sqrdy")]


class LNStats:

    def __init__(self, nc, es, sv, apl_per_chunk, tag="", rep_pairs=2):
        self.sv = sv
        self.apc = apl_per_chunk
        self.tag = tag
        self.rep_pairs = rep_pairs
        sb, ps = _allocs(nc, es)
        nsq = 2 if rep_pairs == 2 else 1
        self.sq = [[sb(f"lnsq{p}{g}", [128, CH], BF16) for g in range(NG)]
                   for p in range(nsq)]
        self.ps_sum = ps("lnpsum", [1, CH], F32)
        self.ps_sq = ps("lnpssq", [1, CH], F32)
        self.rep_m = [ps(f"lnrepm{p}", [128, CH], F32)
                      for p in range(rep_pairs)]
        self.rep_r = [ps(f"lnrepr{p}", [128, CH], F32)
                      for p in range(rep_pairs)]
        self.m_row = [sb(f"lnmrow{p}", [1, CH], F32) for p in range(2)]
        self.sq_row = [sb(f"lnsqrow{p}", [1, CH], F32) for p in range(2)]
        self.t_row = [sb(f"lntrow{p}", [1, CH], F32) for p in range(2)]
        self.r_row = [sb(f"lnrrow{p}", [1, CH], F32) for p in range(2)]
        self.ones_bf = sb("lnonesb", [128, 1], BF16)
        self.ones_f = sb("lnonesf", [1, 128], F32)
        self.eps_c = sb("lneps", [1, 1], F32)
        self.nh_c = sb("lnnh", [1, 1], F32)

    def emit_init(self, block):
        sv = self.sv

        @block.vector
        def _(v):
            v.memset(self.ones_bf[:], 1.0)
            v.memset(self.eps_c[:], LN_EPS)
            v.memset(self.nh_c[:], -0.5)
            sv.inc(v.memset(self.ones_f[:], 1.0), "lninit" + self.tag)

    def rp(self, j):
        return j % self.rep_pairs

    def emit_chunk(self, block, j, xsl):
        sv = self.sv
        p = j % 2 if self.rep_pairs == 2 else 0
        rp = self.rp(j)

        @block.scalar
        def _(s):
            if j == 0:
                sv.wait(s, "lninit" + self.tag, 1)
            if self.rep_pairs == 2:
                if j >= 2:
                    sv.wait(s, "smm" + self.tag, (j - 1) * 2)
            else:
                if j >= 1:
                    sv.wait(s, "smm" + self.tag, j * 2)
            sv.wait(s, "sqrdy" + self.tag, j + 1)
            for g in range(NG):
                sv.inc(s.activation(self.sq[p][g][:], xsl[g], AF.Square),
                       "sq" + self.tag)
        sq_v = sv.val("sq" + self.tag)

        @block.tensor
        def _(tn):
            sv.waita(tn, "sq" + self.tag, sq_v)
            if j >= 1:
                sv.wait(tn, "ev1" + self.tag, j)
            for g in range(NG):
                tn.matmul(self.ps_sum[:], self.ones_bf[:], xsl[g],
                          start=(g == 0), stop=(g == NG - 1))
            for g in range(NG):
                mm = tn.matmul(self.ps_sq[:], self.ones_bf[:],
                               self.sq[p][g][:], start=(g == 0),
                               stop=(g == NG - 1))
            sv.inc(mm, "smm" + self.tag, 2)

        @block.vector
        def _(v):
            sv.wait(v, "smm" + self.tag, (j + 1) * 2)
            if j >= 2:
                sv.wait(v, "rmm" + self.tag, j - 1)
            v.tensor_scalar(self.m_row[p][:], self.ps_sum[:],
                            1.0 / C, None, op0=ALU.mult)
            sv.fence(v, v.tensor_scalar(self.sq_row[p][:], self.ps_sq[:],
                                        1.0 / C, None, op0=ALU.mult))
            sv.fence(v, v.tensor_tensor(self.t_row[p][:], self.m_row[p][:],
                                        self.m_row[p][:], op=ALU.mult))
            sv.inc(v.tensor_tensor(self.t_row[p][:], self.sq_row[p][:],
                                   self.t_row[p][:], op=ALU.subtract),
                   "ev1" + self.tag)

        @block.scalar
        def _(s):
            sv.wait(s, "ev1" + self.tag, j + 1)
            s.activation(self.t_row[p][:], self.t_row[p][:], AF.Ln,
                         bias=self.eps_c[:])
            sv.inc(s.activation(self.r_row[p][:], self.t_row[p][:], AF.Exp,
                                scale=self.nh_c[:]), "actc" + self.tag)

        @block.tensor
        def _(tn):
            sv.wait(tn, "actc" + self.tag, j + 1)
            if self.rep_pairs == 2:
                if j >= 2:
                    sv.wait(tn, "apl" + self.tag, (j - 1) * self.apc)
            else:
                if j >= 1:
                    sv.wait(tn, "apl" + self.tag, j * self.apc)
            tn.matmul(self.rep_m[rp][:], self.ones_f[:], self.m_row[p][:],
                      start=True, stop=True)
            sv.inc(tn.matmul(self.rep_r[rp][:], self.ones_f[:],
                             self.r_row[p][:], start=True, stop=True),
                   "rmm" + self.tag)
        return sv.val("rmm" + self.tag)



def emit_ln(nc, S, block, es, xin, out, sc=None, bi=None, out_dtype=BF16):
    sv = SV(S, ["ld", "ldf", "st", "apl"] + LN_SEMS())
    sb, ps = _allocs(nc, es)
    xb = [sb(f"xb{g}", [128, T], BF16) for g in range(NG)]
    xf = [[sb(f"xf{p}{g}", [128, CH], F32) for g in range(NG)]
          for p in range(2)]
    tmp = sb("tmp", [128, CH], F32)
    ob = [[sb(f"ob{p}{g}", [128, CH], out_dtype) for g in range(NG)]
          for p in range(2)]
    scale_bias = sc is not None
    if scale_bias:
        scs = [sb(f"scs{g}", [128, 1], F32) for g in range(NG)]
        bis = [sb(f"bis{g}", [128, 1], F32) for g in range(NG)]
    L = LNStats(nc, es, sv, apl_per_chunk=NG)
    L.emit_init(block)

    @block.gpsimd
    def _(g_):
        for g in range(NG):
            sv.dinc(g_.dma_start(xb[g][:], xin[g]), "ld")
        if scale_bias:
            for g in range(NG):
                sv.dinc(g_.dma_start(scs[g][:], sc[g]), "ld")
                sv.dinc(g_.dma_start(bis[g][:], bi[g]), "ld")
    ld_all = sv.val("ld")

    for j in range(NCH):
        p = j % 2
        sl = slice(j * CH, (j + 1) * CH)

        @block.gpsimd
        def _(gp, j=j, p=p, sl=sl):
            if j >= 2:
                sv.wait(gp, "apl", (j - 1) * NG)
            for g in range(NG):
                sv.dinc(gp.dma_start(xf[p][g][:], xin[g][:, sl]), "ldf")
        ldf_v = sv.val("ldf")

        @block.scalar
        def _(s, j=j):
            if j == 0:
                sv.waita(s, "ld", ld_all)
            sv.inc(s.nop(), "sqrdy")
        rmm_v = L.emit_chunk(block, j, [xb[g][:, sl] for g in range(NG)])

        @block.vector
        def _(v, j=j, p=p, rmm_v=rmm_v, ldf_v=ldf_v):
            sv.waita(v, "rmm", rmm_v)
            sv.waita(v, "ldf", ldf_v)
            if j >= 2:
                sv.wait(v, "st", 16 * NG * (j - 1))
            for g in range(NG):
                v.tensor_tensor(tmp[:], xf[p][g][:], L.rep_m[p][:],
                                op=ALU.subtract)
                if scale_bias:
                    v.tensor_tensor(tmp[:], tmp[:], L.rep_r[p][:],
                                    op=ALU.mult)
                    sv.inc(v.tensor_scalar(ob[p][g][:], tmp[:], scs[g][:],
                                           bis[g][:], op0=ALU.mult,
                                           op1=ALU.add), "apl")
                else:
                    sv.inc(v.tensor_tensor(ob[p][g][:], tmp[:],
                                           L.rep_r[p][:], op=ALU.mult),
                           "apl")
        apl_v = sv.val("apl")

        @block.sync
        def _(sy, j=j, p=p, sl=sl, apl_v=apl_v):
            sv.waita(sy, "apl", apl_v)
            for g in range(NG):
                sv.dinc(sy.dma_start(out[g][:, sl], ob[p][g][:]), "st")

    @block.sync
    def _(sy):
        sv.waita(sy, "st", sv.val("st"))


def emit_postres(nc, S, block, es, so_in, x_in, out, so_f32=False):
    sv = SV(S, ["ld", "ldx", "st", "apl"] + LN_SEMS())
    sb, ps = _allocs(nc, es)
    sob = [sb(f"sob{g}", [128, T], BF16) for g in range(NG)]
    sof = ([[sb(f"sof{p}{g}", [128, CH], F32) for g in range(NG)]
            for p in range(2)] if so_f32 else None)
    xo = [[sb(f"xo{p}{g}", [128, CH], F32) for g in range(NG)]
          for p in range(2)]
    tmp = sb("tmp", [128, CH], F32)
    ob = [[sb(f"ob{p}{g}", [128, CH], F32) for g in range(NG)]
          for p in range(2)]
    L = LNStats(nc, es, sv, apl_per_chunk=NG)
    L.emit_init(block)

    @block.gpsimd
    def _(g_):
        for g in range(NG):
            sv.dinc(g_.dma_start(sob[g][:], so_in[g]), "ld")
    ld_all = sv.val("ld")

    for j in range(NCH):
        p = j % 2
        sl = slice(j * CH, (j + 1) * CH)

        @block.gpsimd
        def _(gp, j=j, p=p, sl=sl):
            if j >= 2:
                sv.wait(gp, "apl", (j - 1) * NG)
            for g in range(NG):
                sv.dinc(gp.dma_start(xo[p][g][:], x_in[g][:, sl]), "ldx")
                if sof is not None:
                    sv.dinc(gp.dma_start(sof[p][g][:], so_in[g][:, sl]),
                            "ldx")
        ldx_v = sv.val("ldx")

        @block.scalar
        def _(s, j=j):
            if j == 0:
                sv.waita(s, "ld", ld_all)
            sv.inc(s.nop(), "sqrdy")
        rmm_v = L.emit_chunk(block, j, [sob[g][:, sl] for g in range(NG)])

        @block.vector
        def _(v, j=j, p=p, sl=sl, rmm_v=rmm_v, ldx_v=ldx_v):
            sv.waita(v, "rmm", rmm_v)
            sv.waita(v, "ldx", ldx_v)
            if j >= 2:
                sv.wait(v, "st", 16 * NG * (j - 1))
            for g in range(NG):
                v.tensor_tensor(tmp[:],
                                (sof[p][g][:] if sof is not None
                                 else sob[g][:, sl]),
                                L.rep_m[p][:], op=ALU.subtract)
                v.tensor_tensor(tmp[:], tmp[:], L.rep_r[p][:], op=ALU.mult)
                sv.inc(v.tensor_tensor(ob[p][g][:], tmp[:], xo[p][g][:],
                                       op=ALU.add), "apl")
        apl_v = sv.val("apl")

        @block.sync
        def _(sy, j=j, p=p, sl=sl, apl_v=apl_v):
            sv.waita(sy, "apl", apl_v)
            for g in range(NG):
                sv.dinc(sy.dma_start(out[g][:, sl], ob[p][g][:]), "st")

    @block.sync
    def _(sy):
        sv.waita(sy, "st", sv.val("st"))


def emit_postres_ln(nc, S, block, es, so_in, x_in, out_x, out_ln,
                    so_f32=False, ln_dtype=BF16, ln_sc=None, ln_bi=None):
    sv = SV(S, ["ld", "ldx", "ldb", "stx", "st2", "xc"]
            + LN_SEMS() + LN_SEMS("B"))
    sb, ps = _allocs(nc, es)
    sob = [[sb(f"sob{p}{g}", [128, CH], BF16) for g in range(NG)]
           for p in range(2)]
    sof = ([[sb(f"sof{p}{g}", [128, CH], F32) for g in range(NG)]
            for p in range(2)] if so_f32 else None)
    xo = [[sb(f"xo{p}{g}", [128, CH], F32) for g in range(NG)]
          for p in range(2)]
    tmp = sb("tmp", [128, CH], F32)
    xn = [[sb(f"xn{p}{g}", [128, CH], F32) for g in range(NG)]
          for p in range(2)]
    xnb = [[sb(f"xnb{p}{g}", [128, CH], BF16) for g in range(NG)]
           for p in range(2)]
    ob2 = [[sb(f"ob2{p}{g}", [128, CH], ln_dtype) for g in range(NG)]
           for p in range(2)]
    if ln_sc is not None:
        scs = [sb(f"scs{g}", [128, 1], F32) for g in range(NG)]
        bis = [sb(f"bis{g}", [128, 1], F32) for g in range(NG)]
    L1 = LNStats(nc, es, sv, apl_per_chunk=NG, tag="", rep_pairs=1)
    L2 = LNStats(nc, es, sv, apl_per_chunk=NG, tag="B", rep_pairs=1)
    L1.emit_init(block)
    L2.emit_init(block)

    @block.gpsimd
    def _(g_):
        if ln_sc is not None:
            for g in range(NG):
                sv.dinc(g_.dma_start(scs[g][:], ln_sc[g]), "ld")
                sv.dinc(g_.dma_start(bis[g][:], ln_bi[g]), "ld")
        sv.inc(g_.nop(), "ld")
    ld_all = sv.val("ld")

    for j in range(NCH):
        p = j % 2
        sl = slice(j * CH, (j + 1) * CH)

        @block.sync
        def _(sy, j=j, p=p, sl=sl):
            if j >= 2:
                sv.wait(sy, "apl", (j - 1) * NG)
            for g in range(NG):
                sv.dinc(sy.dma_start(xo[p][g][:], x_in[g][:, sl]), "ldx")
                if sof is not None:
                    sv.dinc(sy.dma_start(sof[p][g][:], so_in[g][:, sl]),
                            "ldx")
        ldx_v = sv.val("ldx")

        @block.gpsimd
        def _(g_, j=j, p=p, sl=sl):
            if j >= 2:
                sv.wait(g_, "smm", j * 2)
            for g in range(NG):
                sv.dinc(g_.dma_start(sob[p][g][:], so_in[g][:, sl]), "ldb")
        ldb_v = sv.val("ldb")

        @block.scalar
        def _(s, j=j, ldb_v=ldb_v):
            if j == 0:
                sv.waita(s, "ld", ld_all)
            sv.waita(s, "ldb", ldb_v)
            sv.inc(s.nop(), "sqrdy")
        rmm_v = L1.emit_chunk(block, j,
                              [sob[p][g][:] for g in range(NG)])

        @block.vector
        def _(v, j=j, p=p, sl=sl, rmm_v=rmm_v, ldx_v=ldx_v):
            sv.waita(v, "rmm", rmm_v)
            sv.waita(v, "ldx", ldx_v)
            if j >= 2:
                sv.wait(v, "aplB", (j - 1) * NG)
                sv.wait(v, "stx", 16 * NG * (j - 1))
                sv.wait(v, "smmB", (j - 1) * 2)
            rp1 = L1.rp(j)
            for g in range(NG):
                v.tensor_tensor(tmp[:],
                                (sof[p][g][:] if sof is not None
                                 else sob[p][g][:]),
                                L1.rep_m[rp1][:], op=ALU.subtract)
                v.tensor_tensor(tmp[:], tmp[:], L1.rep_r[rp1][:],
                                op=ALU.mult)
                sv.inc(v.tensor_tensor(xn[p][g][:], tmp[:], xo[p][g][:],
                                       op=ALU.add), "apl")
                sv.inc(v.tensor_copy(xnb[p][g][:], xn[p][g][:]), "xc")
        apl_v = sv.val("apl")
        xc_v = sv.val("xc")

        @block.sync
        def _(sy, j=j, p=p, sl=sl, apl_v=apl_v):
            sv.waita(sy, "apl", apl_v)
            for g in range(NG):
                sv.dinc(sy.dma_start(out_x[g][:, sl], xn[p][g][:]), "stx")

        @block.scalar
        def _(s, xc_v=xc_v):
            sv.waita(s, "xc", xc_v)
            sv.inc(s.nop(), "sqrdyB")
        rmm2_v = L2.emit_chunk(block, j, [xnb[p][g][:] for g in range(NG)])

        @block.vector
        def _(v, j=j, p=p, rmm2_v=rmm2_v):
            sv.waita(v, "rmmB", rmm2_v)
            if j >= 2:
                sv.wait(v, "st2", 16 * NG * (j - 1))
            rp2 = L2.rp(j)
            for g in range(NG):
                v.tensor_tensor(tmp[:], xn[p][g][:], L2.rep_m[rp2][:],
                                op=ALU.subtract)
                if ln_sc is not None:
                    v.tensor_tensor(tmp[:], tmp[:], L2.rep_r[rp2][:],
                                    op=ALU.mult)
                    sv.inc(v.tensor_scalar(ob2[p][g][:], tmp[:], scs[g][:],
                                           bis[g][:], op0=ALU.mult,
                                           op1=ALU.add), "aplB")
                else:
                    sv.inc(v.tensor_tensor(ob2[p][g][:], tmp[:],
                                           L2.rep_r[rp2][:], op=ALU.mult),
                           "aplB")
        aplB_v = sv.val("aplB")

        @block.sync
        def _(sy, j=j, p=p, sl=sl, aplB_v=aplB_v):
            sv.waita(sy, "aplB", aplB_v)
            for g in range(NG):
                sv.dinc(sy.dma_start(out_ln[g][:, sl], ob2[p][g][:]), "st2")

    @block.sync
    def _(sy):
        sv.waita(sy, "stx", sv.val("stx"))
        sv.waita(sy, "st2", sv.val("st2"))


def emit_mm(nc, S, block, es, xin, w_in, out, evac="copy", out_dtype=BF16,
            b_in=None):
    sv = SV(S, ["ld", "st", "mm", "ev"])
    sb, ps = _allocs(nc, es)
    xb = [sb(f"xb{g}", [128, T], BF16) for g in range(NG)]
    wb = [sb(f"wb{g}", [128, C], BF16) for g in range(NG)]
    pp = [ps(f"pp{i}", [128, CH], F32) for i in range(4)]
    ob = [sb(f"ob{i}", [128, CH], out_dtype) for i in range(4)]
    if b_in is not None:
        bs = [sb(f"bs{g}", [128, 1], F32) for g in range(NG)]

    @block.gpsimd
    def _(g_):
        for g in range(NG):
            sv.dinc(g_.dma_start(xb[g][:], xin[g]), "ld")
        for g in range(NG):
            sv.dinc(g_.dma_start(wb[g][:], w_in[g]), "ld")
        if b_in is not None:
            for g in range(NG):
                sv.dinc(g_.dma_start(bs[g][:], b_in[g]), "ld")
    ld_all = sv.val("ld")

    idx = 0
    for g2 in range(NG):
        for j in range(NCH):
            q = idx % 4
            sl = slice(j * CH, (j + 1) * CH)

            @block.tensor
            def _(tn, g2=g2, j=j, q=q, sl=sl, idx=idx):
                if idx == 0:
                    sv.waita(tn, "ld", ld_all)
                if idx >= 4:
                    sv.wait(tn, "ev", idx - 3)
                for d in range(NG):
                    mm = tn.matmul(pp[q][:],
                                   wb[d][:, g2 * 128:(g2 + 1) * 128],
                                   xb[d][:, sl], start=(d == 0),
                                   stop=(d == NG - 1))
                sv.inc(mm, "mm")

            if evac == "sigmoid":
                if j == NCH - 1:
                    @block.scalar
                    def _(s, g2=g2, idx=idx):
                        for jj in range(NCH):
                            ix = g2 * NCH + jj
                            sv.wait(s, "mm", ix + 1)
                            if ix >= 4:
                                sv.wait(s, "st", 16 * (ix - 3))
                            sv.inc(s.activation(ob[ix % 4][:], pp[ix % 4][:],
                                                AF.Sigmoid), "ev")
            else:
                @block.vector
                def _(v, g2=g2, q=q, idx=idx):
                    sv.wait(v, "mm", idx + 1)
                    if idx >= 4:
                        sv.wait(v, "st", 16 * (idx - 3))
                    if b_in is not None:
                        sv.inc(v.tensor_scalar(ob[idx % 4][:], pp[q][:],
                                               bs[g2][:], None, op0=ALU.add),
                               "ev")
                    else:
                        sv.inc(v.tensor_copy(ob[idx % 4][:], pp[q][:]), "ev")

            @block.sync
            def _(sy, g2=g2, q=q, sl=sl, idx=idx):
                sv.wait(sy, "ev", idx + 1)
                sv.dinc(sy.dma_start(out[g2][:, sl], ob[idx % 4][:]), "st")
            idx += 1

    @block.sync
    def _(sy):
        sv.waita(sy, "st", sv.val("st"))


def emit_mix(nc, S, block, es, xin, tms, outs):
    nmix = len(tms)
    sv = SV(S, ["ld", "st", "ev"])
    sb, ps = _allocs(nc, es)
    xb = [sb(f"xb{g}", [128, T], BF16) for g in range(NG)]
    tmv = [[sb(f"tm{k}{g}", [128, 1], F32) for g in range(NG)]
           for k in range(nmix)]
    tmp = sb("tmp", [128, CH], F32)
    ob = [sb(f"ob{i}", [128, CH], BF16) for i in range(4)]

    @block.gpsimd
    def _(g_):
        for g in range(NG):
            sv.dinc(g_.dma_start(xb[g][:], xin[g]), "ld")
        for k in range(nmix):
            for g in range(NG):
                sv.dinc(g_.dma_start(tmv[k][g][:], tms[k][g]), "ld")
    ld_all = sv.val("ld")

    idx = 0
    for k in range(nmix):
        for g in range(NG):
            for j in range(NCH):
                q = idx % 4
                sl = slice(j * CH, (j + 1) * CH)

                @block.vector
                def _(v, k=k, g=g, j=j, q=q, sl=sl, idx=idx):
                    if idx == 0:
                        sv.waita(v, "ld", ld_all)
                    if idx >= 4:
                        sv.wait(v, "st", 16 * (idx - 3))
                    if j == 0:
                        v.tensor_scalar(ob[q][:, 0:1], xb[g][:, 0:1],
                                        tmv[k][g][:], None, op0=ALU.mult)
                        xx = xb[g][:, 0:CH - 1]
                        v.tensor_tensor(tmp[:, 1:CH], xb[g][:, 1:CH], xx,
                                        op=ALU.subtract)
                        sv.inc(v.scalar_tensor_tensor(
                            ob[q][:, 1:CH], tmp[:, 1:CH], tmv[k][g][:], xx,
                            op0=ALU.mult, op1=ALU.add), "ev")
                    else:
                        xx = xb[g][:, j * CH - 1:(j + 1) * CH - 1]
                        v.tensor_tensor(tmp[:], xb[g][:, sl], xx,
                                        op=ALU.subtract)
                        sv.inc(v.scalar_tensor_tensor(
                            ob[q][:], tmp[:], tmv[k][g][:], xx,
                            op0=ALU.mult, op1=ALU.add), "ev")

                @block.sync
                def _(sy, k=k, g=g, q=q, sl=sl, idx=idx):
                    sv.wait(sy, "ev", idx + 1)
                    sv.dinc(sy.dma_start(outs[k][g][:, sl], ob[q][:]), "st")
                idx += 1

    @block.sync
    def _(sy):
        sv.waita(sy, "st", sv.val("st"))


def emit_projln(nc, S, block, es, xin, w_in, b_in, out):
    sv = SV(S, ["ld", "st", "qmm", "qev", "apl"] + LN_SEMS())
    sb, ps = _allocs(nc, es)
    xb = [sb(f"xb{g}", [128, T], BF16) for g in range(NG)]
    wb = [sb(f"wb{g}", [128, C], BF16) for g in range(NG)]
    bs = [sb(f"bs{g}", [128, 1], F32) for g in range(NG)]
    qf = [[sb(f"qf{p}{g}", [128, CH], F32) for g in range(NG)]
          for p in range(2)]
    qb = [[sb(f"qb{p}{g}", [128, CH], BF16) for g in range(NG)]
          for p in range(2)]
    tmp = sb("tmp", [128, CH], F32)
    ob = [[sb(f"ob{p}{g}", [128, CH], BF16) for g in range(NG)]
          for p in range(2)]
    pq = [ps(f"pq{i}", [128, CH], F32) for i in range(2)]
    L = LNStats(nc, es, sv, apl_per_chunk=NG)
    L.emit_init(block)

    @block.gpsimd
    def _(g_):
        for g in range(NG):
            sv.dinc(g_.dma_start(xb[g][:], xin[g]), "ld")
        for g in range(NG):
            sv.dinc(g_.dma_start(wb[g][:], w_in[g]), "ld")
        for g in range(NG):
            sv.dinc(g_.dma_start(bs[g][:], b_in[g]), "ld")
    ld_all = sv.val("ld")

    for j in range(NCH):
        p = j % 2
        sl = slice(j * CH, (j + 1) * CH)
        for g2 in range(NG):
            q = (j * NG + g2) % 2
            idx = j * NG + g2

            @block.tensor
            def _(tn, j=j, g2=g2, q=q, idx=idx, sl=sl):
                if idx == 0:
                    sv.waita(tn, "ld", ld_all)
                if idx >= 2:
                    sv.wait(tn, "qev", idx - 1)
                for d in range(NG):
                    mm = tn.matmul(pq[q][:],
                                   wb[d][:, g2 * 128:(g2 + 1) * 128],
                                   xb[d][:, sl], start=(d == 0),
                                   stop=(d == NG - 1))
                sv.inc(mm, "qmm")

            @block.vector
            def _(v, j=j, g2=g2, q=q, idx=idx, p=p):
                sv.wait(v, "qmm", idx + 1)
                if j >= 2:
                    sv.wait(v, "smm", (j - 1) * 2)
                    sv.wait(v, "apl", (j - 1) * NG)
                v.tensor_scalar(qf[p][g2][:], pq[q][:], bs[g2][:], None,
                                op0=ALU.add)
                sv.inc(v.tensor_copy(qb[p][g2][:], qf[p][g2][:]), "qev")
        qev_v = sv.val("qev")

        @block.scalar
        def _(s, qev_v=qev_v):
            sv.waita(s, "qev", qev_v)
            sv.inc(s.nop(), "sqrdy")
        rmm_v = L.emit_chunk(block, j, [qb[p][g][:] for g in range(NG)])

        @block.vector
        def _(v, j=j, p=p, rmm_v=rmm_v):
            sv.waita(v, "rmm", rmm_v)
            if j >= 2:
                sv.wait(v, "st", 16 * NG * (j - 1))
            for g in range(NG):
                v.tensor_tensor(tmp[:], qf[p][g][:], L.rep_m[p][:],
                                op=ALU.subtract)
                sv.inc(v.tensor_tensor(ob[p][g][:], tmp[:], L.rep_r[p][:],
                                       op=ALU.mult), "apl")
        apl_v = sv.val("apl")

        @block.sync
        def _(sy, j=j, p=p, sl=sl, apl_v=apl_v):
            sv.waita(sy, "apl", apl_v)
            for g in range(NG):
                sv.dinc(sy.dma_start(out[g][:, sl], ob[p][g][:]), "st")

    @block.sync
    def _(sy):
        sv.waita(sy, "st", sv.val("st"))


def emit_vproj(nc, S, block, es, xin, w_in, b_in, out):
    sv = SV(S, ["ld", "st", "mm", "ev", "init"])
    sb, ps = _allocs(nc, es)
    xb = [sb(f"xb{g}", [128, T], BF16) for g in range(NG)]
    wb = [sb(f"wb{g}", [128, C], BF16) for g in range(NG)]
    brow_f = sb("browf", [1, C], F32)
    brow = sb("brow", [1, C], BF16)
    ones_c = sb("onesc", [1, 128], BF16)
    pv = [ps(f"pv{i}", [128, CH], F32) for i in range(2)]
    ob = [sb(f"ob{i}", [128, CH], BF16) for i in range(4)]

    @block.gpsimd
    def _(g_):
        for g in range(NG):
            sv.dinc(g_.dma_start(xb[g][:], xin[g]), "ld")
        for g in range(NG):
            sv.dinc(g_.dma_start(wb[g][:], w_in[g]), "ld")
        sv.dinc(g_.dma_start(brow_f[:], b_in), "ld")
    ld_all = sv.val("ld")

    @block.vector
    def _(v):
        sv.waita(v, "ld", ld_all)
        v.memset(ones_c[:], 1.0)
        sv.inc(v.tensor_copy(brow[:], brow_f[:]), "init")

    idx = 0
    for tt in range(32):
        for eh in range(2):
            q = idx % 2
            esl = slice(eh * CH, (eh + 1) * CH)

            @block.tensor
            def _(tn, tt=tt, q=q, idx=idx, esl=esl):
                if idx == 0:
                    sv.wait(tn, "init", 1)
                if idx >= 2:
                    sv.wait(tn, "ev", idx - 1)
                for d in range(NG):
                    tn.matmul(pv[q][:], xb[d][:, tt * 128:(tt + 1) * 128],
                              wb[d][:, esl], start=(d == 0), stop=False)
                sv.inc(tn.matmul(pv[q][:], ones_c[:], brow[0:1, esl],
                                 start=False, stop=True), "mm")

            @block.vector
            def _(v, q=q, idx=idx):
                sv.wait(v, "mm", idx + 1)
                if idx >= 4:
                    sv.wait(v, "st", 16 * (idx - 3))
                sv.inc(v.tensor_copy(ob[idx % 4][:], pv[q][:]), "ev")

            @block.sync
            def _(sy, tt=tt, idx=idx, esl=esl):
                sv.wait(sy, "ev", idx + 1)
                sv.dinc(sy.dma_start(out[tt][:, esl], ob[idx % 4][:]), "st")
            idx += 1

    @block.sync
    def _(sy):
        sv.waita(sy, "st", sv.val("st"))


def emit_l0attn(nc, S, block, es, qn_in, kn_in, v_in, wo_in, bo_in, ls_in,
                out):
    CHA = 256
    NCA = T // CHA
    sv = SV(S, ["ld", "qld", "st", "init", "ptmm", "sgev", "temm", "oev",
                "pjmm", "pjev"])
    sb, ps = _allocs(nc, es)
    kn = [sb(f"kn{g}", [128, T], BF16) for g in range(NG)]
    vb = [sb(f"vb{n}", [128, C], BF16) for n in range(32)]
    wo = [sb(f"wo{g}", [128, C], BF16) for g in range(NG)]
    bo = [sb(f"bo{g}", [128, 1], F32) for g in range(NG)]
    ls = [sb(f"ls{g}", [128, 1], F32) for g in range(NG)]
    qc = [[sb(f"qc{p}{g}", [128, CHA], BF16) for g in range(NG)]
          for p in range(2)]
    pb = [sb(f"pbuf{p}", [128, CHA], BF16) for p in range(2)]
    obf = [sb(f"obf{g}", [128, CHA], BF16) for g in range(NG)]
    ob = [sb(f"ob{i}", [128, CHA], BF16) for i in range(4)]
    logs_c = sb("logsc", [128, 1], F32)
    scl_c = sb("sclc", [128, 1], F32)
    zrow = sb("zrow", [1, 2 * CHA], BF16)
    onec = sb("onec", [1, 128], BF16)
    psPt = [ps(f"psPt{i}", [128, CHA], F32) for i in range(2)]
    psO = [ps(f"psO{i}", [128, 2 * CHA], F32) for i in range(4)]
    psPj = [ps(f"psPj{i}", [128, CHA], F32) for i in range(2)]

    @block.gpsimd
    def _(g_):
        for g in range(NG):
            sv.dinc(g_.dma_start(kn[g][:], kn_in[g]), "ld")
        for n in range(32):
            sv.dinc(g_.dma_start(vb[n][:], v_in[n]), "ld")
        for g in range(NG):
            sv.dinc(g_.dma_start(wo[g][:], wo_in[g]), "ld")
            sv.dinc(g_.dma_start(bo[g][:], bo_in[g]), "ld")
            sv.dinc(g_.dma_start(ls[g][:], ls_in[g]), "ld")
    ld_all = sv.val("ld")

    @block.vector
    def _(v):
        v.memset(scl_c[:], 1.0 / 32.0)
        v.memset(zrow[:], 0.0)
        v.memset(onec[:], 1.0)
        sv.inc(v.memset(logs_c[:], -float(np.log(np.float32(T)))), "init")

    for sc in range(NCA):
        ssl = slice(sc * CHA, (sc + 1) * CHA)

        @block.sync
        def _(sy, sc=sc, ssl=ssl):
            if sc >= 2:
                sv.wait(sy, "ptmm", (sc - 1) * 32)
            for g in range(NG):
                sv.dinc(sy.dma_start(qc[sc % 2][g][:], qn_in[g][:, ssl]),
                        "qld")
        qld_v = sv.val("qld")

        for tt in range(33):
            hp = tt % 2
            it = sc * 32 + tt

            @block.tensor
            def _(tn, sc=sc, tt=tt, hp=hp, it=it, qld_v=qld_v):
                if tt < 32:
                    if sc == 0 and tt == 0:
                        sv.waita(tn, "ld", ld_all)
                    sv.waita(tn, "qld", qld_v)
                    if it >= 2:
                        sv.wait(tn, "sgev", it - 1)
                    if tt == 0 and sc == 0:
                        sv.wait(tn, "init", 1)
                    for g in range(NG):
                        mm = tn.matmul(psPt[hp][:],
                                       kn[g][:, tt * 128:(tt + 1) * 128],
                                       qc[sc % 2][g][:], start=(g == 0),
                                       stop=(g == NG - 1))
                    sv.inc(mm, "ptmm")
                if tt >= 1:
                    to = tt - 1
                    if to == 0:
                        if sc >= 1:
                            sv.wait(tn, "oev", sc * NG)
                        for i4 in range(4):
                            tn.matmul(psO[i4][:], onec[:], zrow[:],
                                      start=True, stop=False)
                    sv.wait(tn, "sgev", sc * 32 + to + 1)
                    for g in range(NG):
                        mm = tn.matmul(
                            psO[g // 2][:, (g % 2) * CHA:(g % 2 + 1) * CHA],
                            vb[to][:, g * 128:(g + 1) * 128], pb[to % 2][:],
                            start=False, stop=(to == 31))
                    sv.inc(mm, "temm")

        @block.scalar
        def _(s, sc=sc):
            for tt in range(32):
                it = sc * 32 + tt
                hp = tt % 2
                sv.wait(s, "ptmm", it + 1)
                if it >= 2:
                    sv.wait(s, "temm", it - 1)
                sv.inc(s.activation(pb[hp][:], psPt[hp][:], AF.Sigmoid,
                                    bias=logs_c[:], scale=scl_c[:]),
                       "sgev")

        @block.vector
        def _(v, sc=sc):
            sv.wait(v, "temm", (sc + 1) * 32)
            if sc >= 1:
                sv.wait(v, "pjmm", sc * NG)
            for g in range(NG):
                sv.inc(v.tensor_copy(
                    obf[g][:],
                    psO[g // 2][:, (g % 2) * CHA:(g % 2 + 1) * CHA]), "oev")
        oev_v = sv.val("oev")

        for g2 in range(NG):
            qj = g2 % 2
            ij = sc * NG + g2

            @block.tensor
            def _(tn, g2=g2, qj=qj, ij=ij, oev_v=oev_v):
                sv.waita(tn, "oev", oev_v)
                if ij >= 2:
                    sv.wait(tn, "pjev", ij - 1)
                for e in range(NG):
                    mm = tn.matmul(psPj[qj][:],
                                   wo[e][:, g2 * 128:(g2 + 1) * 128],
                                   obf[e][:], start=(e == 0),
                                   stop=(e == NG - 1))
                sv.inc(mm, "pjmm")

            @block.vector
            def _(v, g2=g2, qj=qj, ij=ij):
                sv.wait(v, "pjmm", ij + 1)
                if ij >= 4:
                    sv.wait(v, "st", 16 * (ij - 3))
                sv.inc(v.tensor_scalar(ob[ij % 4][:], psPj[qj][:], bo[g2][:],
                                       ls[g2][:], op0=ALU.add, op1=ALU.mult),
                       "pjev")

            @block.gpsimd
            def _(gp, g2=g2, ij=ij, ssl=ssl):
                sv.wait(gp, "pjev", ij + 1)
                sv.dinc(gp.dma_start(out[g2][:, ssl], ob[ij % 4][:]), "st")

    @block.sync
    def _(sy):
        sv.waita(sy, "st", sv.val("st"))


def emit_l0ffn(nc, S, block, es, xin, k_in, sin_in, sk_in, b_in, out):
    sv = SV(S, ["ld", "st", "mm", "ev", "init", "quant"])
    sb, ps = _allocs(nc, es)
    xb = [sb(f"xb{g}", [128, T], BF16) for g in range(NG)]
    kb = [sb(f"kb{g}", [128, C], BF16) for g in range(NG)]
    xq = [sb(f"xq{g}", [128, T], FP8) for g in range(NG)]
    kq = [sb(f"kq{g}", [128, C], FP8) for g in range(NG)]
    bs = [sb(f"bs{g}", [128, 1], F32) for g in range(NG)]
    s_in = sb("s_in", [128, 1], F32)
    s_k = sb("s_k", [128, 1], F32)
    r_in = sb("r_in", [128, 1], F32)
    r_k = sb("r_k", [128, 1], F32)
    prod = sb("prod", [128, 1], F32)
    yb = sb("ybt", [128, CH], BF16)
    ob = [sb(f"ob{i}", [128, CH], BF16) for i in range(4)]
    pq = [ps(f"pq{i}", [128, CH], F32) for i in range(2)]

    @block.gpsimd
    def _(g_):
        for g in range(NG):
            sv.dinc(g_.dma_start(xb[g][:], xin[g]), "ld")
        for g in range(NG):
            sv.dinc(g_.dma_start(kb[g][:], k_in[g]), "ld")
            sv.dinc(g_.dma_start(bs[g][:], b_in[g]), "ld")
        sv.dinc(g_.dma_start(s_in[:], sin_in), "ld")
        sv.dinc(g_.dma_start(s_k[:], sk_in), "ld")
    ld_all = sv.val("ld")

    @block.vector
    def _(v):
        sv.waita(v, "ld", ld_all)
        v.reciprocal(r_in[:], s_in[:])
        v.reciprocal(r_k[:], s_k[:])
        sv.inc(v.tensor_tensor(prod[:], s_in[:], s_k[:], op=ALU.mult),
               "init")

    @block.scalar
    def _(s):
        sv.wait(s, "init", 1)
        for g in range(NG):
            s.activation(xq[g][:], xb[g][:], AF.Copy, scale=r_in[:])
        for g in range(NG):
            mm = s.activation(kq[g][:], kb[g][:], AF.Copy, scale=r_k[:])
        sv.inc(mm, "quant")
    quant_v = sv.val("quant")

    idx = 0
    for g2 in range(NG):
        for j in range(NCH):
            q = idx % 2
            sl = slice(j * CH, (j + 1) * CH)

            @block.tensor
            def _(tn, g2=g2, q=q, sl=sl, idx=idx):
                if idx == 0:
                    sv.waita(tn, "quant", quant_v)
                if idx >= 2:
                    sv.wait(tn, "ev", idx - 1)
                for d in range(NG):
                    mm = tn.matmul(pq[q][:],
                                   kq[d][:, g2 * 128:(g2 + 1) * 128],
                                   xq[d][:, sl], start=(d == 0),
                                   stop=(d == NG - 1))
                sv.inc(mm, "mm")

            @block.vector
            def _(v, g2=g2, q=q, idx=idx):
                sv.wait(v, "mm", idx + 1)
                if idx >= 4:
                    sv.wait(v, "st", 16 * (idx - 3))
                v.tensor_scalar(yb[:], pq[q][:], prod[:], None, op0=ALU.mult)
                sv.inc(v.tensor_scalar(ob[idx % 4][:], yb[:], bs[g2][:],
                                       None, op0=ALU.add), "ev")

            @block.sync
            def _(sy, g2=g2, sl=sl, idx=idx):
                sv.wait(sy, "ev", idx + 1)
                sv.dinc(sy.dma_start(out[g2][:, sl], ob[idx % 4][:]), "st")
            idx += 1

    @block.sync
    def _(sy):
        sv.waita(sy, "st", sv.val("st"))


def emit_wkv(nc, S, block, es, k_in, v_in, sr_in, wd_in, u_in, out):
    sv = SV(S, ["ldw", "ld", "st", "init", "sm1", "vm1", "sm2", "sm3",
                "vdone", "den", "sden", "zdone"])
    sb, ps = _allocs(nc, es)
    kb = [sb(f"kb{p}", [128, T], F32) for p in range(2)]
    vb = [sb(f"vb{p}", [128, T], F32) for p in range(2)]
    srb = [sb(f"srb{p}", [128, T], BF16) for p in range(2)]
    wd = [sb(f"wd{g}", [128, 1], F32) for g in range(NG)]
    uu = [sb(f"uu{g}", [128, 1], F32) for g in range(NG)]
    ew = sb("ew", [128, 1], F32)
    abuf = sb("abuf", [128, T], F32)
    ebuf = sb("ebuf", [128, T], F32)
    pbuf = sb("pbuf", [128, T], F32)
    zb = [sb(f"zb{p}", [128, T], BF16) for p in range(2)]
    nm1 = sb("nm1", [128, 1], F32)

    @block.vector
    def _(v):
        sv.inc(v.memset(nm1[:], -1.0), "init")

    @block.sync
    def _(sy):
        for g in range(NG):
            sv.dinc(sy.dma_start(wd[g][:], wd_in[g]), "ldw")
            sv.dinc(sy.dma_start(uu[g][:], u_in[g]), "ldw")
    ldw_v = sv.val("ldw")

    for g in range(NG):
        p = g % 2

        @block.sync
        def _(sy, g=g, p=p):
            if g >= 2:
                sv.wait(sy, "zdone", g - 1)
            sv.dinc(sy.dma_start(kb[p][:], k_in[g]), "ld")
            sv.dinc(sy.dma_start(vb[p][:], v_in[g]), "ld")
            sv.dinc(sy.dma_start(srb[p][:], sr_in[g]), "ld")
        ld_v = sv.val("ld")

        @block.scalar
        def _(s, g=g, ld_v=ld_v):
            if g == 0:
                sv.waita(s, "ldw", ldw_v)
                sv.wait(s, "init", 1)
            if g >= 1:
                sv.wait(s, "vdone", g)
            sv.inc(s.activation(ew[:], wd[g][:], AF.Exp), "sm1")

        @block.vector
        def _(v, g=g):
            sv.wait(v, "sm1", g + 1)
            sv.inc(v.tensor_tensor(ew[:], ew[:], nm1[:], op=ALU.mult), "vm1")

        @block.scalar
        def _(s, g=g, p=p, ld_v=ld_v):
            sv.wait(s, "vm1", g + 1)
            sv.inc(s.activation(ew[:], ew[:], AF.Exp), "sm2")
            sv.waita(s, "ld", ld_v)
            if g >= 2:
                sv.wait(s, "zdone", g - 1)
            s.activation(abuf[:], kb[p][:], AF.Exp)
            sv.inc(s.activation(kb[p][:], kb[p][:], AF.Exp, bias=uu[g][:]),
                   "sm3")

        @block.vector
        def _(v, g=g, p=p):
            sv.wait(v, "sm2", g + 1)
            coefb = ew[:].broadcast_to((128, T))
            sv.wait(v, "sm3", g + 1)
            v.tensor_tensor(ebuf[:], abuf[:], vb[p][:], op=ALU.mult)
            v.tensor_tensor_scan(pbuf[:], coefb, ebuf[:], 0.0,
                                 op0=ALU.mult, op1=ALU.add)
            sv.inc(v.tensor_tensor_scan(ebuf[:], coefb, abuf[:], 0.0,
                                        op0=ALU.mult, op1=ALU.add), "vdone")
            v.tensor_tensor(abuf[:], kb[p][:], vb[p][:], op=ALU.mult)
            v.tensor_tensor(abuf[:, 1:T], abuf[:, 1:T], pbuf[:, 0:T - 1],
                            op=ALU.add)
            sv.inc(v.tensor_tensor(kb[p][:, 1:T], kb[p][:, 1:T],
                                   ebuf[:, 0:T - 1], op=ALU.add), "den")

        @block.scalar
        def _(s, g=g, p=p):
            sv.wait(s, "den", g + 1)
            s.activation(kb[p][:], kb[p][:], AF.Ln)
            sv.inc(s.activation(kb[p][:], kb[p][:], AF.Exp, scale=nm1[:]),
                   "sden")

        @block.vector
        def _(v, g=g, p=p):
            sv.wait(v, "sden", g + 1)
            if g >= 2:
                sv.wait(v, "st", 16 * (g - 1))
            v.tensor_tensor(abuf[:], abuf[:], kb[p][:], op=ALU.mult)
            sv.inc(v.tensor_tensor(zb[p][:], abuf[:], srb[p][:],
                                   op=ALU.mult), "zdone")

        @block.sync
        def _(sy, g=g, p=p):
            sv.wait(sy, "zdone", g + 1)
            sv.dinc(sy.dma_start(out[g], zb[p][:]), "st")

    @block.sync
    def _(sy):
        sv.waita(sy, "st", sv.val("st"))


def emit_lif(nc, S, block, es, a_in, b_in, out):
    sv = SV(S, ["lda", "ldb", "st", "init", "rel", "spk", "scan"])
    NL = 32
    NCK = T // NL
    LAN = NG * NCK
    sb, ps = _allocs(nc, es)
    B = sb("B", [128, NG * T], F32)
    A = [sb(f"A{p}", [128, T], F32) for p in range(2)]
    wst = [sb(f"wst{p}", [128, LAN], F32) for p in range(2)]
    m = sb("m", [128, LAN], F32)
    mask = sb("mask", [128, LAN], F32)
    bs = [sb(f"bs{p}", [128, CH], F32) for p in range(2)]
    ob = [sb(f"obl{p}", [128, CH], F32) for p in range(2)]

    @block.vector
    def _(v):
        v.memset(mask[:], 1.0)
        for g in range(NG):
            v.memset(mask[:, g * NCK:g * NCK + 1], 0.0)
        sv.inc(v.memset(wst[0][:], 0.0), "init")

    for g in range(NG):
        pa = g % 2

        @block.sync
        def _(sy, g=g, pa=pa):
            if g >= 2:
                sv.wait(sy, "rel", g - 1)
            sv.dinc(sy.dma_start(A[pa][:], a_in[g]), "lda")
        lda_v = sv.val("lda")

        @block.vector
        def _(v, g=g, pa=pa, lda_v=lda_v):
            sv.waita(v, "lda", lda_v)
            dst = B[:, g * T:(g + 1) * T].rearrange("p (i ch) -> p i ch",
                                                    i=NL)
            srcv = A[pa][:].rearrange("p (ch i) -> p i ch", ch=NCK)
            sv.inc(v.tensor_copy(dst, srcv), "rel")
    rel_v = sv.val("rel")
    Bv = B[:].rearrange("p (g i ch) -> p g i ch", g=NG, i=NL)

    @block.vector
    def _(v, rel_v=rel_v):
        sv.waita(v, "rel", rel_v)
        sv.wait(v, "init", 1)
        wprev = 0
        for j in range(1, NL):
            xw = B[:].rearrange("p (g r) -> p g r", g=NG)[
                :, :, j * NCK - 1:j * NCK - 1 + NCK]
            mm_ = m[:].rearrange("p (g ch) -> p g ch", g=NG)
            wp = wst[wprev][:].rearrange("p (g ch) -> p g ch", g=NG)
            wc = wst[1 - wprev][:].rearrange("p (g ch) -> p g ch", g=NG)
            v.scalar_tensor_tensor(mm_, wp, 2.0, wp, op0=ALU.is_lt,
                                   op1=ALU.mult)
            v.scalar_tensor_tensor(wc, mm_, 0.5, xw, op0=ALU.mult,
                                   op1=ALU.add)
            wprev = 1 - wprev
        v.tensor_tensor(wst[wprev][:], wst[wprev][:], mask[:], op=ALU.mult)
        for i in range(NL):
            prev = (wst[wprev][:].rearrange("p (g ch) -> p g ch", g=NG)
                    if i == 0 else Bv[:, :, i - 1, :])
            mm_ = m[:].rearrange("p (g ch) -> p g ch", g=NG)
            v.scalar_tensor_tensor(mm_, prev, 2.0, prev, op0=ALU.is_lt,
                                   op1=ALU.mult)
            v.scalar_tensor_tensor(Bv[:, :, i, :], mm_, 0.5, Bv[:, :, i, :],
                                   op0=ALU.mult, op1=ALU.add)
        sv.inc(v.nop(), "scan")

    for g in range(NG):
        for j in range(NCH):
            pa = (g * NCH + j) % 2
            sl = slice(j * CH, (j + 1) * CH)
            it = g * NCH + j

            @block.gpsimd
            def _(gp, g=g, pa=pa, sl=sl, it=it):
                if it >= 2:
                    sv.wait(gp, "spk", it - 1)
                sv.dinc(gp.dma_start(bs[pa][:], b_in[g][:, sl]), "ldb")
            ldb_v = sv.val("ldb")

            @block.vector
            def _(v, g=g, j=j, pa=pa, ldb_v=ldb_v, it=it):
                sv.waita(v, "ldb", ldb_v)
                if it >= 2:
                    sv.wait(v, "st", 16 * (it - 1))
                pv = B[:, g * T:(g + 1) * T].rearrange(
                    "p (i ch) -> p ch i", i=NL)[:, j * 16:(j + 1) * 16, :]
                bb = bs[pa][:].rearrange("p (ch i) -> p ch i", ch=16)
                oo = ob[pa][:].rearrange("p (ch i) -> p ch i", ch=16)
                sv.inc(v.scalar_tensor_tensor(oo, pv, 2.0, bb,
                                              op0=ALU.is_ge, op1=ALU.add),
                       "spk")

            @block.sync
            def _(sy, g=g, pa=pa, sl=sl, it=it):
                sv.wait(sy, "spk", it + 1)
                sv.dinc(sy.dma_start(out[g][:, sl], ob[pa][:]), "st")

    @block.sync
    def _(sy):
        sv.waita(sy, "st", sv.val("st"))


def emit_cmix(nc, S, block, es, xk_in, sr_in, wk_in, wv_in, out):
    H4 = 4096
    CHA = 256
    NCA = T // CHA
    sv = SV(S, ["ld", "ldx", "st", "zinit", "kmm", "kev", "vmm", "cev"])
    sb, ps = _allocs(nc, es)
    wk = [sb(f"wk{g}", [128, H4], BF16) for g in range(NG)]
    wv = [sb(f"wv{h}", [128, C], BF16) for h in range(32)]
    xk = [[sb(f"xk{p}{g}", [128, CHA], BF16) for g in range(NG)]
          for p in range(2)]
    sr = [[sb(f"sr{p}{g}", [128, CHA], BF16) for g in range(NG)]
          for p in range(2)]
    kk = [sb(f"kk{p}", [128, CHA], BF16) for p in range(2)]
    kt = sb("ktmp", [128, CHA], F32)
    ob = [sb(f"ob{i}", [128, CHA], F32) for i in range(4)]
    zrow = sb("zrow", [1, 2 * CHA], BF16)
    onec = sb("onec", [1, 128], BF16)
    psK = [ps(f"psK{i}", [128, CHA], F32) for i in range(2)]
    psC = [ps(f"psC{i}", [128, 2 * CHA], F32) for i in range(4)]

    @block.vector
    def _(v):
        v.memset(zrow[:], 0.0)
        sv.inc(v.memset(onec[:], 1.0), "zinit")

    @block.gpsimd
    def _(g_):
        for g in range(NG):
            sv.dinc(g_.dma_start(wk[g][:], wk_in[g]), "ld")
        for h in range(32):
            sv.dinc(g_.dma_start(wv[h][:], wv_in[h]), "ld")
    ld_all = sv.val("ld")

    for sc in range(NCA):
        p = sc % 2
        ssl = slice(sc * CHA, (sc + 1) * CHA)

        @block.gpsimd
        def _(gp, sc=sc, p=p, ssl=ssl):
            if sc >= 2:
                sv.wait(gp, "kmm", (sc - 1) * 32)
                sv.wait(gp, "cev", (sc - 1) * NG)
            for g in range(NG):
                sv.dinc(gp.dma_start(xk[p][g][:], xk_in[g][:, ssl]), "ldx")
                sv.dinc(gp.dma_start(sr[p][g][:], sr_in[g][:, ssl]), "ldx")
        ldx_v = sv.val("ldx")

        for h in range(33):
            hp = h % 2
            it = sc * 32 + h

            @block.tensor
            def _(tn, sc=sc, h=h, hp=hp, it=it, p=p, ldx_v=ldx_v):
                if h < 32:
                    if it == 0:
                        sv.waita(tn, "ld", ld_all)
                        sv.wait(tn, "zinit", 1)
                    sv.waita(tn, "ldx", ldx_v)
                    if it >= 2:
                        sv.wait(tn, "kev", it - 1)

                    for d in range(NG):
                        mm = tn.matmul(psK[hp][:],
                                       wk[d][:, h * 128:(h + 1) * 128],
                                       xk[p][d][:], start=(d == 0),
                                       stop=(d == NG - 1))
                    sv.inc(mm, "kmm")
                if h >= 1:
                    ho = h - 1
                    if ho == 0:
                        if sc >= 1:
                            sv.wait(tn, "cev", sc * NG)
                        for i4 in range(4):
                            tn.matmul(psC[i4][:], onec[:], zrow[:],
                                      start=True, stop=False)
                    sv.wait(tn, "kev", sc * 32 + ho + 1)
                    for g2 in range(NG):
                        mm = tn.matmul(
                            psC[g2 // 2][:, (g2 % 2) * CHA:(g2 % 2 + 1) * CHA],
                            wv[ho][:, g2 * 128:(g2 + 1) * 128],
                            kk[ho % 2][:], start=False, stop=(ho == 31))
                    sv.inc(mm, "vmm")

            if h < 32:
                @block.vector
                def _(v, h=h, hp=hp, it=it):
                    sv.wait(v, "kmm", it + 1)
                    if it >= 2:
                        sv.wait(v, "vmm", it - 1)
                    v.tensor_scalar(kt[:], psK[hp][:], 0.0, None,
                                    op0=ALU.max)
                    sv.inc(v.tensor_tensor(kk[hp][:], kt[:], psK[hp][:],
                                           op=ALU.mult), "kev")

        for g2 in range(NG):
            ic = sc * NG + g2

            @block.vector
            def _(v, sc=sc, g2=g2, p=p, ic=ic):
                sv.wait(v, "vmm", (sc + 1) * 32)
                if ic >= 4:
                    sv.wait(v, "st", 16 * (ic - 3))
                sv.inc(v.tensor_tensor(
                    ob[ic % 4][:],
                    psC[g2 // 2][:, (g2 % 2) * CHA:(g2 % 2 + 1) * CHA],
                    sr[p][g2][:], op=ALU.mult), "cev")

            @block.sync
            def _(sy, g2=g2, ssl=ssl, ic=ic):
                sv.wait(sy, "cev", ic + 1)
                sv.dinc(sy.dma_start(out[g2][:, ssl], ob[ic % 4][:]), "st")

    @block.sync
    def _(sy):
        sv.waita(sy, "st", sv.val("st"))


def emit_router(nc, S, block, es, h_in, emb_in, id_in, wm_out, wmt_out):
    E = 16
    sv = SV(S, ["ld", "stw", "init", "lmm", "wmd", "tmm", "tev"])
    sb, ps = _allocs(nc, es)
    hb = [sb(f"hb{g}", [128, T], F32) for g in range(NG)]
    embf = sb("embf", [128, E], F32)
    ident = sb("identsb", [128, 128], F32)
    wms = [sb(f"wms{p}", [128, 512], F32) for p in range(2)]
    wmts = [sb(f"wmts{p}", [16, T], F32) for p in range(2)]
    el = sb("el", [128, 512], F32)
    w2b = sb("w2b", [128, 512], F32)
    mx = sb("mx", [128, 32], F32)
    psL = [ps(f"psL{p}", [128, 512], F32) for p in range(2)]
    psT = [ps(f"psT{p}", [128, 512], F32) for p in range(2)]

    @block.gpsimd
    def _(g_):
        for g in range(NG):
            sv.dinc(g_.dma_start(hb[g][:], h_in[g]), "ld")
        sv.dinc(g_.dma_start(embf[:], emb_in), "ld")
        sv.dinc(g_.dma_start(ident[:], id_in), "ld")
    ld_all = sv.val("ld")

    @block.vector
    def _(v):
        sv.waita(v, "ld", ld_all)
        sv.inc(v.nop(), "init")

    for g in range(NG):
        pg = g % 2

        @block.tensor
        def _(tn, g=g, pg=pg):
            if g == 0:
                sv.wait(tn, "init", 1)
            if g >= 2:
                sv.wait(tn, "wmd", 3 * (g - 2) + 1)
            for st in range(32):
                mm = tn.matmul(psL[pg][:, st * 16:(st + 1) * 16],
                               hb[g][:, st * 128:(st + 1) * 128],
                               embf[:], start=True, stop=True)
            sv.inc(mm, "lmm")

        @block.vector
        def _(v, g=g, pg=pg):
            sv.wait(v, "lmm", g + 1)
            if g >= 2:
                sv.wait(v, "stw", 16 * 2 * (g - 1))
            bc = lambda a: a.unsqueeze(-1).broadcast_to((128, 32, E))
            Lv = psL[pg][:].rearrange("p (s e) -> p s e", s=32)
            elv = el[:].rearrange("p (s e) -> p s e", s=32)
            wv_ = wms[pg][:].rearrange("p (s e) -> p s e", s=32)
            w2v = w2b[:].rearrange("p (s e) -> p s e", s=32)
            sv.fence(v, v.tensor_reduce(mx[:], Lv, axis=AXX, op=ALU.max))
            sv.inc(v.tensor_tensor(elv, Lv, bc(mx[:]), op=ALU.subtract),
                   "wmd")

        @block.scalar
        def _(s, g=g):
            sv.wait(s, "wmd", 3 * g + 1)
            sv.inc(s.activation(el[:], el[:], AF.Exp), "wmd")

        @block.vector
        def _(v, g=g, pg=pg):
            sv.wait(v, "wmd", 3 * g + 2)
            bc = lambda a: a.unsqueeze(-1).broadcast_to((128, 32, E))
            elv = el[:].rearrange("p (s e) -> p s e", s=32)
            wv_ = wms[pg][:].rearrange("p (s e) -> p s e", s=32)
            w2v = w2b[:].rearrange("p (s e) -> p s e", s=32)
            sv.fence(v, v.tensor_reduce(mx[:], elv, axis=AXX, op=ALU.add))
            sv.fence(v, v.reciprocal(mx[:], mx[:]))
            sv.fence(v, v.tensor_tensor(wv_, elv, bc(mx[:]), op=ALU.mult))
            sv.fence(v, v.tensor_reduce(mx[:], wv_, axis=AXX, op=ALU.max))
            sv.fence(v, v.tensor_tensor(w2v, wv_, bc(mx[:]), op=ALU.is_ge))
            sv.fence(v, v.scalar_tensor_tensor(w2b[:], w2b[:], -1e30,
                                               wms[pg][:], op0=ALU.mult,
                                               op1=ALU.add))
            sv.fence(v, v.tensor_reduce(mx[:], w2v, axis=AXX, op=ALU.max))
            sv.fence(v, v.tensor_tensor(w2v, wv_, bc(mx[:]), op=ALU.is_ge))
            sv.inc(v.tensor_tensor(wms[pg][:], wms[pg][:], w2b[:],
                                   op=ALU.mult), "wmd")

        for st in range(32):
            it = g * 32 + st

            @block.tensor
            def _(tn, g=g, st=st, it=it, pg=pg):
                sv.wait(tn, "wmd", 3 * (g + 1))
                if it >= 2:
                    sv.wait(tn, "tev", it - 1)
                sv.inc(tn.transpose(psT[st % 2][0:16, 0:128],
                                    wms[pg][:, st * 16:st * 16 + 16],
                                    ident[:]), "tmm")

            @block.vector
            def _(v, st=st, it=it, pg=pg):
                sv.wait(v, "tmm", it + 1)
                sv.inc(v.tensor_copy(wmts[pg][:, st * 128:(st + 1) * 128],
                                     psT[st % 2][0:16, 0:128]), "tev")

        @block.sync
        def _(sy, g=g, pg=pg):
            sv.wait(sy, "tev", (g + 1) * 32)
            sv.dinc(sy.dma_start(
                wm_out[g].rearrange("s p e -> p s e"),
                wms[pg][:].rearrange("p (s e) -> p s e", s=32)), "stw")
            sv.dinc(sy.dma_start(wmt_out[g], wmts[pg][:]), "stw")

    @block.sync
    def _(sy):
        sv.waita(sy, "stw", sv.val("stw"))


def emit_experts(nc, S, block, es, h_in, w1_in, b1_in, w2_in, b2_in, wm_in,
                 wmt_in, id_in, out):
    E, II = 16, 512
    SCE = 512
    NSC = T // SCE
    sv = SV(S, ["ld", "ldw", "stc", "init", "cdone", "bmm", "bev", "ctev",
                "hmm", "gev", "emm", "eev", "c16", "ctmm"])
    sb, ps = _allocs(nc, es)
    hb = [sb(f"hb{g}", [128, T], BF16) for g in range(NG)]
    w1b = [sb(f"w1b{e}", [128, II], BF16) for e in range(E)]
    w2b = [[sb(f"w2b{e}{i}", [128, 128], BF16) for i in range(4)]
           for e in range(E)]
    b1s = [[sb(f"b1s{e}{i}", [128, 1], F32) for i in range(4)]
           for e in range(E)]
    b2s = sb("b2s", [16, 128], F32)
    wmg = [sb(f"wmg{p}", [128, 512], F32) for p in range(2)]
    wmtg = [sb(f"wmtg{p}", [16, T], F32) for p in range(2)]
    identf = sb("identfsb", [128, 128], F32)
    identb = sb("identb", [128, 128], BF16)
    hm = [[sb(f"hm{p}{i}", [128, SCE], BF16) for i in range(4)]
          for p in range(2)]
    comb = sb("comb", [128, T], F32)
    cmb16 = [sb(f"cmb16{p}", [128, 128], BF16) for p in range(2)]
    ctb = [sb(f"ctb{p}", [128, T], BF16) for p in range(2)]
    psH = [ps(f"psH{i}", [128, 512], F32) for i in range(4)]
    psE = [ps(f"psE{i}", [128, 512], F32) for i in range(2)]
    psB = ps("psB", [128, 512], F32)
    psT = ps("psTe", [128, 512], BF16)

    @block.gpsimd
    def _(g_):
        for g in range(NG):
            sv.dinc(g_.dma_start(hb[g][:], h_in[g]), "ld")
        for e in range(E):
            sv.dinc(g_.dma_start(w1b[e][:], w1_in[e]), "ld")
            for i in range(4):
                sv.dinc(g_.dma_start(w2b[e][i][:], w2_in[e, i]), "ld")
                sv.dinc(g_.dma_start(b1s[e][i][:], b1_in[e, i]), "ld")
        sv.dinc(g_.dma_start(b2s[:], b2_in), "ld")
        sv.dinc(g_.dma_start(identf[:], id_in), "ld")
    ld_all = sv.val("ld")

    @block.vector
    def _(v):
        sv.waita(v, "ld", ld_all)
        sv.inc(v.tensor_copy(identb[:], identf[:]), "init")

    def psh(j, i):
        return psH[i][:]

    def te_mm2(tn, j, g, e, sc):
        jp = j % 2
        pe = j % 2
        if j >= 2:
            sv.wait(tn, "eev", (j - 1) * 4)
        for st2 in range(4):
            for i in range(4):
                mm = tn.matmul(psE[pe][:, st2 * 128:(st2 + 1) * 128],
                               hm[jp][i][:, st2 * 128:(st2 + 1) * 128],
                               w2b[e][i][:], start=(i == 0), stop=(i == 3))
        sv.inc(mm, "emm")

    def ve_evac(j, g, e, sc):
        @block.vector
        def _(v, g=g, e=e, sc=sc, j=j):
            pg = g % 2
            pe = j % 2
            sv.wait(v, "emm", j + 1)
            for st2 in range(4):
                st = sc * 4 + st2
                csl = comb[:, st * 128:(st + 1) * 128]
                sv.inc(v.scalar_tensor_tensor(
                    csl, psE[pe][:, st2 * 128:(st2 + 1) * 128],
                    wmg[pg][:, st * 16 + e:st * 16 + e + 1], csl,
                    op0=ALU.mult, op1=ALU.add), "eev")

    j = 0
    for g in range(NG):
        pg = g % 2

        @block.sync
        def _(sy, g=g, pg=pg):
            if g >= 2:
                sv.wait(sy, "cdone", g - 1)
            sv.dinc(sy.dma_start(
                wmg[pg][:].rearrange("p (s e) -> p s e", s=32),
                wm_in[g].rearrange("s p e -> p s e")), "ldw")
            sv.dinc(sy.dma_start(wmtg[pg][:], wmt_in[g]), "ldw")
        ldw_v = sv.val("ldw")

        for st in range(32):
            ib = g * 32 + st

            @block.tensor
            def _(tn, g=g, st=st, ib=ib, pg=pg, ldw_v=ldw_v):
                if ib == 0:
                    sv.waita(tn, "ld", ld_all)
                sv.waita(tn, "ldw", ldw_v)
                if ib >= 1:
                    sv.wait(tn, "bev", ib)
                sv.inc(tn.matmul(psB[:, (ib % 4) * 128:(ib % 4 + 1) * 128],
                                 wmtg[pg][:, st * 128:(st + 1) * 128],
                                 b2s[:], start=True, stop=True), "bmm")

            @block.vector
            def _(v, g=g, st=st, ib=ib):
                sv.wait(v, "bmm", ib + 1)
                if st == 0 and g >= 1:
                    sv.wait(v, "ctev", g * 32)
                sv.inc(v.tensor_copy(
                    comb[:, st * 128:(st + 1) * 128],
                    psB[:, (ib % 4) * 128:(ib % 4 + 1) * 128]), "bev")
        bev_v = sv.val("bev")

        first_j = j
        pend = []

        def flush_scalar():
            if not pend:
                return
            batch = list(pend)
            pend.clear()

            @block.scalar
            def _(s, batch=batch):
                for (jj, ee) in batch:
                    jp2 = jj % 2
                    if jj >= 2:
                        sv.wait(s, "emm", jj - 1)
                    for i in range(4):
                        sv.wait(s, "hmm", jj * 4 + i + 1)
                        sv.inc(s.activation(hm[jp2][i][:], psh(jj, i),
                                            AF.Gelu, bias=b1s[ee][i][:]),
                               "gev")

        for e in range(E):
            for sc in range(NSC):
                @block.tensor
                def _(tn, g=g, e=e, sc=sc, j=j, first_j=first_j,
                      bev_v=bev_v):
                    if e == 0 and sc == 0:
                        sv.waita(tn, "bev", bev_v)
                    for i in range(4):
                        if j >= 1:
                            sv.wait(tn, "gev", (j - 1) * 4 + i + 1)
                        mm = tn.matmul(psh(j, i),
                                       w1b[e][:, i * 128:(i + 1) * 128],
                                       hb[g][:, sc * SCE:(sc + 1) * SCE],
                                       start=True, stop=True)
                        sv.inc(mm, "hmm")
                    if j > first_j:
                        sv.wait(tn, "gev", j * 4)
                        ge, gsc = divmod(j - 1 - first_j, NSC)
                        te_mm2(tn, j - 1, g, ge, gsc)

                pend.append((j, e))
                if len(pend) >= 16:
                    flush_scalar()

                if j > first_j:
                    ge, gsc = divmod(j - 1 - first_j, NSC)
                    ve_evac(j - 1, g, ge, gsc)
                j += 1
        flush_scalar()

        @block.tensor
        def _(tn, g=g, j=j):
            sv.wait(tn, "gev", j * 4)
            te_mm2(tn, j - 1, g, E - 1, NSC - 1)
        ve_evac(j - 1, g, E - 1, NSC - 1)

        eev_need = j * 4
        for st in range(32):
            ic = g * 32 + st

            @block.vector
            def _(v, g=g, st=st, ic=ic, eev_need=eev_need):
                sv.waita(v, "eev", sv.base["eev"] + eev_need)
                if ic >= 2:
                    sv.wait(v, "ctmm", ic - 1)
                sv.inc(v.tensor_copy(cmb16[st % 2][:],
                                     comb[:, st * 128:(st + 1) * 128]),
                       "c16")

            @block.tensor
            def _(tn, st=st, ic=ic):
                sv.wait(tn, "c16", ic + 1)
                sv.wait(tn, "init", 1)
                if ic >= 1:
                    sv.wait(tn, "ctev", ic)
                sv.inc(tn.transpose(
                    psT[:, (st % 2) * 128:(st % 2) * 128 + 128],
                    cmb16[st % 2][:], identb[:]), "ctmm")

            @block.vector
            def _(v, g=g, st=st, ic=ic, pg=pg):
                sv.wait(v, "ctmm", ic + 1)
                if st == 0 and g >= 2:
                    sv.wait(v, "stc", 16 * (g - 1))
                sv.inc(v.tensor_copy(
                    ctb[pg][:, st * 128:(st + 1) * 128],
                    psT[:, (st % 2) * 128:(st % 2) * 128 + 128]), "ctev")

        @block.sync
        def _(sy, g=g, pg=pg):
            sv.wait(sy, "ctev", (g + 1) * 32)
            sv.inc(sy.nop(), "cdone")
            sv.dinc(sy.dma_start(out[g], ctb[pg][:]), "stc")

    @block.sync
    def _(sy):
        sv.waita(sy, "stc", sv.val("stc"))



def build_fused():
    nc = bass.Bass("TRN2", target_bir_lowering=False)
    EI = "ExternalInput"
    xT = cmajor(nc, "xT", EI, F32)
    wqT = pcol(nc, "wqT", EI, F32, n=C)
    wkT = pcol(nc, "wkT", EI, F32, n=C)
    wvT = pcol(nc, "wvT", EI, F32, n=C)
    woT = pcol(nc, "woT", EI, F32, n=C)
    bq = pcol(nc, "bq", EI)
    bk = pcol(nc, "bk", EI)
    bv_row = nc.dram_tensor("bvrow", [1, C], F32, kind=EI).ap()
    bo = pcol(nc, "bo", EI)
    lsc = pcol(nc, "lsc", EI)
    fker = pcol(nc, "fker", EI, BF16, n=C)
    sin_r = nc.dram_tensor("sinr", [128, 1], F32, kind=EI).ap()
    sk_r = nc.dram_tensor("skr", [128, 1], F32, kind=EI).ap()
    fbias = pcol(nc, "fbias", EI)
    ln1s = pcol(nc, "ln1s", EI)
    ln1b = pcol(nc, "ln1b", EI)
    ln2s = pcol(nc, "ln2s", EI)
    ln2b = pcol(nc, "ln2b", EI)
    tmk = pcol(nc, "tmk", EI)
    tmv = pcol(nc, "tmv", EI)
    tmr = pcol(nc, "tmr", EI)
    wdec = pcol(nc, "wdec", EI)
    ufr = pcol(nc, "ufr", EI)
    l1wk = pcol(nc, "l1wk", EI, F32, n=C)
    l1wv = pcol(nc, "l1wv", EI, F32, n=C)
    l1wr = pcol(nc, "l1wr", EI, F32, n=C)
    l1wo = pcol(nc, "l1wo", EI, F32, n=C)
    ftmk = pcol(nc, "ftmk", EI)
    ftmr = pcol(nc, "ftmr", EI)
    fwk = pcol(nc, "fwk", EI, F32, n=4096)
    fwr = pcol(nc, "fwr", EI, F32, n=C)
    fwv = pcol(nc, "fwv", EI, F32, n=C, c=4096)
    wm_w = pcol(nc, "wmw", EI, F32, n=C)
    bm = pcol(nc, "bm", EI)
    emb = nc.dram_tensor("emb", [128, 16], F32, kind=EI).ap()
    ident = nc.dram_tensor("ident", [128, 128], F32, kind=EI).ap()
    w1 = nc.dram_tensor("w1", [16 * 128, 512], F32, kind=EI).ap().rearrange(
        "(e p) i -> e p i", p=128)
    b1 = nc.dram_tensor("b1", [16 * 4 * 128, 1], F32, kind=EI).ap().rearrange(
        "(e i p) o -> e i p o", i=4, p=128)
    w2 = nc.dram_tensor("w2", [16 * 512, 128], F32, kind=EI).ap().rearrange(
        "(e i p) o -> e i p o", i=4, p=128)
    b2 = nc.dram_tensor("b2", [16, 128], F32, kind=EI).ap()
    ffwo = pcol(nc, "ffwo", EI, F32, n=C)
    ffbo = pcol(nc, "ffbo", EI)
    x4 = cmajor(nc, "x4", "ExternalOutput", F32)
    ln0 = cmajor(nc, "s_ln0", None, BF16)
    qn = cmajor(nc, "s_qn", None, BF16)
    kn = cmajor(nc, "s_kn", None, BF16)
    vT = nc.dram_tensor("s_vT", [T, C], BF16).ap().rearrange(
        "(n p) e -> n p e", p=128)
    a0 = cmajor(nc, "s_a0", None, BF16)
    x1 = cmajor(nc, "s_x1", None, F32)
    lnx1 = cmajor(nc, "s_lnx1", None, BF16)
    ybs = cmajor(nc, "s_yb", None, BF16)
    x2 = cmajor(nc, "s_x2", None, F32)
    z0 = cmajor(nc, "s_z0", None, F32)
    ln1 = cmajor(nc, "s_ln1", None, BF16)
    xk = cmajor(nc, "s_xk", None, BF16)
    xv = cmajor(nc, "s_xv", None, BF16)
    xr = cmajor(nc, "s_xr", None, BF16)
    k1 = cmajor(nc, "s_k1", None, F32)
    v1 = cmajor(nc, "s_v1", None, F32)
    srs = cmajor(nc, "s_sr", None, BF16)
    zs = cmajor(nc, "s_z", None, BF16)
    o1 = cmajor(nc, "s_o1", None, F32)
    x_i = cmajor(nc, "s_xi", None, F32)
    ln2 = cmajor(nc, "s_ln2", None, BF16)
    xk2 = cmajor(nc, "s_xk2", None, BF16)
    xr2 = cmajor(nc, "s_xr2", None, BF16)
    sr2 = cmajor(nc, "s_sr2", None, BF16)
    cmv = cmajor(nc, "s_cm", None, F32)
    att = cmajor(nc, "s_att", None, F32)
    x3 = cmajor(nc, "s_x3", None, F32)
    ln4 = cmajor(nc, "s_ln4", None, BF16)
    hs = cmajor(nc, "s_h", None, F32)
    wm = nc.dram_tensor("s_wm", [NG * 32 * 128, 16], F32).ap().rearrange(
        "(g s p) e -> g s p e", s=32, p=128)
    wmt = nc.dram_tensor("s_wmt", [NG * 16, T], F32).ap().rearrange(
        "(g p) t -> g p t", p=16)
    combt = cmajor(nc, "s_combt", None, BF16)
    mout = cmajor(nc, "s_mout", None, BF16)

    es0 = ExitStack()
    with es0:
        S = Sems(nc, es0)
        with nc.Block() as block:
            def stage(emitfn, *a, **k):
                with ExitStack() as es:
                    emitfn(nc, S, block, es, *a, **k)
                barrier(nc, S, block)

            stage(emit_ln, xT, ln0)
            stage(emit_projln, ln0, wqT, bq, qn)
            stage(emit_projln, ln0, wkT, bk, kn)
            stage(emit_vproj, ln0, wvT, bv_row, vT)
            stage(emit_l0attn, qn, kn, vT, woT, bo, lsc, a0)
            stage(emit_postres, a0, xT, x1)
            stage(emit_ln, x1, lnx1)
            stage(emit_l0ffn, lnx1, fker, sin_r, sk_r, fbias, ybs)
            stage(emit_postres, ybs, x1, x2)
            stage(emit_ln, x2, z0, out_dtype=F32)
            stage(emit_ln, z0, ln1, sc=ln1s, bi=ln1b)
            stage(emit_mix, ln1, [tmk, tmv, tmr], [xk, xv, xr])
            stage(emit_mm, xk, l1wk, k1, evac="copy", out_dtype=F32)
            stage(emit_mm, xv, l1wv, v1, evac="copy", out_dtype=F32)
            stage(emit_mm, xr, l1wr, srs, evac="sigmoid")
            stage(emit_wkv, k1, v1, srs, wdec, ufr, zs)
            stage(emit_mm, zs, l1wo, o1, evac="copy", out_dtype=F32)
            stage(emit_lif, o1, z0, x_i)
            stage(emit_ln, x_i, ln2, sc=ln2s, bi=ln2b)
            stage(emit_mix, ln2, [ftmk, ftmr], [xk2, xr2])
            stage(emit_mm, xr2, fwr, sr2, evac="sigmoid")
            stage(emit_cmix, xk2, sr2, fwk, fwv, cmv)
            stage(emit_lif, cmv, x_i, att)
            stage(emit_postres, att, x2, x3, so_f32=True)
            stage(emit_ln, x3, ln4)
            stage(emit_mm, ln4, wm_w, hs, evac="copy", out_dtype=F32,
                  b_in=bm)
            stage(emit_router, hs, emb, ident, wm, wmt)
            stage(emit_experts, hs, w1, b1, w2, b2, wm, wmt, ident, combt)
            stage(emit_mm, combt, ffwo, mout, evac="copy", out_dtype=BF16,
                  b_in=ffbo)
            stage(emit_postres, mout, x3, x4)
    return nc


_FUSED_NC = [None]


def kernel(**inp):
    f32 = np.float32
    x = inp["L0_attn_x"]
    col = lambda a: np.ascontiguousarray(np.asarray(a, f32).reshape(-1, 1))
    rep = lambda s: np.repeat(np.asarray(s, f32), 128).reshape(128, 1)
    ca = np.ascontiguousarray
    shared = {
        "wqT": ca(inp["L0_attn_Wq"].T.astype(f32)),
        "wkT": ca(inp["L0_attn_Wk"].T.astype(f32)),
        "wvT": ca(inp["L0_attn_Wv"].T.astype(f32)),
        "woT": ca(inp["L0_attn_Wo"].T.astype(f32)),
        "bq": col(inp["L0_attn_bq"]),
        "bk": col(inp["L0_attn_bk"]),
        "bvrow": ca(inp["L0_attn_bv"].astype(f32).reshape(1, -1)),
        "bo": col(inp["L0_attn_bo"]),
        "lsc": col(inp["L0_attn_layerscale"]),
        "fker": ca(inp["L0_ffn_kernel"]),
        "sinr": rep(inp["L0_ffn_input_scale"]),
        "skr": rep(inp["L0_ffn_kernel_scale"]),
        "fbias": ca(inp["L0_ffn_bias"].astype(f32).reshape(-1, 1)),
        "ln1s": col(inp["L1_attn_ln1_s"]),
        "ln1b": col(inp["L1_attn_ln1_b"]),
        "ln2s": col(inp["L1_attn_ln2_s"]),
        "ln2b": col(inp["L1_attn_ln2_b"]),
        "tmk": col(inp["L1_attn_tmk"]),
        "tmv": col(inp["L1_attn_tmv"]),
        "tmr": col(inp["L1_attn_tmr"]),
        "wdec": col(inp["L1_attn_w_decay"]),
        "ufr": col(inp["L1_attn_u_first"]),
        "l1wk": ca(inp["L1_attn_Wk"].astype(f32)),
        "l1wv": ca(inp["L1_attn_Wv"].astype(f32)),
        "l1wr": ca(inp["L1_attn_Wr"].astype(f32)),
        "l1wo": ca(inp["L1_attn_Wo"].astype(f32)),
        "ftmk": col(inp["L1_attn_f_tmk"]),
        "ftmr": col(inp["L1_attn_f_tmr"]),
        "fwk": ca(inp["L1_attn_fWk"].astype(f32)),
        "fwr": ca(inp["L1_attn_fWr"].astype(f32)),
        "fwv": ca(inp["L1_attn_fWv"].astype(f32)),
        "wmw": ca(inp["L1_ffn_Wm"].astype(f32)),
        "bm": col(inp["L1_ffn_bm"]),
        "emb": ca(inp["L1_ffn_expert_emb"].astype(f32)),
        "ident": np.eye(128, dtype=f32),
        "w1": ca(inp["L1_ffn_W1"].astype(f32).reshape(-1, 512)),
        "b1": ca(inp["L1_ffn_b1"].astype(f32).reshape(-1, 1)),
        "w2": ca(inp["L1_ffn_W2"].astype(f32).reshape(-1, 128)),
        "b2": ca(inp["L1_ffn_b2"].astype(f32)),
        "ffwo": ca(inp["L1_ffn_Wo"].astype(f32)),
        "ffbo": col(inp["L1_ffn_bo"]),
    }
    if _FUSED_NC[0] is None:
        _FUSED_NC[0] = build_fused()
    nc = _FUSED_NC[0]
    in_maps = []
    for b in range(NCORES):
        m = dict(shared)
        m["xT"] = np.ascontiguousarray(x[b].T)
        in_maps.append(m)
    res = run_bass_kernel_spmd(nc, in_maps, core_ids=list(range(NCORES)))
    out = np.stack([np.ascontiguousarray(res.results[b]["x4"].T)
                    for b in range(NCORES)])
    return out.astype(np.float32)
